# version 12
# baseline (speedup 1.0000x reference)
from contextlib import ExitStack
import numpy as np
import ml_dtypes
import concourse.bass as bass
import concourse.mybir as mybir
from concourse.bass_utils import run_bass_kernel_spmd

F32 = mybir.dt.float32
BF16 = mybir.dt.bfloat16
AF = mybir.ActivationFunctionType
ALU = mybir.AluOpType
AXX = mybir.AxisListType.X
BF = ml_dtypes.bfloat16

B, T, D = 2, 8192, 1024
NCORES = 8
TPC = T // 4
DC = D // 128
LN_EPS = 1e-5
CC = 64
NCH = TPC // CC
DCC = 128
DNCH = TPC // DCC
HB = 48
HLO = 32


class _DrainProxy:
    def __init__(self, eng):
        object.__setattr__(self, "_eng", eng)
        object.__setattr__(self, "_n", 0)

    def __getattr__(self, name):
        eng = object.__getattribute__(self, "_eng")
        attr = getattr(eng, name)
        if name in ("drain", "wait_ge", "then_inc") or not callable(attr):
            return attr

        def wrapped(*a, **k):
            if object.__getattribute__(self, "_n") > 0:
                eng.drain()
            object.__setattr__(self, "_n", object.__getattribute__(self, "_n") + 1)
            return attr(*a, **k)
        return wrapped


class S:
    def __init__(self, nc, block, tk, gk=None):
        self.nc, self.block, self.tk, self.gk = nc, block, tk, gk
        self.target = 0
        self.gtarget = 0
        self.n_steps = 0
        self.n_dmas = 0

    def _waits(self, eng, wait_t, wait_g):
        eng.wait_ge(self.tk, wait_t)
        if wait_g > 0:
            eng.wait_ge(self.gk, wait_g)

    def step(self, eng_name, nodrain=False, wait_back=0, wait_abs=None):
        auto = (eng_name in ("vector", "scalar", "gpsimd")) and not nodrain

        def deco(fn):
            wait_t = self.target - wait_back if wait_abs is None else wait_abs
            wait_t = max(0, wait_t)
            wait_g = self.gtarget

            def run(eng):
                self._waits(eng, wait_t, wait_g)
                last = fn(_DrainProxy(eng) if auto else eng)
                assert last is not None
                last.then_inc(self.tk, 1)

            getattr(self.block, eng_name)(run)
            self.target = self.target + 1
            self.n_steps += 1
        return deco

    def dma(self, eng_name):
        use_gk = eng_name == "gpsimd"

        def deco(fn):
            wait_t, wait_g = self.target, self.gtarget
            cell = {}

            def run(eng):
                self._waits(eng, wait_t, wait_g)
                insts = fn(eng)
                for i in insts:
                    i.then_inc(self.gk if use_gk else self.tk, 16)
                cell["n"] = len(insts)

            getattr(self.block, eng_name)(run)
            if use_gk:
                self.gtarget = wait_g + 16 * cell["n"]
            else:
                self.target = wait_t + 16 * cell["n"]
            self.n_dmas += cell["n"]
        return deco

    def fence(self, eng_name="sync"):
        wait_t, wait_g = self.target, self.gtarget

        def run(eng):
            self._waits(eng, wait_t, wait_g)

        getattr(self.block, eng_name)(run)


def bcast(a, b):
    return bass.broadcast_tensor_aps(a, b)


class Alloc:
    def __init__(self, nc, st):
        self.nc, self.st = nc, st

    def sb(self, name, shape, dtype=F32):
        return self.st.enter_context(self.nc.sbuf_tensor(name, shape, dtype))[:, :]

    def ps(self, name, shape, dtype=F32):
        return self.st.enter_context(self.nc.psum_tensor(name, shape, dtype))[:, :]

    def sem(self, name):
        return self.st.enter_context(self.nc.semaphore(name))

    def block(self):
        return self.st.enter_context(self.nc.Block())


def vec_tile(v):
    v = np.asarray(v, dtype=np.float32)
    return np.ascontiguousarray(v.reshape(-1, 128).T)


def const_inputs():
    return {
        "ones_k": np.ones((128, 1), dtype=BF),
        "ones_b": np.ones((1, 128), dtype=BF),
    }


def emit_weight_load(s, wf32, wb16, hbm_ap, nchunk, mwid):
    @s.dma("gpsimd")
    def _(eng):
        return [eng.dma_start(out=wb16.rearrange("p (c m) -> p c m", c=nchunk),
                              in_=hbm_ap.rearrange("(c p) m -> p c m", p=128))]


def emit_ln_fm(s, nc, x3, ln3, sq3, srow, srow_bf, ps_stA, ps_stB, ps_a, ps_b, tmp3,
               ones_k, ones_b, nt, nchunk=DC, dim=D, eps=LN_EPS):
    ntmax = srow.shape[1] // 4
    MEAN = srow[0:1, 0:nt]
    VAR = srow[0:1, ntmax:ntmax + nt]
    RSTD = srow[0:1, 2 * ntmax:2 * ntmax + nt]
    BROW = srow[0:1, 3 * ntmax:3 * ntmax + nt]
    RSTD_BF = srow_bf[0:1, 0:nt]
    BROW_BF = srow_bf[0:1, ntmax:ntmax + nt]

    @s.step("vector")
    def _(eng):
        return eng.tensor_copy(ln3, x3)

    @s.step("tensor")
    def _(eng):
        for c in range(nchunk):
            i = eng.matmul(ps_stA[0:1, 0:nt], ones_k, ln3[:, c, :],
                           start=(c == 0), stop=(c == nchunk - 1))
        return i

    @s.step("scalar")
    def _(eng):
        return eng.activation(sq3, ln3, AF.Square)

    @s.step("tensor")
    def _(eng):
        for c in range(nchunk):
            i = eng.matmul(ps_stB[0:1, 0:nt], ones_k, sq3[:, c, :],
                           start=(c == 0), stop=(c == nchunk - 1))
        return i

    @s.step("vector")
    def _(eng):
        eng.tensor_single_scalar(MEAN, ps_stA[0:1, 0:nt], 1.0 / dim, ALU.mult)
        eng.tensor_single_scalar(VAR, ps_stB[0:1, 0:nt], 1.0 / dim, ALU.mult)
        eng.drain()
        eng.tensor_tensor(RSTD, MEAN, MEAN, ALU.mult)
        eng.drain()
        eng.tensor_tensor(VAR, VAR, RSTD, ALU.subtract)
        eng.drain()
        return eng.tensor_single_scalar(VAR, VAR, eps, ALU.add)

    @s.step("scalar")
    def _(eng):
        return eng.activation(VAR, VAR, AF.Sqrt)

    @s.step("vector")
    def _(eng):
        eng.reciprocal(RSTD, VAR)
        eng.drain()
        eng.tensor_tensor(BROW, MEAN, RSTD, ALU.mult)
        eng.drain()
        eng.tensor_single_scalar(BROW, BROW, -1.0, ALU.mult)
        eng.drain()
        eng.tensor_copy(RSTD_BF, RSTD)
        return eng.tensor_copy(BROW_BF, BROW)

    @s.step("tensor")
    def _(eng):
        eng.matmul(ps_a[:, 0:nt], ones_b, RSTD_BF, start=True, stop=True)
        return eng.matmul(ps_b[:, 0:nt], ones_b, BROW_BF, start=True, stop=True)

    @s.step("vector")
    def _(eng):
        am, asm = bcast(x3, ps_a[:, 0:nt].rearrange("p (o t) -> p o t", o=1))
        eng.tensor_tensor(tmp3, am, asm, ALU.mult)
        eng.drain()
        bm, bsm = bcast(tmp3, ps_b[:, 0:nt].rearrange("p (o t) -> p o t", o=1))
        return eng.tensor_tensor(ln3, bm, bsm, ALU.add)




def emit_proj_grouped(s, w_b, in3, ps_list, wdt, nmc, post, kw=D, kchunks=DC):
    gsz = len(ps_list)
    prev_evac = None
    for g0 in range(0, nmc, gsz):
        mcs = list(range(g0, min(g0 + gsz, nmc)))

        @s.step("tensor", wait_abs=prev_evac)
        def _(eng, mcs=mcs):
            for idx, mc in enumerate(mcs):
                ps = ps_list[idx]
                for kc in range(kchunks):
                    i = eng.matmul(ps[:, 0:wdt], w_b[:, kc * kw + mc * 128: kc * kw + (mc + 1) * 128],
                                   in3[:, kc, 0:wdt], start=(kc == 0), stop=(kc == kchunks - 1))
            return i

        t_pre = s.target
        post(mcs)
        prev_evac = t_pre + 1

def emit_ffn(s, nc, al, tag, xs_hbm, xo_hbms, ncols, nhalo, wr_b, wk_b, wv_b,
             mur_t, muk_t, onk_t, onb_t, half=True):
    widths = []
    rem = ncols
    while rem > 0:
        w = min(512, rem)
        widths.append(w)
        rem -= w
    nt = 512
    xs_t = al.sb(f"{tag}xs_t", [128, DC * (nt + 1)], F32)
    ln_t = al.sb(f"{tag}ln_t", [128, DC * nt], BF16)
    lnp_t = al.sb(f"{tag}lnp_t", [128, DC], BF16)
    sq_t = al.sb(f"{tag}sq_t", [128, DC * nt], BF16)
    dd_t = al.sb(f"{tag}dd_t", [128, DC * nt], F32)
    xr_t = al.sb(f"{tag}xr_t", [128, DC * nt], BF16)
    xk_t = al.sb(f"{tag}xk_t", [128, DC * nt], BF16)
    rs_t = al.sb(f"{tag}rs_t", [128, DC * nt], BF16)
    ka_t = al.sb(f"{tag}ka_t", [128, DC * nt], BF16)
    krelu = al.sb(f"{tag}krelu", [128, 4 * nt], F32)
    hout = al.sb(f"{tag}hout", [128, 4 * nt], F32)
    srow = al.sb(f"{tag}srow", [1, 4 * nt], F32)
    srow_bf = al.sb(f"{tag}srow_bf", [1, 2 * nt], BF16)
    ps_mm = al.ps(f"{tag}ps_mm", [128, nt], F32)
    ps_stA = al.ps(f"{tag}ps_stA", [1, nt], F32)
    ps_stB = al.ps(f"{tag}ps_stB", [1, nt], F32)
    ps_a = al.ps(f"{tag}ps_a", [128, nt], F32)
    ps_b = al.ps(f"{tag}ps_b", [128, nt], F32)
    ps_c = al.ps(f"{tag}ps_c", [128, nt], F32)
    ps_list = [ps_mm, ps_a, ps_b, ps_c]

    xs3 = xs_t.rearrange("p (c t) -> p c t", c=DC)
    ln3 = ln_t.rearrange("p (c t) -> p c t", c=DC)
    lnp3 = lnp_t.rearrange("p (c o) -> p c o", o=1)
    sq3 = sq_t.rearrange("p (c t) -> p c t", c=DC)
    dd3 = dd_t.rearrange("p (c t) -> p c t", c=DC)
    xr3 = xr_t.rearrange("p (c t) -> p c t", c=DC)
    xk3 = xk_t.rearrange("p (c t) -> p c t", c=DC)
    ka3 = ka_t.rearrange("p (c t) -> p c t", c=DC)
    bufs = dict(xs_t=xs_t, ln_t=ln_t, sq_t=sq_t, dd_t=dd_t, srow=srow,
                srow_bf=srow_bf, ps_mm=ps_mm, ps_stA=ps_stA, ps_stB=ps_stB,
                ps_a=ps_a, ps_b=ps_b)

    @s.dma("sync")
    def _(eng):
        return [eng.dma_start(out=xs3[:, :, 0:1],
                              in_=xs_hbm.rearrange("(c p) t -> p c t", p=128)[:, :, nhalo - 1:nhalo],
                              allow_slow_non_contiguous=True)]

    emit_ln_fm(s, nc, xs3[:, :, 0:1], lnp3, sq3[:, :, 0:1], srow, srow_bf,
               ps_stA, ps_stB, ps_a, ps_b, dd3[:, :, 0:1], onk_t, onb_t, 1)

    col0 = 0
    for wdt in widths:
        lo = nhalo + col0

        @s.dma("sync")
        def _(eng, lo=lo, wdt=wdt):
            return [eng.dma_start(out=xs3[:, :, 1:1 + wdt],
                                  in_=xs_hbm.rearrange("(c p) t -> p c t", p=128)[:, :, lo:lo + wdt])]

        emit_ln_fm(s, nc, xs3[:, :, 1:1 + wdt], ln3[:, :, 0:wdt], sq3[:, :, 0:wdt],
                   srow, srow_bf, ps_stA, ps_stB, ps_a, ps_b, dd3[:, :, 0:wdt],
                   onk_t, onb_t, wdt)

        @s.step("vector")
        def _(eng, wdt=wdt):
            if wdt > 1:
                eng.tensor_tensor(dd3[:, :, 1:wdt], ln3[:, :, 1:wdt], ln3[:, :, 0:wdt - 1],
                                  ALU.subtract)
                eng.drain()
            return eng.tensor_tensor(dd3[:, :, 0:1], ln3[:, :, 0:1], lnp3, ALU.subtract)

        @s.step("vector")
        def _(eng, wdt=wdt):
            for c in range(DC):
                if wdt > 1:
                    eng.scalar_tensor_tensor(xr3[:, c, 1:wdt], dd3[:, c, 1:wdt], mur_t[:, c:c + 1],
                                             ln3[:, c, 0:wdt - 1], ALU.mult, ALU.add)
                eng.scalar_tensor_tensor(xr3[:, c, 0:1], dd3[:, c, 0:1], mur_t[:, c:c + 1],
                                         lnp3[:, c, :], ALU.mult, ALU.add)
                if wdt > 1:
                    eng.scalar_tensor_tensor(xk3[:, c, 1:wdt], dd3[:, c, 1:wdt], muk_t[:, c:c + 1],
                                             ln3[:, c, 0:wdt - 1], ALU.mult, ALU.add)
                eng.scalar_tensor_tensor(xk3[:, c, 0:1], dd3[:, c, 0:1], muk_t[:, c:c + 1],
                                         lnp3[:, c, :], ALU.mult, ALU.add)
            return eng.tensor_copy(lnp3, ln3[:, :, wdt - 1:wdt])

        def post_r(mcs, wdt=wdt):
            @s.step("scalar", nodrain=True)
            def _(eng):
                for idx, mc in enumerate(mcs):
                    i = eng.activation(rs_t[:, mc * nt:mc * nt + wdt], ps_list[idx][:, 0:wdt], AF.Sigmoid)
                return i

        emit_proj_grouped(s, wr_b, xr3, ps_list, wdt, DC, post_r)

        def post_k(mcs, wdt=wdt):
            @s.step("scalar", nodrain=True)
            def _(eng):
                for idx, mc in enumerate(mcs):
                    eng.activation(krelu[:, idx * nt:idx * nt + wdt], ps_list[idx][:, 0:wdt], AF.Relu)
                eng.drain()
                for idx, mc in enumerate(mcs):
                    i = eng.activation(ka_t[:, mc * nt:mc * nt + wdt], krelu[:, idx * nt:idx * nt + wdt], AF.Square)
                return i

        emit_proj_grouped(s, wk_b, xk3, ps_list, wdt, DC, post_k)

        def post_v(mcs, wdt=wdt, col0=col0):
            @s.step("vector", nodrain=True)
            def _(eng):
                for idx, mc in enumerate(mcs):
                    eng.tensor_tensor(hout[:, idx * nt:idx * nt + wdt], rs_t[:, mc * nt:mc * nt + wdt],
                                      ps_list[idx][:, 0:wdt], ALU.mult)
                eng.drain()
                for idx, mc in enumerate(mcs):
                    i = eng.scalar_tensor_tensor(hout[:, idx * nt:idx * nt + wdt], hout[:, idx * nt:idx * nt + wdt],
                                                 0.5 if half else 1.0,
                                                 xs3[:, mc, 1:1 + wdt], ALU.mult, ALU.add)
                return i

            @s.dma("sync")
            def _(eng):
                return [eng.dma_start(out=h[mc * 128:(mc + 1) * 128, col0:col0 + wdt],
                                      in_=hout[:, idx * nt:idx * nt + wdt])
                        for idx, mc in enumerate(mcs) for h in xo_hbms]

        emit_proj_grouped(s, wv_b, ka3, ps_list, wdt, DC, post_v)

        col0 += wdt
    return bufs


def build_L1():
    nc = bass.Bass("TRN2", target_bir_lowering=False, debug=False)
    xs = nc.dram_tensor("xs", [D, TPC + 1], F32, kind="ExternalInput").ap()
    Wr = nc.dram_tensor("Wr", [D, D], F32, kind="ExternalInput").ap()
    Wk = nc.dram_tensor("Wk", [D, D], F32, kind="ExternalInput").ap()
    Wv = nc.dram_tensor("Wv", [D, D], F32, kind="ExternalInput").ap()
    mur = nc.dram_tensor("mur", [128, DC], F32, kind="ExternalInput").ap()
    muk = nc.dram_tensor("muk", [128, DC], F32, kind="ExternalInput").ap()
    Wq = nc.dram_tensor("Wq", [D, D], F32, kind="ExternalInput").ap()
    Wf = nc.dram_tensor("Wf", [D, D], F32, kind="ExternalInput").ap()
    Wi = nc.dram_tensor("Wi", [D, D], F32, kind="ExternalInput").ap()
    onk = nc.dram_tensor("ones_k", [128, 1], BF16, kind="ExternalInput").ap()
    onb = nc.dram_tensor("ones_b", [1, 128], BF16, kind="ExternalInput").ap()
    msku = nc.dram_tensor("mask_u", [CC, CC], F32, kind="ExternalInput").ap()
    idin = nc.dram_tensor("ident", [128, 128], BF16, kind="ExternalInput").ap()
    x1o = nc.dram_tensor("x1o", [D, TPC], F32, kind="ExternalOutput").ap()
    oloc = nc.dram_tensor("oloc", [D, TPC], BF16, kind="ExternalOutput").ap()
    qto = nc.dram_tensor("qto", [D, TPC], BF16, kind="ExternalOutput").ap()
    bco = nc.dram_tensor("bco", [128, 8 * NCH], F32, kind="ExternalOutput").ap()
    sloco = nc.dram_tensor("sloco", [128, 8 * 128], F32, kind="ExternalOutput").ap()
    atoto = nc.dram_tensor("atoto", [128, 8], F32, kind="ExternalOutput").ap()
    ln1 = nc.dram_tensor("ln1", [D, TPC], BF16).ap()

    LNSC = float(np.log(128.0 ** -0.5))
    nt = 512
    ntile = TPC // nt
    nch = nt // CC

    with ExitStack() as st:
        al = Alloc(nc, st)
        tk = al.sem("tk")
        gk = al.sem("gk")
        block = al.block()
        s = S(nc, block, tk, gk)

        onk_t = al.sb("onk_t", [128, 1], BF16)
        onb_t = al.sb("onb_t", [1, 128], BF16)
        msk_t = al.sb("msk_t", [CC, CC], F32)
        ident = al.sb("ident_t", [128, 128], BF16)

        @s.dma("sync")
        def _(eng):
            return [eng.dma_start(out=onk_t, in_=onk[:, :]),
                    eng.dma_start(out=onb_t, in_=onb[:, :]),
                    eng.dma_start(out=msk_t, in_=msku[:, :]),
                    eng.dma_start(out=ident, in_=idin[:, :])]

        with ExitStack() as stA:
            alA = Alloc(nc, stA)
            wstage = alA.sb("A_wstage", [128, D], F32)
            wr_b = alA.sb("A_wr_b", [128, DC * D], BF16)
            wk_b = alA.sb("A_wk_b", [128, DC * D], BF16)
            wv_b = alA.sb("A_wv_b", [128, DC * D], BF16)
            mur_t = alA.sb("A_mur", [128, DC], F32)
            muk_t = alA.sb("A_muk", [128, DC], F32)

            @s.dma("sync")
            def _(eng):
                return [eng.dma_start(out=mur_t, in_=mur[:, :]),
                        eng.dma_start(out=muk_t, in_=muk[:, :])]

            emit_weight_load(s, wstage, wr_b, Wr, DC, D)
            emit_weight_load(s, wstage, wk_b, Wk, DC, D)
            emit_weight_load(s, wstage, wv_b, Wv, DC, D)
            x1int = nc.dram_tensor("x1int", [D, TPC], F32).ap()
            fb = emit_ffn(s, nc, alA, "A_", xs, [x1o, x1int], TPC, 1,
                          wr_b, wk_b, wv_b, mur_t, muk_t, onk_t, onb_t)

            xt3 = fb["xs_t"].rearrange("p (c t) -> p c t", c=DC)[:, :, 0:512]
            lt3 = fb["ln_t"].rearrange("p (c t) -> p c t", c=DC)
            sq3 = fb["sq_t"].rearrange("p (c t) -> p c t", c=DC)
            tm3 = fb["dd_t"].rearrange("p (c t) -> p c t", c=DC)
            for u in range(4):
                @s.dma("sync")
                def _(eng, u=u):
                    return [eng.dma_start(out=xt3,
                                          in_=x1int.rearrange("(c p) t -> p c t", p=128)[:, :, u * 512:(u + 1) * 512])]

                emit_ln_fm(s, nc, xt3, lt3, sq3, fb["srow"], fb["srow_bf"],
                           fb["ps_stA"], fb["ps_stB"], fb["ps_a"], fb["ps_b"],
                           tm3, onk_t, onb_t, 512)

                @s.dma("sync")
                def _(eng, u=u):
                    return [eng.dma_start(out=ln1.rearrange("(c p) t -> p c t", p=128)[:, :, u * 512:(u + 1) * 512],
                                          in_=lt3)]

        with ExitStack() as stB:
            alB = Alloc(nc, stB)
            ln_t = alB.sb("B_ln", [128, DC * nt], BF16)
            f_t = alB.sb("B_f", [128, 2 * nt], F32)
            g_t = alB.sb("B_g", [128, 2 * nt], F32)
            sc1_t = alB.sb("B_sc1", [128, 2 * nt], F32)
            arel_t = alB.sb("B_arel", [128, 2 * nt], F32)
            apad_t = alB.sb("B_apad", [128, 2 * (nt + 1)], F32)
            eb_t = alB.sb("B_eb", [128, 2 * nt], BF16)
            qs_t = alB.sb("B_qs", [128, 2 * nt], BF16)
            qt_t = alB.sb("B_qt", [128, 2 * nt], BF16)
            kg_t = alB.sb("B_kg", [128, 2 * nt], BF16)
            kt_t = alB.sb("B_kt", [128, 2 * nt], BF16)
            kh_t = alB.sb("B_kh", [128, 2 * nt], BF16)
            v_t = alB.sb("B_v", [128, (nt // 128) * 256], BF16)
            ob_t = alB.sb("B_ob", [128, 2 * nt], BF16)
            oacc_t = alB.sb("B_oacc", [128, 2 * TPC], BF16)
            qacc_t = alB.sb("B_qacc", [128, 2 * TPC], BF16)
            d_t = alB.sb("B_d", [128, 2 * nch], F32)
            bc_t = alB.sb("B_bc", [128, 2 * NCH], F32)
            zcol = alB.sb("B_zcol", [128, 1], F32)
            lnsc_t = alB.sb("B_lnsc", [128, 1], F32)
            carry = alB.sb("B_carry", [128, 2], F32)
            s_f32 = alB.sb("B_sf32", [128, 2 * 128], F32)
            s_bf = alB.sb("B_sbf", [128, 2 * 128], BF16)
            pm_t = alB.sb("B_pm", [128, 2 * CC], BF16)
            ktm_t = alB.sb("B_ktm", [128, 2 * 128], BF16)
            wq_b = alB.sb("B_wq", [128, DC * 256], BF16)
            wf_b = alB.sb("B_wf", [128, DC * 256], BF16)
            wi_b = alB.sb("B_wi", [128, DC * 256], BF16)
            wstage = alB.sb("B_wstage", [128, 256], F32)
            ps_mm = alB.ps("B_psmm", [128, 512], F32)
            ps_o2 = alB.ps("B_pso2", [128, 512], F32)
            ps_p = alB.ps("B_psp", [64, 2 * CC], F32)
            ps_o = alB.ps("B_pso", [128, 2 * CC], F32)
            ps_u = alB.ps("B_psu", [128, 2 * 128], F32)
            ps_tr = alB.ps("B_pstr", [64, 2 * 128], BF16)

            ln3 = ln_t.rearrange("p (c t) -> p c t", c=DC)

            for hp in range(4):
                @s.dma("gpsimd")
                def _(eng, hp=hp):
                    return [eng.dma_start(out=buf.rearrange("p (c m) -> p c m", c=DC),
                                          in_=hbm[:, hp * 256:(hp + 1) * 256].rearrange("(c p) m -> p c m", p=128))
                            for hbm, buf in ((Wq, wq_b), (Wf, wf_b), (Wi, wi_b))]

                @s.step("vector")
                def _(eng):
                    eng.memset(zcol, 0.0)
                    eng.memset(lnsc_t, LNSC)
                    eng.memset(carry, 0.0)
                    eng.memset(s_f32, 0.0)
                    return eng.memset(s_bf, 0.0)

                for j in range(ntile):
                    t0 = j * nt

                    @s.dma("sync")
                    def _(eng, t0=t0):
                        return [eng.dma_start(out=ln3,
                                              in_=ln1.rearrange("(c p) t -> p c t", p=128)[:, :, t0:t0 + nt])]

                    @s.step("tensor")
                    def _(eng):
                        for hc in range(2):
                            ps = ps_mm if hc == 0 else ps_o2
                            for kc in range(DC):
                                i = eng.matmul(ps[:, 0:nt],
                                               wf_b[:, kc * 256 + hc * 128: kc * 256 + (hc + 1) * 128],
                                               ln3[:, kc, :], start=(kc == 0), stop=(kc == DC - 1))
                        return i

                    @s.step("scalar", nodrain=True)
                    def _(eng):
                        eng.copy(f_t[:, 0:nt], ps_mm[:, 0:nt])
                        return eng.copy(f_t[:, nt:2 * nt], ps_o2[:, 0:nt])

                    @s.step("scalar", nodrain=True)
                    def _(eng):
                        eng.activation(kg_t, f_t, AF.Sigmoid, scale=-1.0)
                        return eng.activation(sc1_t, f_t, AF.Relu, scale=-1.0)

                    @s.step("vector")
                    def _(eng):
                        return eng.scalar_tensor_tensor(g_t, f_t, -1.0, f_t, ALU.mult, ALU.max)

                    @s.step("scalar")
                    def _(eng):
                        eng.activation(g_t, g_t, AF.Exp, scale=-1.0)
                        eng.drain()
                        return eng.activation(g_t, g_t, AF.Ln, bias=1.0)

                    @s.step("vector")
                    def _(eng):
                        eng.tensor_tensor(g_t, g_t, sc1_t, ALU.add)
                        eng.drain()
                        return eng.tensor_single_scalar(g_t, g_t, -1.0, ALU.mult)

                    @s.step("vector")
                    def _(eng):
                        eng.tensor_copy(apad_t[:, 0:1], carry[:, 0:1])
                        eng.tensor_copy(apad_t[:, nt + 1:nt + 2], carry[:, 1:2])
                        eng.drain()
                        d0, dz0 = bcast(g_t[:, 0:nt], zcol)
                        eng.tensor_tensor_scan(apad_t[:, 1:nt + 1], d0, dz0, carry[:, 0:1], ALU.add, ALU.add)
                        eng.drain()
                        d1, dz1 = bcast(g_t[:, nt:2 * nt], zcol)
                        eng.tensor_tensor_scan(apad_t[:, nt + 2:2 * nt + 2], d1, dz1, carry[:, 1:2], ALU.add, ALU.add)
                        eng.drain()
                        eng.tensor_copy(carry[:, 0:1], apad_t[:, nt:nt + 1])
                        return eng.tensor_copy(carry[:, 1:2], apad_t[:, 2 * nt + 1:2 * nt + 2])

                    @s.step("vector")
                    def _(eng):
                        for h in range(2):
                            off = h * (nt + 1)
                            a_in = apad_t[:, off + 1:off + nt + 1].rearrange("p (c t) -> p c t", t=CC)
                            base = apad_t[:, off:off + nt].rearrange("p (c t) -> p c t", t=CC)[:, :, 0:1]
                            am, bm = bcast(a_in, base)
                            eng.tensor_tensor(arel_t[:, h * nt:(h + 1) * nt].rearrange("p (c t) -> p c t", t=CC),
                                              am, bm, ALU.subtract)
                            eng.drain()
                            ends = a_in[:, :, CC - 1:CC]
                            bm2, am2 = bcast(a_in, ends)
                            i = eng.tensor_tensor(sc1_t[:, h * nt:(h + 1) * nt].rearrange("p (c t) -> p c t", t=CC),
                                                  am2, bm2, ALU.subtract)
                            eng.drain()
                        return i

                    @s.step("scalar")
                    def _(eng, j=j):
                        for h in range(2):
                            eng.activation(
                                d_t[:, h * nch:(h + 1) * nch].rearrange("p (c o) -> p c o", o=1),
                                arel_t[:, h * nt:(h + 1) * nt].rearrange("p (c t) -> p c t", t=CC)[:, :, CC - 1:CC],
                                AF.Exp)
                            eng.drain()
                            i = eng.activation(
                                bc_t[:, h * NCH + j * nch:h * NCH + (j + 1) * nch].rearrange("p (c o) -> p c o", o=1),
                                apad_t[:, h * (nt + 1):h * (nt + 1) + nt].rearrange("p (c t) -> p c t", t=CC)[:, :, 0:1],
                                AF.Exp)
                            eng.drain()
                        return i

                    @s.step("scalar")
                    def _(eng):
                        return eng.activation(eb_t, arel_t, AF.Exp, scale=-1.0)

                    @s.step("vector")
                    def _(eng):
                        return eng.tensor_tensor(kt_t, kg_t, eb_t, ALU.mult)

                    @s.step("scalar")
                    def _(eng):
                        return eng.activation(eb_t, sc1_t, AF.Exp)

                    @s.step("vector")
                    def _(eng):
                        return eng.tensor_tensor(kh_t, kg_t, eb_t, ALU.mult)

                    @s.step("tensor")
                    def _(eng):
                        for hc in range(2):
                            ps = ps_mm if hc == 0 else ps_o2
                            for kc in range(DC):
                                i = eng.matmul(ps[:, 0:nt],
                                               wq_b[:, kc * 256 + hc * 128: kc * 256 + (hc + 1) * 128],
                                               ln3[:, kc, :], start=(kc == 0), stop=(kc == DC - 1))
                        return i

                    @s.step("scalar", nodrain=True)
                    def _(eng):
                        eng.activation(qs_t[:, 0:nt], ps_mm[:, 0:nt], AF.Silu)
                        return eng.activation(qs_t[:, nt:2 * nt], ps_o2[:, 0:nt], AF.Silu)

                    @s.step("scalar")
                    def _(eng):
                        return eng.activation(eb_t, arel_t, AF.Exp, bias=lnsc_t)

                    @s.step("vector")
                    def _(eng):
                        return eng.tensor_tensor(qt_t, qs_t, eb_t, ALU.mult)

                    @s.step("tensor")
                    def _(eng):
                        for tb in range(nt // 128):
                            ps = (ps_mm if tb < 2 else ps_o2)[:, (tb % 2) * 256:(tb % 2) * 256 + 256]
                            for kc in range(DC):
                                i = eng.matmul(ps,
                                               ln3[:, kc, tb * 128:(tb + 1) * 128],
                                               wi_b[:, kc * 256:(kc + 1) * 256],
                                               start=(kc == 0), stop=(kc == DC - 1))
                        return i

                    @s.step("scalar", nodrain=True)
                    def _(eng):
                        for tb in range(nt // 128):
                            ps = (ps_mm if tb < 2 else ps_o2)[:, (tb % 2) * 256:(tb % 2) * 256 + 256]
                            i = eng.copy(v_t[:, tb * 256:(tb + 1) * 256], ps)
                        return i

                    prev_v1 = None
                    for cc_ in range(nch):
                        c0 = cc_ * CC
                        tb, poff = (c0 // 128), (c0 % 128)

                        @s.step("tensor", wait_abs=prev_v1)
                        def _(eng, c0=c0):
                            for h in range(2):
                                eng.matmul(ps_p[:, h * CC:(h + 1) * CC],
                                           kt_t[:, h * nt + c0:h * nt + c0 + CC],
                                           qt_t[:, h * nt + c0:h * nt + c0 + CC],
                                           start=True, stop=True)
                            for h in range(2):
                                i = eng.transpose(ps_tr[:, h * 128:(h + 1) * 128],
                                                  kh_t[:, h * nt + c0:h * nt + c0 + CC], ident)
                            return i

                        @s.step("vector", nodrain=True)
                        def _(eng, poff=poff):
                            for h in range(2):
                                eng.tensor_tensor(pm_t[poff:poff + CC, h * CC:(h + 1) * CC],
                                                  ps_p[:, h * CC:(h + 1) * CC], msk_t, ALU.mult)
                            return eng.tensor_copy(ktm_t[poff:poff + CC, :], ps_tr[:, :])

                        prev_v1 = s.target

                        @s.step("tensor")
                        def _(eng, c0=c0, tb=tb, poff=poff):
                            for h in range(2):
                                eng.matmul(ps_o[:, h * CC:(h + 1) * CC],
                                           s_bf[:, h * 128:(h + 1) * 128],
                                           qt_t[:, h * nt + c0:h * nt + c0 + CC],
                                           start=True, stop=False)
                                eng.matmul(ps_o[:, h * CC:(h + 1) * CC],
                                           v_t[poff:poff + CC, tb * 256 + h * 128: tb * 256 + (h + 1) * 128],
                                           pm_t[poff:poff + CC, h * CC:(h + 1) * CC],
                                           start=False, stop=True)
                            for h in range(2):
                                i = eng.matmul(ps_u[:, h * 128:(h + 1) * 128],
                                               ktm_t[poff:poff + CC, h * 128:(h + 1) * 128],
                                               v_t[poff:poff + CC, tb * 256 + h * 128: tb * 256 + (h + 1) * 128],
                                               start=True, stop=True)
                            return i

                        @s.step("vector", nodrain=True)
                        def _(eng, cc_=cc_, c0=c0):
                            for h in range(2):
                                eng.tensor_copy(ob_t[:, h * nt + c0:h * nt + c0 + CC],
                                                ps_o[:, h * CC:(h + 1) * CC])
                            for h in range(2):
                                eng.scalar_tensor_tensor(s_f32[:, h * 128:(h + 1) * 128],
                                                         s_f32[:, h * 128:(h + 1) * 128],
                                                         d_t[:, h * nch + cc_:h * nch + cc_ + 1],
                                                         ps_u[:, h * 128:(h + 1) * 128],
                                                         ALU.mult, ALU.add)
                            eng.drain()
                            return eng.tensor_copy(s_bf, s_f32)

                    @s.step("vector", nodrain=True)
                    def _(eng, t0=t0):
                        for h in range(2):
                            eng.tensor_copy(oacc_t[:, h * TPC + t0:h * TPC + t0 + nt],
                                            ob_t[:, h * nt:(h + 1) * nt])
                            i = eng.tensor_copy(qacc_t[:, h * TPC + t0:h * TPC + t0 + nt],
                                                qt_t[:, h * nt:(h + 1) * nt])
                        return i

                @s.dma("sync")
                def _(eng, hp=hp):
                    return [eng.dma_start(out=oloc.rearrange("(c p) t -> p c t", p=128)[:, 2 * hp:2 * hp + 2, :],
                                          in_=oacc_t.rearrange("p (c t) -> p c t", c=2)),
                            eng.dma_start(out=qto.rearrange("(c p) t -> p c t", p=128)[:, 2 * hp:2 * hp + 2, :],
                                          in_=qacc_t.rearrange("p (c t) -> p c t", c=2)),
                            eng.dma_start(out=sloco[:, hp * 256:(hp + 1) * 256], in_=s_f32),
                            eng.dma_start(out=atoto[:, hp * 2:hp * 2 + 2], in_=carry),
                            eng.dma_start(out=bco[:, hp * 2 * NCH:(hp + 1) * 2 * NCH], in_=bc_t)]

        s.fence()
        nc._steps = (s.n_steps, s.n_dmas, s.target, s.gtarget)
    return nc

def emit_moe(s, nc, al, tag, xs_hbm, xo_hbms, ncols, wb_b, aa_b, ba_b, rw_b,
             bb_t, rb_t, e2_t, id8_t, id128_t, onk_t, onb_t):
    widths = []
    rem = ncols
    while rem > 0:
        w = min(512, rem)
        widths.append(w)
        rem -= w
    nt = 512
    xs_t = al.sb(f"{tag}xs", [128, DC * nt], F32)
    ln_t = al.sb(f"{tag}ln", [128, DC * nt], BF16)
    sq_t = al.sb(f"{tag}sq", [128, DC * nt], BF16)
    lntmp_t = al.sb(f"{tag}lntmp", [128, DC * nt], F32)
    ax_t = al.sb(f"{tag}ax", [128, nt], BF16)
    axs_t = al.sb(f"{tag}axs", [128, nt], BF16)
    lg_t = al.sb(f"{tag}lg", [8, nt], BF16)
    lgtm_t = al.sb(f"{tag}lgtm", [128, 4 * 8], F32)
    wtok_t = al.sb(f"{tag}wtok", [128, 4 * 8], BF16)
    wfm_t = al.sb(f"{tag}wfm", [8, nt], BF16)
    srow = al.sb(f"{tag}srow", [1, 4 * nt], F32)
    srow_bf = al.sb(f"{tag}srowb", [1, 2 * nt], BF16)
    stat_t = al.sb(f"{tag}stat", [128, 32], F32)
    hout = al.sb(f"{tag}hout", [128, 4 * nt], F32)
    ps_a = al.ps(f"{tag}psa", [128, nt], F32)
    ps_b = al.ps(f"{tag}psb", [128, nt], F32)
    ps_mm = al.ps(f"{tag}psmm", [128, nt], F32)
    ps_ax = al.ps(f"{tag}psax", [128, nt], F32)
    ps_list = [ps_mm, ps_a, ps_b, ps_ax]
    ps_lg = al.ps(f"{tag}pslg", [8, nt], F32)
    ps_t1 = al.ps(f"{tag}pst1", [128, 8], BF16)
    ps_t2 = al.ps(f"{tag}pst2", [8, 128], BF16)
    ps_stA = ps_a[0:1, :]
    ps_stB = ps_b[0:1, :]

    xs3 = xs_t.rearrange("p (c t) -> p c t", c=DC)
    ln3 = ln_t.rearrange("p (c t) -> p c t", c=DC)
    sq3 = sq_t.rearrange("p (c t) -> p c t", c=DC)
    lntmp3 = lntmp_t.rearrange("p (c t) -> p c t", c=DC)

    col0 = 0
    for wdt in widths:
        blks = []
        k0 = 0
        while k0 < wdt:
            blks.append((k0, min(128, wdt - k0)))
            k0 += min(128, wdt - k0)

        @s.dma("sync")
        def _(eng, col0=col0, wdt=wdt):
            return [eng.dma_start(out=xs3[:, :, 0:wdt],
                                  in_=xs_hbm.rearrange("(c p) t -> p c t", p=128)[:, :, col0:col0 + wdt])]

        emit_ln_fm(s, nc, xs3[:, :, 0:wdt], ln3[:, :, 0:wdt], sq3[:, :, 0:wdt],
                   srow, srow_bf, ps_stA, ps_stB, ps_a, ps_b, lntmp3[:, :, 0:wdt],
                   onk_t, onb_t, wdt)

        @s.step("tensor")
        def _(eng, wdt=wdt):
            for kc in range(DC):
                i = eng.matmul(ps_ax[:, 0:wdt], aa_b[:, kc * 128:(kc + 1) * 128],
                               ln3[:, kc, 0:wdt], start=(kc == 0), stop=(kc == DC - 1))
            return i

        @s.step("vector")
        def _(eng, wdt=wdt):
            return eng.tensor_copy(ax_t[:, 0:wdt], ps_ax[:, 0:wdt])

        @s.step("tensor")
        def _(eng, wdt=wdt):
            for kc in range(DC):
                i = eng.matmul(ps_lg[:, 0:wdt], rw_b[:, kc * 8:(kc + 1) * 8],
                               ln3[:, kc, 0:wdt], start=(kc == 0), stop=(kc == DC - 1))
            return i

        @s.step("vector")
        def _(eng, wdt=wdt):
            return eng.tensor_scalar_add(lg_t[:, 0:wdt], ps_lg[:, 0:wdt], rb_t[:, 0:1])

        for nb, (b0, bw) in enumerate(blks):
            @s.step("tensor")
            def _(eng, b0=b0, bw=bw):
                return eng.transpose(ps_t1[0:bw, :], lg_t[:, b0:b0 + bw], id8_t)

            @s.step("vector")
            def _(eng, nb=nb, bw=bw):
                return eng.tensor_copy(lgtm_t[0:bw, nb * 8:(nb + 1) * 8], ps_t1[0:bw, :])

        for nb, (b0, bw) in enumerate(blks):
            LG = lgtm_t[0:bw, nb * 8:(nb + 1) * 8]
            m1 = stat_t[0:bw, 0:1]; cnt = stat_t[0:bw, 1:2]; m2 = stat_t[0:bw, 2:3]
            kth = stat_t[0:bw, 3:4]; nm1 = stat_t[0:bw, 4:5]; ssum = stat_t[0:bw, 5:6]
            eqb = stat_t[0:bw, 8:16]; rbuf = stat_t[0:bw, 16:24]; ebuf = stat_t[0:bw, 24:32]

            @s.step("vector")
            def _(eng, LG=LG, m1=m1, cnt=cnt, m2=m2, kth=kth, nm1=nm1,
                  ssum=ssum, eqb=eqb, rbuf=rbuf, ebuf=ebuf):
                eng.tensor_reduce(m1, LG, AXX, ALU.max)
                am, bm = bcast(LG, m1)
                eng.tensor_tensor(eqb, am, bm, ALU.subtract)
                eng.tensor_single_scalar(eqb, eqb, 0.0, ALU.is_equal)
                eng.tensor_reduce(cnt, eqb, AXX, ALU.add)
                eng.scalar_tensor_tensor(rbuf, eqb, -1e9, LG, ALU.mult, ALU.add)
                eng.tensor_reduce(m2, rbuf, AXX, ALU.max)
                eng.tensor_single_scalar(cnt, cnt, 2.0, ALU.is_ge)
                eng.tensor_tensor(kth, m1, m2, ALU.subtract)
                eng.tensor_tensor(kth, kth, cnt, ALU.mult)
                eng.tensor_tensor(kth, kth, m2, ALU.add)
                am2, bm2 = bcast(LG, kth)
                eng.tensor_tensor(eqb, am2, bm2, ALU.subtract)
                eng.tensor_single_scalar(eqb, eqb, 0.0, ALU.is_ge)
                return eng.tensor_single_scalar(nm1, m1, -1.0, ALU.mult)

            @s.step("scalar")
            def _(eng, LG=LG, nm1=nm1, ebuf=ebuf):
                return eng.activation(ebuf, LG, AF.Exp, bias=nm1)

            @s.step("vector")
            def _(eng, eqb=eqb, ebuf=ebuf, ssum=ssum, nb=nb, bw=bw):
                eng.tensor_tensor(ebuf, ebuf, eqb, ALU.mult)
                eng.tensor_reduce(ssum, ebuf, AXX, ALU.add)
                eng.reciprocal(ssum, ssum)
                am, bm = bcast(ebuf, ssum)
                return eng.tensor_tensor(wtok_t[0:bw, nb * 8:(nb + 1) * 8], am, bm, ALU.mult)

        for nb, (b0, bw) in enumerate(blks):
            @s.step("tensor")
            def _(eng, nb=nb, b0=b0, bw=bw):
                return eng.transpose(ps_t2[:, 0:bw], wtok_t[0:bw, nb * 8:(nb + 1) * 8], id128_t[0:bw, 0:bw])

            @s.step("vector")
            def _(eng, nb=nb, b0=b0, bw=bw):
                return eng.tensor_copy(wfm_t[:, b0:b0 + bw], ps_t2[:, 0:bw])

        @s.step("tensor")
        def _(eng, wdt=wdt):
            return eng.matmul(ps_ax[0:128, 0:wdt], e2_t, wfm_t[:, 0:wdt], start=True, stop=True)

        @s.step("vector")
        def _(eng, wdt=wdt):
            return eng.tensor_tensor(axs_t[:, 0:wdt], ax_t[:, 0:wdt], ps_ax[0:128, 0:wdt], ALU.mult)

        prev_evac = None
        for g0 in range(0, DC, 4):
            mcs = list(range(g0, g0 + 4))

            @s.step("tensor", wait_abs=prev_evac)
            def _(eng, mcs=mcs, wdt=wdt):
                for idx, mc in enumerate(mcs):
                    ps = ps_list[idx]
                    for kc in range(DC):
                        eng.matmul(ps[:, 0:wdt], wb_b[:, kc * D + mc * 128: kc * D + (mc + 1) * 128],
                                   ln3[:, kc, 0:wdt], start=(kc == 0), stop=False)
                    i = eng.matmul(ps[:, 0:wdt], ba_b[:, mc * 128:(mc + 1) * 128],
                                   axs_t[:, 0:wdt], start=False, stop=True)
                return i

            prev_evac = s.target + 1

            @s.step("vector", nodrain=True)
            def _(eng, mcs=mcs, wdt=wdt):
                for idx, mc in enumerate(mcs):
                    eng.tensor_scalar_add(hout[:, idx * nt:idx * nt + wdt], ps_list[idx][:, 0:wdt],
                                          bb_t[:, mc:mc + 1])
                eng.drain()
                for idx, mc in enumerate(mcs):
                    i = eng.scalar_tensor_tensor(hout[:, idx * nt:idx * nt + wdt],
                                                 hout[:, idx * nt:idx * nt + wdt], 0.5,
                                                 xs3[:, mc, 0:wdt], ALU.mult, ALU.add)
                return i

            @s.dma("sync")
            def _(eng, mcs=mcs, col0=col0, wdt=wdt):
                return [eng.dma_start(out=h[mc * 128:(mc + 1) * 128, col0:col0 + wdt],
                                      in_=hout[:, idx * nt:idx * nt + wdt])
                        for idx, mc in enumerate(mcs) for h in xo_hbms]

        col0 += wdt

def emit_attnout(s, nc, al, tag, x1_hbm, o_hbm, xo_hbms, ncols, wo_b,
                 onk_t, onb_t):
    widths = []
    rem = ncols
    while rem > 0:
        w = min(512, rem)
        widths.append(w)
        rem -= w
    nt = 512
    x1_t = al.sb(f"{tag}x1", [128, DC * nt], F32)
    o_t = al.sb(f"{tag}o", [128, DC * nt], BF16)
    on_t = al.sb(f"{tag}on", [128, DC * nt], BF16)
    sq_t = al.sb(f"{tag}sq", [128, DC * nt], BF16)
    srow = al.sb(f"{tag}srow", [1, 4 * nt], F32)
    srow_bf = al.sb(f"{tag}srowb", [1, 2 * nt], BF16)
    hout = al.sb(f"{tag}hout", [128, 4 * nt], F32)
    ps_a = al.ps(f"{tag}psa", [128, nt], F32)
    ps_mm = al.ps(f"{tag}psmm", [128, nt], F32)
    ps_b = al.ps(f"{tag}psb", [128, nt], F32)
    ps_c = al.ps(f"{tag}psc", [128, nt], F32)
    ps_list = [ps_mm, ps_a, ps_b, ps_c]
    x13 = x1_t.rearrange("p (c t) -> p c t", c=DC)
    o3 = o_t.rearrange("p (c t) -> p c t", c=DC)
    on3 = on_t.rearrange("p (c t) -> p c t", c=DC)
    sq3 = sq_t.rearrange("p (c t) -> p c t", c=DC)
    RSTD = srow[0:1, 0:nt]
    RSTD_BF = srow_bf[0:1, 0:nt]

    col0 = 0
    for wdt in widths:
        @s.dma("sync")
        def _(eng, col0=col0, wdt=wdt):
            return [eng.dma_start(out=x13[:, :, 0:wdt],
                                  in_=x1_hbm.rearrange("(c p) t -> p c t", p=128)[:, :, col0:col0 + wdt]),
                    eng.dma_start(out=o3[:, :, 0:wdt],
                                  in_=o_hbm.rearrange("(c p) t -> p c t", p=128)[:, :, col0:col0 + wdt])]

        @s.step("scalar")
        def _(eng, wdt=wdt):
            return eng.activation(sq3[:, :, 0:wdt], o3[:, :, 0:wdt], AF.Square)

        @s.step("tensor")
        def _(eng, wdt=wdt):
            for c in range(DC):
                i = eng.matmul(ps_a[0:1, 0:wdt], onk_t, sq3[:, c, 0:wdt],
                               start=(c == 0), stop=(c == DC - 1))
            return i

        @s.step("vector")
        def _(eng, wdt=wdt):
            eng.tensor_single_scalar(RSTD[0:1, 0:wdt], ps_a[0:1, 0:wdt], 1.0 / D, ALU.mult)
            eng.drain()
            return eng.tensor_single_scalar(RSTD[0:1, 0:wdt], RSTD[0:1, 0:wdt], LN_EPS, ALU.add)

        @s.step("scalar")
        def _(eng, wdt=wdt):
            return eng.activation(RSTD[0:1, 0:wdt], RSTD[0:1, 0:wdt], AF.Sqrt)

        @s.step("vector")
        def _(eng, wdt=wdt):
            eng.reciprocal(RSTD[0:1, 0:wdt], RSTD[0:1, 0:wdt])
            eng.drain()
            return eng.tensor_copy(RSTD_BF[0:1, 0:wdt], RSTD[0:1, 0:wdt])

        @s.step("tensor")
        def _(eng, wdt=wdt):
            return eng.matmul(ps_a[:, 0:wdt], onb_t, RSTD_BF[0:1, 0:wdt], start=True, stop=True)

        @s.step("vector")
        def _(eng, wdt=wdt):
            am, bm = bcast(o3[:, :, 0:wdt], ps_a[:, 0:wdt].rearrange("p (o t) -> p o t", o=1))
            return eng.tensor_tensor(on3[:, :, 0:wdt], am, bm, ALU.mult)

        def post_o(mcs, wdt=wdt, col0=col0):
            @s.step("vector", nodrain=True)
            def _(eng):
                for idx, mc in enumerate(mcs):
                    i = eng.tensor_tensor(hout[:, idx * nt:idx * nt + wdt], x13[:, mc, 0:wdt],
                                          ps_list[idx][:, 0:wdt], ALU.add)
                return i

            @s.dma("sync")
            def _(eng):
                return [eng.dma_start(out=h[mc * 128:(mc + 1) * 128, col0:col0 + wdt],
                                      in_=hout[:, idx * nt:idx * nt + wdt])
                        for idx, mc in enumerate(mcs) for h in xo_hbms]

        emit_proj_grouped(s, wo_b, on3, ps_list, wdt, DC, post_o)

        col0 += wdt


def emit_l1prep(s, nc, al, tag, xs_hbm, Wq, Wk, Wv, Wb, cq_t, ck_t, cv_t,
                lw_t, mw_t, hmask_t, onk_t, onb_t, qf, kf, vf, beta, loc, mid):
    ntok = TPC
    ntot = ntok + HLO
    xstage = al.sb(f"{tag}xstage", [128, DC * 512], F32)
    ln_t = al.sb(f"{tag}ln", [128, DC * ntot], BF16)
    sq_t = al.sb(f"{tag}sq", [128, DC * 512], BF16)
    lntmp_t = al.sb(f"{tag}lntmp", [128, DC * 512], F32)
    acc0 = al.sb(f"{tag}acc0", [128, ntot], F32)
    acc1 = al.sb(f"{tag}acc1", [128, ntot], F32)
    oout = al.sb(f"{tag}oout", [128, ntok], BF16)
    w_b = al.sb(f"{tag}w_b", [128, DC * D], BF16)
    wb_b = al.sb(f"{tag}wb_b", [128, DC * 4], BF16)
    wstage = al.sb(f"{tag}wstage", [128, D], F32)
    srow = al.sb(f"{tag}srow", [1, 4 * 512], F32)
    srow_bf = al.sb(f"{tag}srowb", [1, 2 * 512], BF16)
    bt_t = al.sb(f"{tag}bt", [4, ntok], F32)
    zbuf = al.sb(f"{tag}zbuf", [128, DC * ntot], BF16)
    vflat = al.sb(f"{tag}vflat", [128, DC * ntot], BF16)
    ps_a = al.ps(f"{tag}psa", [128, 512], F32)
    ps_b = al.ps(f"{tag}psb", [128, 512], F32)
    ps_mm = al.ps(f"{tag}psmm", [128, 512], F32)
    ps_c = al.ps(f"{tag}psc", [128, 512], F32)
    ps_bt = al.ps(f"{tag}psbt", [4, 512], F32)
    ps_list = [ps_mm, ps_a, ps_b, ps_c]
    ps_stA = ps_a[0:1, :]
    ps_stB = ps_b[0:1, :]

    xs3 = xstage.rearrange("p (c t) -> p c t", c=DC)
    ln3 = ln_t.rearrange("p (c t) -> p c t", c=DC)
    sq3 = sq_t.rearrange("p (c t) -> p c t", c=DC)
    lntmp3 = lntmp_t.rearrange("p (c t) -> p c t", c=DC)
    zb3 = zbuf.rearrange("p (c t) -> p c t", c=DC)
    vf3 = vflat.rearrange("p (c t) -> p c t", c=DC)

    for u in range((ntot + 511) // 512):
        c0, c1 = u * 512, min((u + 1) * 512, ntot)

        @s.dma("sync")
        def _(eng, c0=c0, c1=c1):
            return [eng.dma_start(out=xs3[:, :, 0:c1 - c0],
                                  in_=xs_hbm.rearrange("(c p) t -> p c t", p=128)[:, :, c0:c1])]

        emit_ln_fm(s, nc, xs3[:, :, 0:c1 - c0], ln3[:, :, c0:c1],
                   sq3[:, :, 0:c1 - c0], srow, srow_bf, ps_stA, ps_stB,
                   ps_a, ps_b, lntmp3[:, :, 0:c1 - c0], onk_t, onb_t, c1 - c0)

    emit_weight_load(s, wstage, wb_b, Wb, DC, 4)
    for u in range(ntok // 512):
        @s.step("tensor")
        def _(eng, u=u):
            for kc in range(DC):
                i = eng.matmul(ps_bt[:, :], wb_b[:, kc * 4:(kc + 1) * 4],
                               ln3[:, kc, HLO + u * 512:HLO + (u + 1) * 512],
                               start=(kc == 0), stop=(kc == DC - 1))
            return i

        @s.step("scalar")
        def _(eng, u=u):
            return eng.activation(bt_t[:, u * 512:(u + 1) * 512], ps_bt[:, :], AF.Sigmoid)

    @s.dma("sync")
    def _(eng):
        return [eng.dma_start(out=beta[:, :], in_=bt_t)]

    def proj(W_hbm):
        emit_weight_load(s, wstage, w_b, W_hbm, DC, D)
        for u in range((ntot + 511) // 512):
            c0, c1 = u * 512, min((u + 1) * 512, ntot)

            def post_z(mcs, c0=c0, c1=c1):
                @s.step("scalar", nodrain=True)
                def _(eng):
                    for idx, mc in enumerate(mcs):
                        i = eng.copy(zb3[:, mc, c0:c1], ps_list[idx][:, 0:c1 - c0])
                    return i

            emit_proj_grouped(s, w_b, ln3[:, :, c0:c1], ps_list, c1 - c0, DC, post_z)

        @s.step("vector")
        def _(eng):
            am, bm = bcast(zb3[:, :, 0:HLO], hmask_t.rearrange("p (o t) -> p o t", o=1))
            return eng.tensor_tensor(zb3[:, :, 0:HLO], am, bm, ALU.mult)

    def conv4_silu(cw_t, out_fm3, keep_halo):
        lo = 3 if keep_halo else HLO
        nw = ntot - 3 if keep_halo else ntok
        for mc in range(DC):
            @s.step("vector")
            def _(eng, mc=mc, lo=lo, nw=nw):
                eng.tensor_scalar_mul(acc0[:, 0:nw], zb3[:, mc, lo - 3:lo - 3 + nw], cw_t[:, mc * 4 + 0:mc * 4 + 1])
                eng.drain()
                eng.scalar_tensor_tensor(acc1[:, 0:nw], zb3[:, mc, lo - 2:lo - 2 + nw], cw_t[:, mc * 4 + 1:mc * 4 + 2], acc0[:, 0:nw], ALU.mult, ALU.add)
                eng.drain()
                eng.scalar_tensor_tensor(acc0[:, 0:nw], zb3[:, mc, lo - 1:lo - 1 + nw], cw_t[:, mc * 4 + 2:mc * 4 + 3], acc1[:, 0:nw], ALU.mult, ALU.add)
                eng.drain()
                return eng.scalar_tensor_tensor(acc1[:, 0:nw], zb3[:, mc, lo:lo + nw], cw_t[:, mc * 4 + 3:mc * 4 + 4], acc0[:, 0:nw], ALU.mult, ALU.add)

            @s.step("scalar")
            def _(eng, mc=mc, lo=lo, nw=nw):
                off = lo - HLO
                return eng.activation(out_fm3[:, mc, HLO + off:HLO + off + nw], acc1[:, 0:nw], AF.Silu)

    proj(Wv)
    conv4_silu(cv_t, vf3, keep_halo=True)
    @s.dma("sync")
    def _(eng):
        return [eng.dma_start(out=vf.rearrange("(c p) t -> p c t", p=128), in_=vf3[:, :, HLO:ntot])]

    cvo3 = zb3[:, :, 0:ntok]
    for w_t, taps, out_hbm in ((lw_t, 7, loc), (mw_t, 25, mid)):
        for mc in range(DC):
            @s.step("vector")
            def _(eng, mc=mc, w_t=w_t, taps=taps):
                a, b = acc0[:, 0:ntok], acc1[:, 0:ntok]
                eng.tensor_scalar_mul(a, vf3[:, mc, HLO - (taps - 1):HLO - (taps - 1) + ntok],
                                      w_t[:, mc * taps + 0:mc * taps + 1])
                for tp in range(1, taps):
                    eng.drain()
                    sh = HLO - (taps - 1) + tp
                    eng.scalar_tensor_tensor(b, vf3[:, mc, sh:sh + ntok],
                                             w_t[:, mc * taps + tp:mc * taps + tp + 1],
                                             a, ALU.mult, ALU.add)
                    a, b = b, a
                eng.drain()
                return eng.tensor_copy(cvo3[:, mc, :], a)

        @s.dma("sync")
        def _(eng, out_hbm=out_hbm):
            return [eng.dma_start(out=out_hbm.rearrange("(c p) t -> p c t", p=128), in_=cvo3)]

    proj(Wq)
    conv4_silu(cq_t, vf3, keep_halo=False)

    @s.dma("sync")
    def _(eng):
        return [eng.dma_start(out=qf.rearrange("(c p) t -> p c t", p=128), in_=vf3[:, :, HLO:ntot])]

    proj(Wk)
    conv4_silu(ck_t, vf3, keep_halo=False)

    @s.dma("sync")
    def _(eng):
        return [eng.dma_start(out=kf.rearrange("(c p) t -> p c t", p=128), in_=vf3[:, :, HLO:ntot])]

OH = 128
EXT1 = OH + TPC
EXT2 = EXT1 - 1
X4OFF = EXT2 - TPC


def build_L2():
    nc = bass.Bass("TRN2", target_bir_lowering=False, debug=False)
    x1e = nc.dram_tensor("x1e", [D, EXT1], F32, kind="ExternalInput").ap()
    oloc = nc.dram_tensor("oloc", [D, TPC], BF16, kind="ExternalInput").ap()
    qto = nc.dram_tensor("qto", [D, TPC], BF16, kind="ExternalInput").ap()
    bco = nc.dram_tensor("bco", [128, 8 * NCH], F32, kind="ExternalInput").ap()
    oh = nc.dram_tensor("oh", [D, OH], BF16, kind="ExternalInput").ap()
    qth = nc.dram_tensor("qth", [D, OH], BF16, kind="ExternalInput").ap()
    bch = nc.dram_tensor("bch", [128, 16], F32, kind="ExternalInput").ap()
    SL = nc.dram_tensor("SL", [128, 4 * 1024], F32, kind="ExternalInput").ap()
    AT = nc.dram_tensor("AT", [128, 4 * 8], F32, kind="ExternalInput").ap()
    am0 = nc.dram_tensor("am0", [128, 3], F32, kind="ExternalInput").ap()
    am1 = nc.dram_tensor("am1", [128, 3], F32, kind="ExternalInput").ap()
    hmask = nc.dram_tensor("hmask", [128, HLO], F32, kind="ExternalInput").ap()
    Wo = nc.dram_tensor("Wo", [D, D], F32, kind="ExternalInput").ap()
    gw = nc.dram_tensor("gw", [128, DC], F32, kind="ExternalInput").ap()
    Wr = nc.dram_tensor("Wr", [D, D], F32, kind="ExternalInput").ap()
    Wk = nc.dram_tensor("Wk", [D, D], F32, kind="ExternalInput").ap()
    Wv = nc.dram_tensor("Wv", [D, D], F32, kind="ExternalInput").ap()
    mur = nc.dram_tensor("mur", [128, DC], F32, kind="ExternalInput").ap()
    muk = nc.dram_tensor("muk", [128, DC], F32, kind="ExternalInput").ap()
    bwT = nc.dram_tensor("bwT", [D, D], F32, kind="ExternalInput").ap()
    bb = nc.dram_tensor("bb", [128, DC], F32, kind="ExternalInput").ap()
    Aall = nc.dram_tensor("Aall", [D, 128], F32, kind="ExternalInput").ap()
    Ball = nc.dram_tensor("Ball", [128, D], F32, kind="ExternalInput").ap()
    rwT = nc.dram_tensor("rwT", [D, 8], F32, kind="ExternalInput").ap()
    rb = nc.dram_tensor("rb", [8, 1], F32, kind="ExternalInput").ap()
    E2 = nc.dram_tensor("E2", [8, 128], BF16, kind="ExternalInput").ap()
    id8 = nc.dram_tensor("id8", [8, 8], BF16, kind="ExternalInput").ap()
    id128 = nc.dram_tensor("id128", [128, 128], BF16, kind="ExternalInput").ap()
    Wq1 = nc.dram_tensor("Wq1", [D, D], F32, kind="ExternalInput").ap()
    Wk1 = nc.dram_tensor("Wk1", [D, D], F32, kind="ExternalInput").ap()
    Wv1 = nc.dram_tensor("Wv1", [D, D], F32, kind="ExternalInput").ap()
    Wb1 = nc.dram_tensor("Wb1", [D, 4], F32, kind="ExternalInput").ap()
    cq = nc.dram_tensor("cq", [128, DC * 4], F32, kind="ExternalInput").ap()
    ck = nc.dram_tensor("ck", [128, DC * 4], F32, kind="ExternalInput").ap()
    cv = nc.dram_tensor("cv", [128, DC * 4], F32, kind="ExternalInput").ap()
    lw = nc.dram_tensor("lw", [128, DC * 7], F32, kind="ExternalInput").ap()
    mw = nc.dram_tensor("mw", [128, DC * 25], F32, kind="ExternalInput").ap()
    msl = nc.dram_tensor("msl", [DCC, DCC], F32, kind="ExternalInput").ap()
    msu = nc.dram_tensor("msu", [DCC, DCC], F32, kind="ExternalInput").ap()
    mui = nc.dram_tensor("mui", [DCC, DCC], F32, kind="ExternalInput").ap()
    idin = nc.dram_tensor("ident", [128, 128], BF16, kind="ExternalInput").ap()
    onk = nc.dram_tensor("ones_k", [128, 1], BF16, kind="ExternalInput").ap()
    onb = nc.dram_tensor("ones_b", [1, 128], BF16, kind="ExternalInput").ap()
    x4o = nc.dram_tensor("x4o", [D, TPC], F32, kind="ExternalOutput").ap()
    vdo = nc.dram_tensor("vdo", [D, TPC], BF16, kind="ExternalOutput").ap()
    loco = nc.dram_tensor("loco", [D, TPC], BF16, kind="ExternalOutput").ap()
    mido = nc.dram_tensor("mido", [D, TPC], BF16, kind="ExternalOutput").ap()
    qho = nc.dram_tensor("qho", [D, TPC], BF16, kind="ExternalOutput").ap()
    wfo = nc.dram_tensor("wfo", [D, TPC], BF16, kind="ExternalOutput").ap()
    utm = nc.dram_tensor("utm", [TPC, 1024], BF16, kind="ExternalOutput").ap()
    khtmo = nc.dram_tensor("khtmo", [TPC, 1024], BF16, kind="ExternalOutput").ap()
    atno = nc.dram_tensor("atno", [DCC, 64 * DCC], BF16, kind="ExternalOutput").ap()
    Mo = nc.dram_tensor("Mo", [128, 4 * 512], F32, kind="ExternalOutput").ap()
    Co = nc.dram_tensor("Co", [128, 4 * 512], F32, kind="ExternalOutput").ap()
    betao = nc.dram_tensor("betao", [4, TPC], F32, kind="ExternalOutput").ap()
    o_int = nc.dram_tensor("o_int", [D, EXT1], BF16).ap()
    x2int = nc.dram_tensor("x2int", [D, EXT1], F32).ap()
    x3int = nc.dram_tensor("x3int", [D, EXT2], F32).ap()
    x4int = nc.dram_tensor("x4int", [D, EXT2], F32).ap()
    qfint = nc.dram_tensor("qfint", [D, TPC], BF16).ap()
    kfint = nc.dram_tensor("kfint", [D, TPC], BF16).ap()
    vfint = nc.dram_tensor("vfint", [D, TPC], BF16).ap()

    with ExitStack() as st:
        al = Alloc(nc, st)
        tk = al.sem("tk")
        gk = al.sem("gk")
        block = al.block()
        s = S(nc, block, tk, gk)

        onk_t = al.sb("onk_t", [128, 1], BF16)
        onb_t = al.sb("onb_t", [1, 128], BF16)
        ident = al.sb("ident_t", [128, 128], BF16)

        @s.dma("sync")
        def _(eng):
            return [eng.dma_start(out=onk_t, in_=onk[:, :]),
                    eng.dma_start(out=onb_t, in_=onb[:, :]),
                    eng.dma_start(out=ident, in_=idin[:, :])]

        with ExitStack() as st1:
            al1 = Alloc(nc, st1)
            sin = al1.sb("C_sin", [128, 1024], F32)
            sin_bf = al1.sb("C_sinb", [128, 1024], BF16)
            spv = al1.sb("C_spv", [128, 1024], F32)
            spv_bf = al1.sb("C_spvb", [128, 1024], BF16)
            slt = al1.sb("C_sl", [128, 1024], F32)
            att = al1.sb("C_at", [128, 4 * 8], F32)
            am0_t = al1.sb("C_am0", [128, 3], F32)
            am1_t = al1.sb("C_am1", [128, 3], F32)
            dec = al1.sb("C_dec", [128, 8], F32)
            qtt = al1.sb("C_qt", [128, 8 * 512], BF16)
            olc = al1.sb("C_ol", [128, 8 * 512], BF16)
            tmpq = al1.sb("C_tmpq", [128, 8 * 512], BF16)
            bc_t = al1.sb("C_bc", [128, 8 * NCH], F32)
            bch_t = al1.sb("C_bch", [128, 16], F32)
            obuf = al1.sb("C_obuf", [128, 8 * 512], BF16)
            ps_c = al1.ps("C_ps", [128, 512], F32)
            ps_c2 = al1.ps("C_ps2", [128, 512], F32)
            ps_c3 = al1.ps("C_ps3", [128, 512], F32)
            ps_c4 = al1.ps("C_ps4", [128, 512], F32)
            ps_cl = [ps_c, ps_c2, ps_c3, ps_c4]

            @s.dma("sync")
            def _(eng):
                return [eng.dma_start(out=att, in_=AT[:, :]),
                        eng.dma_start(out=am0_t, in_=am0[:, :]),
                        eng.dma_start(out=am1_t, in_=am1[:, :]),
                        eng.dma_start(out=bc_t, in_=bco[:, :]),
                        eng.dma_start(out=bch_t, in_=bch[:, :])]

            for Sx, Sb, amx in ((sin, sin_bf, am0_t), (spv, spv_bf, am1_t)):
                @s.step("vector")
                def _(eng, Sx=Sx):
                    return eng.memset(Sx, 0.0)

                for p in range(3):
                    @s.dma("sync")
                    def _(eng, p=p):
                        return [eng.dma_start(out=slt, in_=SL[:, p * 1024:(p + 1) * 1024])]

                    @s.step("vector")
                    def _(eng, p=p, amx=amx):
                        return eng.tensor_scalar_mul(dec, att[:, p * 8:(p + 1) * 8], amx[:, p:p + 1])

                    @s.step("scalar")
                    def _(eng):
                        return eng.activation(dec, dec, AF.Exp)

                    @s.step("vector")
                    def _(eng, p=p, Sx=Sx, amx=amx):
                        for h in range(8):
                            eng.tensor_scalar_mul(Sx[:, h * 128:(h + 1) * 128],
                                                  Sx[:, h * 128:(h + 1) * 128], dec[:, h:h + 1])
                            eng.drain()
                        eng.scalar_tensor_tensor(slt, slt, amx[:, p:p + 1], Sx, ALU.mult, ALU.add)
                        eng.drain()
                        return eng.tensor_copy(Sx, slt)

                @s.step("vector")
                def _(eng, Sx=Sx, Sb=Sb):
                    return eng.tensor_copy(Sb, Sx)

            @s.dma("sync")
            def _(eng):
                return [eng.dma_start(out=qtt[:, 0:8 * 128].rearrange("p (c t) -> p c t", c=8),
                                      in_=qth.rearrange("(c p) t -> p c t", p=128)),
                        eng.dma_start(out=olc[:, 0:8 * 128].rearrange("p (c t) -> p c t", c=8),
                                      in_=oh.rearrange("(c p) t -> p c t", p=128))]

            @s.step("vector", nodrain=True)
            def _(eng):
                for h in range(8):
                    for hc in range(2):
                        i = eng.tensor_scalar_mul(
                            tmpq[:, h * 128 + hc * CC: h * 128 + (hc + 1) * CC],
                            qtt[:, h * 128 + hc * CC: h * 128 + (hc + 1) * CC],
                            bch_t[:, h * 2 + hc:h * 2 + hc + 1])
                return i

            @s.step("tensor")
            def _(eng):
                for h in range(8):
                    ps = ps_cl[h // 4]
                    i = eng.matmul(ps[:, (h % 4) * 128:(h % 4 + 1) * 128],
                                   spv_bf[:, h * 128:(h + 1) * 128],
                                   tmpq[:, h * 128:(h + 1) * 128], start=True, stop=True)
                return i

            @s.step("vector", nodrain=True)
            def _(eng):
                for h in range(8):
                    ps = ps_cl[h // 4]
                    i = eng.tensor_tensor(obuf[:, h * 128:(h + 1) * 128],
                                          olc[:, h * 128:(h + 1) * 128],
                                          ps[:, (h % 4) * 128:(h % 4 + 1) * 128], ALU.add)
                return i

            @s.dma("sync")
            def _(eng):
                return [eng.dma_start(out=o_int.rearrange("(c p) t -> p c t", p=128)[:, :, 0:OH],
                                      in_=obuf[:, 0:8 * 128].rearrange("p (c t) -> p c t", c=8))]

            for grp in range(4):
                g0 = grp * 512

                @s.dma("sync")
                def _(eng, g0=g0):
                    return [eng.dma_start(out=qtt.rearrange("p (c t) -> p c t", c=8),
                                          in_=qto.rearrange("(c p) t -> p c t", p=128)[:, :, g0:g0 + 512]),
                            eng.dma_start(out=olc.rearrange("p (c t) -> p c t", c=8),
                                          in_=oloc.rearrange("(c p) t -> p c t", p=128)[:, :, g0:g0 + 512])]

                @s.step("vector")
                def _(eng, grp=grp):
                    for h in range(8):
                        qv = qtt[:, h * 512:(h + 1) * 512].rearrange("p (c t) -> p c t", t=CC)
                        bv = bc_t[:, h * NCH + grp * 8:h * NCH + (grp + 1) * 8].rearrange("p (c o) -> p c o", o=1)
                        amv, bmv = bcast(qv, bv)
                        i = eng.tensor_tensor(tmpq[:, h * 512:(h + 1) * 512].rearrange("p (c t) -> p c t", t=CC),
                                              amv, bmv, ALU.mult)
                        eng.drain()
                    return i

                for hg in range(2):
                    @s.step("tensor")
                    def _(eng, hg=hg):
                        for idx in range(4):
                            h = hg * 4 + idx
                            i = eng.matmul(ps_cl[idx][:, :], sin_bf[:, h * 128:(h + 1) * 128],
                                           tmpq[:, h * 512:(h + 1) * 512], start=True, stop=True)
                        return i

                    @s.step("vector", nodrain=True)
                    def _(eng, hg=hg):
                        for idx in range(4):
                            h = hg * 4 + idx
                            i = eng.tensor_tensor(obuf[:, h * 512:(h + 1) * 512],
                                                  olc[:, h * 512:(h + 1) * 512], ps_cl[idx][:, :], ALU.add)
                        return i

                @s.dma("sync")
                def _(eng, g0=g0):
                    return [eng.dma_start(out=o_int.rearrange("(c p) t -> p c t", p=128)[:, :, OH + g0:OH + g0 + 512],
                                          in_=obuf.rearrange("p (c t) -> p c t", c=8))]

        with ExitStack() as st2:
            al2 = Alloc(nc, st2)
            wo_b = al2.sb("AO_wo", [128, DC * D], BF16)
            wstage = al2.sb("AO_wst", [128, D], F32)
            gw_t = al2.sb("AO_gw", [128, DC], F32)

            @s.dma("sync")
            def _(eng):
                return [eng.dma_start(out=gw_t, in_=gw[:, :])]

            for c in range(DC):
                @s.dma("sync")
                def _(eng, c=c):
                    return [eng.dma_start(out=wstage, in_=Wo[c * 128:(c + 1) * 128, :])]

                @s.step("vector")
                def _(eng, c=c):
                    return eng.tensor_scalar_mul(wo_b[:, c * D:(c + 1) * D], wstage, gw_t[:, c:c + 1])

            emit_attnout(s, nc, al2, "AO_", x1e, o_int, [x2int], EXT1, wo_b, onk_t, onb_t)

        with ExitStack() as st3:
            al3 = Alloc(nc, st3)
            wstage = al3.sb("F2_wst", [128, D], F32)
            wr_b = al3.sb("F2_wr", [128, DC * D], BF16)
            wk_b = al3.sb("F2_wk", [128, DC * D], BF16)
            wv_b = al3.sb("F2_wv", [128, DC * D], BF16)
            mur_t = al3.sb("F2_mur", [128, DC], F32)
            muk_t = al3.sb("F2_muk", [128, DC], F32)

            @s.dma("sync")
            def _(eng):
                return [eng.dma_start(out=mur_t, in_=mur[:, :]),
                        eng.dma_start(out=muk_t, in_=muk[:, :])]

            emit_weight_load(s, wstage, wr_b, Wr, DC, D)
            emit_weight_load(s, wstage, wk_b, Wk, DC, D)
            emit_weight_load(s, wstage, wv_b, Wv, DC, D)
            emit_ffn(s, nc, al3, "F2_", x2int, [x3int], EXT2, 1,
                     wr_b, wk_b, wv_b, mur_t, muk_t, onk_t, onb_t)

        with ExitStack() as st4:
            al4 = Alloc(nc, st4)
            wstage = al4.sb("M1_wst", [128, D], F32)
            wb_b = al4.sb("M1_wb", [128, DC * D], BF16)
            aa_b = al4.sb("M1_aa", [128, DC * 128], BF16)
            ba_b = al4.sb("M1_ba", [128, D], BF16)
            rw_b = al4.sb("M1_rw", [128, DC * 8], BF16)
            bb_t = al4.sb("M1_bb", [128, DC], F32)
            rb_t = al4.sb("M1_rb", [8, 1], F32)
            e2_t = al4.sb("M1_e2", [8, 128], BF16)
            id8_t = al4.sb("M1_id8", [8, 8], BF16)

            @s.dma("sync")
            def _(eng):
                return [eng.dma_start(out=bb_t, in_=bb[:, :]),
                        eng.dma_start(out=rb_t, in_=rb[:, :]),
                        eng.dma_start(out=e2_t, in_=E2[:, :]),
                        eng.dma_start(out=id8_t, in_=id8[:, :])]

            emit_weight_load(s, wstage, wb_b, bwT, DC, D)
            emit_weight_load(s, wstage, aa_b, Aall, DC, 128)
            emit_weight_load(s, wstage, ba_b, Ball, 1, D)
            emit_weight_load(s, wstage, rw_b, rwT, DC, 8)
            emit_moe(s, nc, al4, "M1_", x3int, [x4int], EXT2, wb_b, aa_b, ba_b,
                     rw_b, bb_t, rb_t, e2_t, id8_t, ident, onk_t, onb_t)

            @s.dma("sync")
            def _(eng):
                return [eng.dma_start(out=x4o.rearrange("d t -> d t"),
                                      in_=x4int[:, X4OFF:X4OFF + TPC])]

        with ExitStack() as st5:
            al5 = Alloc(nc, st5)
            cq_t = al5.sb("P_cq", [128, DC * 4], F32)
            ck_t = al5.sb("P_ck", [128, DC * 4], F32)
            cv_t = al5.sb("P_cv", [128, DC * 4], F32)
            lw_t = al5.sb("P_lw", [128, DC * 7], F32)
            mw_t = al5.sb("P_mw", [128, DC * 25], F32)
            hmask_t = al5.sb("P_hm", [128, HLO], F32)

            @s.dma("sync")
            def _(eng):
                return [eng.dma_start(out=cq_t, in_=cq[:, :]), eng.dma_start(out=ck_t, in_=ck[:, :]),
                        eng.dma_start(out=cv_t, in_=cv[:, :]), eng.dma_start(out=lw_t, in_=lw[:, :]),
                        eng.dma_start(out=mw_t, in_=mw[:, :]), eng.dma_start(out=hmask_t, in_=hmask[:, :])]

            emit_l1prep(s, nc, al5, "P_", x4int[:, X4OFF - HLO:EXT2], Wq1, Wk1, Wv1, Wb1,
                        cq_t, ck_t, cv_t, lw_t, mw_t, hmask_t, onk_t, onb_t,
                        qfint, kfint, vfint, betao, loco, mido)

            @s.dma("sync")
            def _(eng):
                return [eng.dma_start(out=vdo.rearrange("d t -> d t"), in_=vfint[:, :])]

        emit_delta_phase1(s, nc, al, qfint, kfint, vfint, betao, msl, msu, mui,
                          ident, onk_t, onb_t, qho, wfo, utm, khtmo, atno, Mo, Co)

        s.fence()
        nc._steps = (s.n_steps, s.n_dmas, s.target, s.gtarget)
    return nc

def emit_delta_phase1(s, nc, al, qfint, kfint, vfint, betao, msl, msu, mui,
                      ident, onk_t, onb_t, qho, wfo, utm, khtmo, atno, Mo, Co):
    nt = TPC
    with ExitStack() as st6:
        al6 = Alloc(nc, st6)
        qf_t = al6.sb("D_qf", [128, 2 * nt], BF16)
        kf_t = al6.sb("D_kf", [128, 2 * nt], BF16)
        vf_t = al6.sb("D_vf", [128, 2 * nt], BF16)
        qh_t = al6.sb("D_qh", [128, 2 * nt], BF16)
        kh_t = al6.sb("D_kh", [128, 2 * nt], BF16)
        kb_t = al6.sb("D_kb", [128, 2 * nt], BF16)
        vb_t = al6.sb("D_vb", [128, 2 * nt], BF16)
        sq_t = al6.sb("D_sq", [128, 2 * nt], BF16)
        rq_t = al6.sb("D_rq", [1, 2 * nt], F32)
        rqb_t = al6.sb("D_rqb", [1, 2 * nt], BF16)
        btf_t = al6.sb("D_btf", [1, nt], F32)
        bt_t = al6.sb("D_bt", [1, nt], BF16)
        s_f32 = al6.sb("D_sf", [128, 512], F32)
        s_bf = al6.sb("D_sb", [128, 512], BF16)
        p_f32 = al6.sb("D_pf", [128, 512], F32)
        p_bf = al6.sb("D_pb", [128, 512], BF16)
        pA = al6.sb("D_pA", [128, 7 * DCC], F32)
        pG = al6.sb("D_pG", [128, 7 * DCC], F32)
        atn_t = al6.sb("D_atn", [128, DNCH * DCC], BF16)
        uall_t = al6.sb("D_uall", [128, DNCH * 256], BF16)
        kall_t = al6.sb("D_kall", [128, DNCH * 256], BF16)
        wall_t = al6.sb("D_wall", [128, DNCH * 2 * DCC], BF16)
        z0 = al6.sb("D_z0", [128, 512], F32)
        z1 = al6.sb("D_z1", [128, 512], F32)
        
        khc_t = al6.sb("D_khc", [64, 256], BF16)
        ut_t = al6.sb("D_ut", [128, 256], BF16)
        ub_t = al6.sb("D_ub", [64, 256], BF16)
        xb_t = al6.sb("D_xb", [128, 256], BF16)
        wf_t = al6.sb("D_wf", [128, 2 * DCC], F32)
        wfb_t = al6.sb("D_wfb", [128, 2 * CC], BF16)
        msl_t = al6.sb("D_msl", [DCC, DCC], F32)
        msu_t = al6.sb("D_msu", [DCC, DCC], F32)
        mui_t = al6.sb("D_mui", [DCC, DCC], F32)
        psA = al6.ps("D_psA", [128, 512], F32)
        psB = al6.ps("D_psB", [128, 2 * DCC], F32)
        psC = al6.ps("D_psC", [128, DCC], F32)
        psD = al6.ps("D_psD", [128, 512], F32)
        psE = al6.ps("D_psE", [128, 2 * DCC], F32)
        psF = al6.ps("D_psF", [128, 256], F32)
        psX = al6.ps("D_psX", [128, 256], F32)
        psTR = al6.ps("D_psTR", [128, 768], BF16)

        @s.dma("sync")
        def _(eng):
            return [eng.dma_start(out=msl_t, in_=msl[:, :]), eng.dma_start(out=msu_t, in_=msu[:, :]),
                    eng.dma_start(out=mui_t, in_=mui[:, :])]

        qf3 = qf_t.rearrange("p (c t) -> p c t", c=2)
        kf3 = kf_t.rearrange("p (c t) -> p c t", c=2)
        vf3 = vf_t.rearrange("p (c t) -> p c t", c=2)
        qh3 = qh_t.rearrange("p (c t) -> p c t", c=2)
        kh3 = kh_t.rearrange("p (c t) -> p c t", c=2)
        kb3 = kb_t.rearrange("p (c t) -> p c t", c=2)
        vb3 = vb_t.rearrange("p (c t) -> p c t", c=2)
        sq3 = sq_t.rearrange("p (c t) -> p c t", c=2)

        def l2norm_fm(src3, src_flat, dst3, roff):
            @s.step("scalar")
            def _(eng):
                return eng.activation(sq_t, src_flat, AF.Square)

            for u in range(nt // 512):
                @s.step("tensor")
                def _(eng, u=u):
                    for c in range(2):
                        i = eng.matmul(psA[0:1, 0:512], onk_t, sq3[:, c, u * 512:(u + 1) * 512],
                                       start=(c == 0), stop=(c == 1))
                    return i

                @s.step("vector")
                def _(eng, u=u):
                    return eng.tensor_single_scalar(rq_t[0:1, roff + u * 512:roff + (u + 1) * 512],
                                                    psA[0:1, 0:512], 1e-12, ALU.add)

            @s.step("scalar")
            def _(eng):
                return eng.activation(rq_t[0:1, roff:roff + nt], rq_t[0:1, roff:roff + nt], AF.Sqrt)

            @s.step("vector")
            def _(eng):
                eng.reciprocal(rq_t[0:1, roff:roff + nt], rq_t[0:1, roff:roff + nt])
                eng.drain()
                return eng.tensor_copy(rqb_t[0:1, roff:roff + nt], rq_t[0:1, roff:roff + nt])

            for u in range(nt // 512):
                @s.step("tensor")
                def _(eng, u=u):
                    return eng.matmul(psA[:, 0:512], onb_t,
                                      rqb_t[0:1, roff + u * 512:roff + (u + 1) * 512],
                                      start=True, stop=True)

                @s.step("vector")
                def _(eng, u=u):
                    am, bm = bcast(src3[:, :, u * 512:(u + 1) * 512],
                                   psA[:, 0:512].rearrange("p (o t) -> p o t", o=1))
                    return eng.tensor_tensor(dst3[:, :, u * 512:(u + 1) * 512], am, bm, ALU.mult)

        for h in range(4):
            @s.dma("sync")
            def _(eng, h=h):
                return [
                    eng.dma_start(out=qf3, in_=qfint.rearrange("(c p) t -> p c t", p=128)[:, 2 * h:2 * h + 2, :]),
                    eng.dma_start(out=kf3, in_=kfint.rearrange("(c p) t -> p c t", p=128)[:, 2 * h:2 * h + 2, :]),
                    eng.dma_start(out=vf3, in_=vfint.rearrange("(c p) t -> p c t", p=128)[:, 2 * h:2 * h + 2, :]),
                    eng.dma_start(out=btf_t, in_=betao[h:h + 1, :]),
                ]

            @s.step("vector")
            def _(eng):
                return eng.tensor_copy(bt_t, btf_t)

            l2norm_fm(qf3, qf_t, qh3, 0)
            l2norm_fm(kf3, kf_t, kh3, nt)

            @s.dma("sync")
            def _(eng, h=h):
                return [eng.dma_start(out=qho[h * 256:h * 256 + 128, :], in_=qh_t[:, 0:nt]),
                        eng.dma_start(out=qho[h * 256 + 128:h * 256 + 256, :], in_=qh_t[:, nt:2 * nt])]

            for u in range(nt // 512):
                @s.step("tensor")
                def _(eng, u=u):
                    return eng.matmul(psA[:, 0:512], onb_t, bt_t[0:1, u * 512:(u + 1) * 512],
                                      start=True, stop=True)

                @s.step("vector")
                def _(eng, u=u):
                    am, bm = bcast(kh3[:, :, u * 512:(u + 1) * 512],
                                   psA[:, 0:512].rearrange("p (o t) -> p o t", o=1))
                    eng.tensor_tensor(kb3[:, :, u * 512:(u + 1) * 512], am, bm, ALU.mult)
                    eng.drain()
                    am2, bm2 = bcast(vf3[:, :, u * 512:(u + 1) * 512],
                                     psA[:, 0:512].rearrange("p (o t) -> p o t", o=1))
                    return eng.tensor_tensor(vb3[:, :, u * 512:(u + 1) * 512], am2, bm2, ALU.mult)

            @s.step("vector")
            def _(eng):
                eng.memset(s_f32, 0.0)
                eng.memset(s_bf, 0.0)
                eng.memset(p_f32, 0.0)
                return eng.memset(p_bf, 0.0)

            @s.step("vector")
            def _(eng):
                i = eng.tensor_tensor(p_f32[:, 0:128], ident[:, 0:128], ident[:, 0:128], ALU.mult)
                eng.drain()
                i = eng.tensor_tensor(p_f32[:, 256 + 128:512], ident[:, 0:128], ident[:, 0:128], ALU.mult)
                eng.drain()
                return eng.tensor_copy(p_bf, p_f32)

            chain_done = None
            for cc_ in range(DNCH):
                c0 = cc_ * DCC

                @s.step("tensor", wait_abs=chain_done)
                def _(eng, c0=c0):
                    for gi, src3 in enumerate((kh3, vb3, kb3)):
                        for c in range(2):
                            i = eng.transpose(psTR[:, gi * 256 + c * 128:gi * 256 + (c + 1) * 128],
                                              src3[:, c, c0:c0 + DCC], ident)
                    return i

                khc_t = kall_t[:, cc_ * 256:(cc_ + 1) * 256]
                ub_t = uall_t[:, cc_ * 256:(cc_ + 1) * 256]
                wfb_t = wall_t[:, cc_ * 2 * DCC:(cc_ + 1) * 2 * DCC]
                atnc_t = atn_t[:, cc_ * DCC:(cc_ + 1) * DCC]

                @s.step("vector", nodrain=True)
                def _(eng, c0=c0, khc_t=khc_t):
                    eng.tensor_copy(khc_t, psTR[:, 0:256])
                    eng.tensor_copy(z0[:, 0:256], psTR[:, 256:512])
                    eng.tensor_copy(z0[:, 256:512], psTR[:, 512:768])
                    eng.tensor_copy(wf_t[:, 0:DCC], kb3[:, 0, c0:c0 + DCC])
                    return eng.tensor_copy(wf_t[:, DCC:2 * DCC], kb3[:, 1, c0:c0 + DCC])

                @s.step("tensor")
                def _(eng, c0=c0):
                    for c in range(2):
                        eng.matmul(psB[:, 0:DCC], kb3[:, c, c0:c0 + DCC], kh3[:, c, c0:c0 + DCC],
                                   start=(c == 0), stop=(c == 1))
                    for c in range(2):
                        eng.matmul(psB[:, DCC:2 * DCC], kh3[:, c, c0:c0 + DCC], kb3[:, c, c0:c0 + DCC],
                                   start=(c == 0), stop=(c == 1))
                    for c in range(2):
                        i = eng.matmul(psC[:, :], kh3[:, c, c0:c0 + DCC], qh3[:, c, c0:c0 + DCC],
                                       start=(c == 0), stop=(c == 1))
                    return i

                @s.step("vector")
                def _(eng):
                    eng.tensor_tensor(pA[:, 0:DCC], psB[:, 0:DCC], msl_t, ALU.mult)
                    eng.drain()
                    eng.tensor_tensor(pG[:, 0:DCC], psB[:, DCC:2 * DCC], msu_t, ALU.mult)
                    eng.drain()
                    return eng.tensor_tensor(atnc_t, psC[:, :], mui_t, ALU.mult)

                zs = [z0, z1]
                for l in range(7):
                    zc, zn = zs[l % 2], zs[(l + 1) % 2]

                    @s.step("tensor")
                    def _(eng, l=l, zc=zc):
                        eng.matmul(psE[:, 0:DCC], zc[:, 256:384], pG[:, l * DCC:(l + 1) * DCC],
                                   start=True, stop=True)
                        eng.matmul(psE[:, DCC:2 * DCC], zc[:, 384:512], pG[:, l * DCC:(l + 1) * DCC],
                                   start=True, stop=True)
                        i = eng.matmul(psA[:, :], pG[:, l * DCC:(l + 1) * DCC], zc[:, :],
                                       start=True, stop=True)
                        if l < 6:
                            eng.matmul(psD[:, 0:DCC], pG[:, l * DCC:(l + 1) * DCC],
                                       pA[:, l * DCC:(l + 1) * DCC], start=True, stop=True)
                            i = eng.matmul(psD[:, DCC:2 * DCC], pA[:, l * DCC:(l + 1) * DCC],
                                           pG[:, l * DCC:(l + 1) * DCC], start=True, stop=True)
                        return i

                    @s.step("vector", nodrain=True)
                    def _(eng, l=l, zc=zc, zn=zn):
                        eng.tensor_tensor(wf_t, wf_t, psE[:, :], ALU.add)
                        i = eng.tensor_tensor(zn, zc, psA[:, :], ALU.add)
                        if l < 6:
                            eng.tensor_copy(pA[:, (l + 1) * DCC:(l + 2) * DCC], psD[:, 0:DCC])
                            i = eng.tensor_copy(pG[:, (l + 1) * DCC:(l + 2) * DCC], psD[:, DCC:2 * DCC])
                        return i

                zfin = zs[1]

                @s.step("vector", nodrain=True)
                def _(eng, zfin=zfin, wfb_t=wfb_t, ub_t=ub_t):
                    eng.tensor_copy(wfb_t, wf_t)
                    return eng.tensor_copy(ub_t, zfin[:, 0:256])

                chain_done = s.target

                @s.step("tensor")
                def _(eng):
                    for c in range(2):
                        eng.matmul(psF[:, :], wfb_t[:, c * DCC:(c + 1) * DCC],
                                   s_bf[:, c * 256:(c + 1) * 256], start=(c == 0), stop=(c == 1))
                    for c in range(2):
                        i = eng.matmul(psX[:, :], wfb_t[:, c * DCC:(c + 1) * DCC],
                                       p_bf[:, c * 256:(c + 1) * 256], start=(c == 0), stop=(c == 1))
                    return i

                @s.step("vector", nodrain=True)
                def _(eng):
                    eng.tensor_tensor(ut_t, ub_t, psF[:, :], ALU.subtract)
                    return eng.tensor_copy(xb_t, psX[:, :])

                @s.step("tensor")
                def _(eng):
                    eng.matmul(psA[:, 0:256], khc_t[:, 0:128], ut_t, start=True, stop=True)
                    eng.matmul(psA[:, 256:512], khc_t[:, 128:256], ut_t, start=True, stop=True)
                    eng.matmul(psD[:, 0:256], khc_t[:, 0:128], xb_t, start=True, stop=True)
                    return eng.matmul(psD[:, 256:512], khc_t[:, 128:256], xb_t, start=True, stop=True)

                @s.step("vector", nodrain=True)
                def _(eng):
                    eng.tensor_tensor(s_f32, s_f32, psA[:, :], ALU.add)
                    eng.tensor_tensor(p_f32, p_f32, psD[:, :], ALU.subtract)
                    eng.drain()
                    eng.tensor_copy(s_bf, s_f32)
                    return eng.tensor_copy(p_bf, p_f32)

            @s.dma("sync")
            def _(eng, h=h):
                return [
                    eng.dma_start(out=atno[:, h * DNCH * DCC:(h + 1) * DNCH * DCC], in_=atn_t),
                    eng.dma_start(out=utm.rearrange("(c p) f -> p c f", p=DCC)[:, :, h * 256:(h + 1) * 256],
                                  in_=uall_t.rearrange("p (c f) -> p c f", f=256)),
                    eng.dma_start(out=khtmo.rearrange("(c p) f -> p c f", p=DCC)[:, :, h * 256:(h + 1) * 256],
                                  in_=kall_t.rearrange("p (c f) -> p c f", f=256)),
                    eng.dma_start(out=wfo[h * 256:h * 256 + 128, :].rearrange("p (c t) -> p c t", t=DCC),
                                  in_=wall_t.rearrange("p (c two t) -> p (c two) t", two=2, t=DCC)[:, 0::2, :]),
                    eng.dma_start(out=wfo[h * 256 + 128:h * 256 + 256, :].rearrange("p (c t) -> p c t", t=DCC),
                                  in_=wall_t.rearrange("p (c two t) -> p (c two) t", two=2, t=DCC)[:, 1::2, :]),
                    eng.dma_start(out=Mo[:, h * 512:(h + 1) * 512], in_=p_f32),
                    eng.dma_start(out=Co[:, h * 512:(h + 1) * 512], in_=s_f32),
                ]

R1K = 1080
R1N = 2176
NCH1 = 17


def build_L3():
    nc = bass.Bass("TRN2", target_bir_lowering=False, debug=False)
    x4i = nc.dram_tensor("x4i", [D, TPC], F32, kind="ExternalInput").ap()
    loci = nc.dram_tensor("loci", [D, TPC], BF16, kind="ExternalInput").ap()
    midi = nc.dram_tensor("midi", [D, TPC], BF16, kind="ExternalInput").ap()
    vdi = nc.dram_tensor("vdi", [D, TPC], BF16, kind="ExternalInput").ap()
    qhi = nc.dram_tensor("qhi", [D, TPC], BF16, kind="ExternalInput").ap()
    wfi = nc.dram_tensor("wfi", [D, TPC], BF16, kind="ExternalInput").ap()
    utmi = nc.dram_tensor("utmi", [TPC, 1024], BF16, kind="ExternalInput").ap()
    khtmi = nc.dram_tensor("khtmi", [TPC, 1024], BF16, kind="ExternalInput").ap()
    atni = nc.dram_tensor("atni", [DCC, 64 * DCC], BF16, kind="ExternalInput").ap()
    Mcat = nc.dram_tensor("Mcat", [128, 4 * 2048], F32, kind="ExternalInput").ap()
    Ccat = nc.dram_tensor("Ccat", [128, 4 * 2048], F32, kind="ExternalInput").ap()
    am0 = nc.dram_tensor("am0", [128, 3], F32, kind="ExternalInput").ap()
    Ifm = nc.dram_tensor("Ifm", [128, 512], F32, kind="ExternalInput").ap()
    rw1 = nc.dram_tensor("rw1", [R1K, R1N], F32, kind="ExternalInput").ap()
    rb1 = nc.dram_tensor("rb1", [128, NCH1], F32, kind="ExternalInput").ap()
    rw2 = nc.dram_tensor("rw2", [R1N, 16], F32, kind="ExternalInput").ap()
    rb2 = nc.dram_tensor("rb2", [16, 1], F32, kind="ExternalInput").ap()
    mixw = nc.dram_tensor("mixw", [128, DC], F32, kind="ExternalInput").ap()
    onw = nc.dram_tensor("onw", [128, DC], F32, kind="ExternalInput").ap()
    Wo1 = nc.dram_tensor("Wo1", [D, D], F32, kind="ExternalInput").ap()
    id16 = nc.dram_tensor("id16", [16, 16], BF16, kind="ExternalInput").ap()
    hsel = nc.dram_tensor("hsel", [128, DC * 4], BF16, kind="ExternalInput").ap()
    sel4 = nc.dram_tensor("sel4", [4, 4 * 128], BF16, kind="ExternalInput").ap()
    esel = nc.dram_tensor("esel", [16, 16 * 128], BF16, kind="ExternalInput").ap()
    bwT = nc.dram_tensor("bwT", [D, D], F32, kind="ExternalInput").ap()
    bb = nc.dram_tensor("bb", [128, DC], F32, kind="ExternalInput").ap()
    Aall = nc.dram_tensor("Aall", [D, 128], F32, kind="ExternalInput").ap()
    Ball = nc.dram_tensor("Ball", [128, D], F32, kind="ExternalInput").ap()
    rwT = nc.dram_tensor("rwT", [D, 8], F32, kind="ExternalInput").ap()
    rb = nc.dram_tensor("rb", [8, 1], F32, kind="ExternalInput").ap()
    E2 = nc.dram_tensor("E2", [8, 128], BF16, kind="ExternalInput").ap()
    id8 = nc.dram_tensor("id8", [8, 8], BF16, kind="ExternalInput").ap()
    id128 = nc.dram_tensor("id128", [128, 128], BF16, kind="ExternalInput").ap()
    onk = nc.dram_tensor("ones_k", [128, 1], BF16, kind="ExternalInput").ap()
    onb = nc.dram_tensor("ones_b", [1, 128], BF16, kind="ExternalInput").ap()
    x6o = nc.dram_tensor("x6o", [D, TPC], F32, kind="ExternalOutput").ap()
    dlt = nc.dram_tensor("dlt", [D, TPC], BF16).ap()
    x5int = nc.dram_tensor("x5int", [D, TPC], F32).ap()
    fscratch = nc.dram_tensor("fscratch", [8, 512], F32).ap()

    with ExitStack() as st:
        al = Alloc(nc, st)
        tk = al.sem("tk")
        gk = al.sem("gk")
        block = al.block()
        s = S(nc, block, tk, gk)

        onk_t = al.sb("onk_t", [128, 1], BF16)
        onb_t = al.sb("onb_t", [1, 128], BF16)
        id128_t = al.sb("id128_t", [128, 128], BF16)

        @s.dma("sync")
        def _(eng):
            return [eng.dma_start(out=onk_t, in_=onk[:, :]),
                    eng.dma_start(out=onb_t, in_=onb[:, :]),
                    eng.dma_start(out=id128_t, in_=id128[:, :])]

        with ExitStack() as st1:
            al1 = Alloc(nc, st1)
            sin4 = al1.sb("D2_sin", [128, 4 * 512], F32)
            sin4b = al1.sb("D2_sinb", [128, 4 * 512], BF16)
            mp_t = al1.sb("D2_mp", [128, 2048], F32)
            mpb_t = al1.sb("D2_mpb", [128, 2048], BF16)
            mpc_t = al1.sb("D2_mpc", [128, 2048], BF16)
            cp_t = al1.sb("D2_cp", [128, 2048], F32)
            am_t = al1.sb("D2_am", [128, 3], F32)
            ifm_t = al1.sb("D2_ifm", [128, 512], F32)
            s_f32 = al1.sb("D2_sf", [128, 512], F32)
            s_bf = al1.sb("D2_sb", [128, 512], BF16)
            qh_t = al1.sb("D2_qh", [128, 2 * 512], BF16)
            wf_t = al1.sb("D2_wf", [128, 2 * 512], BF16)
            u_t = al1.sb("D2_u", [128, 4 * 256], BF16)
            kh_t = al1.sb("D2_kh", [128, 4 * 256], BF16)
            atn_t = al1.sb("D2_atn", [128, 4 * DCC], BF16)
            ut_t = al1.sb("D2_ut", [128, 256], BF16)
            ob_t = al1.sb("D2_ob", [128, 256], BF16)
            dlt_t = al1.sb("D2_dlt", [128, 2 * 512], BF16)
            psF = al1.ps("D2_psF", [128, 256], F32)
            psO = al1.ps("D2_psO", [128, 256], F32)
            psS = al1.ps("D2_psS", [128, 512], F32)
            psTR = al1.ps("D2_psTR", [128, 256], BF16)

            @s.dma("sync")
            def _(eng):
                return [eng.dma_start(out=am_t, in_=am0[:, :]),
                        eng.dma_start(out=ifm_t, in_=Ifm[:, :])]

            @s.step("vector")
            def _(eng):
                return eng.memset(sin4, 0.0)

            for p in range(3):
                @s.dma("sync")
                def _(eng, p=p):
                    return [eng.dma_start(out=mp_t, in_=Mcat[:, p * 2048:(p + 1) * 2048]),
                            eng.dma_start(out=cp_t, in_=Ccat[:, p * 2048:(p + 1) * 2048])]

                @s.step("vector")
                def _(eng, p=p):
                    for h in range(4):
                        sl = slice(h * 512, (h + 1) * 512)
                        eng.tensor_tensor(mp_t[:, sl], mp_t[:, sl], ifm_t, ALU.subtract)
                        eng.drain()
                        eng.tensor_scalar_mul(mp_t[:, sl], mp_t[:, sl], am_t[:, p:p + 1])
                        eng.drain()
                        eng.tensor_tensor(mp_t[:, sl], mp_t[:, sl], ifm_t, ALU.add)
                        eng.drain()
                    return eng.tensor_copy(mpc_t, mp_t)

                for h in range(4):
                    for bi in range(2):
                        @s.step("tensor")
                        def _(eng, h=h, bi=bi):
                            for bj in range(2):
                                i = eng.transpose(
                                    psTR[:, bj * 128:(bj + 1) * 128],
                                    mpc_t[:, h * 512 + bi * 256 + bj * 128: h * 512 + bi * 256 + (bj + 1) * 128],
                                    id128_t)
                            return i

                        @s.step("vector", nodrain=True)
                        def _(eng, h=h, bi=bi):
                            for bj in range(2):
                                i = eng.tensor_copy(
                                    mpb_t[:, h * 512 + bj * 256 + bi * 128: h * 512 + bj * 256 + (bi + 1) * 128],
                                    psTR[:, bj * 128:(bj + 1) * 128])
                            return i

                @s.step("vector")
                def _(eng):
                    return eng.tensor_copy(sin4b, sin4)

                for h in range(4):
                    @s.step("tensor")
                    def _(eng, h=h):
                        for r in range(2):
                            for kb_ in range(2):
                                i = eng.matmul(psS[:, r * 256:(r + 1) * 256],
                                               mpb_t[:, h * 512 + kb_ * 256 + r * 128: h * 512 + kb_ * 256 + (r + 1) * 128],
                                               sin4b[:, h * 512 + kb_ * 256:h * 512 + (kb_ + 1) * 256],
                                               start=(kb_ == 0), stop=(kb_ == 1))
                        return i

                    @s.step("vector")
                    def _(eng, h=h, p=p):
                        sl = slice(h * 512, (h + 1) * 512)
                        eng.scalar_tensor_tensor(sin4[:, sl], cp_t[:, sl], am_t[:, p:p + 1],
                                                 psS[:, :], ALU.mult, ALU.add)
                        eng.drain()
                        return i if False else eng.tensor_copy(sin4b[:, sl], sin4[:, sl])

            for h in range(4):
                @s.step("vector")
                def _(eng, h=h):
                    eng.tensor_copy(s_f32, sin4[:, h * 512:(h + 1) * 512])
                    eng.drain()
                    return eng.tensor_copy(s_bf, s_f32)

                for j in range(4):
                    t0 = j * 512

                    @s.dma("sync")
                    def _(eng, h=h, j=j, t0=t0):
                        return [
                            eng.dma_start(out=qh_t.rearrange("p (c t) -> p c t", c=2),
                                          in_=qhi.rearrange("(c p) t -> p c t", p=128)[:, 2 * h:2 * h + 2, t0:t0 + 512]),
                            eng.dma_start(out=wf_t.rearrange("p (c t) -> p c t", c=2),
                                          in_=wfi.rearrange("(c p) t -> p c t", p=128)[:, 2 * h:2 * h + 2, t0:t0 + 512]),
                            eng.dma_start(out=u_t.rearrange("p (b f) -> p b f", f=256),
                                          in_=utmi.rearrange("(b p) f -> p b f", p=DCC)[:, j * 4:(j + 1) * 4, h * 256:(h + 1) * 256]),
                            eng.dma_start(out=kh_t.rearrange("p (b f) -> p b f", f=256),
                                          in_=khtmi.rearrange("(b p) f -> p b f", p=DCC)[:, j * 4:(j + 1) * 4, h * 256:(h + 1) * 256]),
                            eng.dma_start(out=atn_t,
                                          in_=atni[:, (h * DNCH + j * 4) * DCC:(h * DNCH + (j + 1) * 4) * DCC]),
                        ]

                    for c in range(4):
                        c0 = c * DCC

                        @s.step("tensor")
                        def _(eng, c0=c0):
                            for kb_ in range(2):
                                i = eng.matmul(psF[:, :], wf_t[:, kb_ * 512 + c0:kb_ * 512 + c0 + DCC],
                                               s_bf[:, kb_ * 256:(kb_ + 1) * 256],
                                               start=(kb_ == 0), stop=(kb_ == 1))
                            return i

                        @s.step("vector")
                        def _(eng, c=c):
                            return eng.tensor_tensor(
                                ut_t, u_t.rearrange("p (b f) -> p b f", f=256)[:, c, :],
                                psF[:, :], ALU.subtract)

                        @s.step("tensor")
                        def _(eng, c0=c0, c=c):
                            for kb_ in range(2):
                                eng.matmul(psO[:, :], qh_t[:, kb_ * 512 + c0:kb_ * 512 + c0 + DCC],
                                           s_bf[:, kb_ * 256:(kb_ + 1) * 256],
                                           start=(kb_ == 0), stop=False)
                            eng.matmul(psO[:, :], atn_t[:, c * DCC:(c + 1) * DCC], ut_t,
                                       start=False, stop=True)
                            kh3 = kh_t.rearrange("p (b f) -> p b f", f=256)
                            eng.matmul(psS[:, 0:256], kh3[:, c, 0:128], ut_t, start=True, stop=True)
                            return eng.matmul(psS[:, 256:512], kh3[:, c, 128:256], ut_t, start=True, stop=True)

                        @s.step("scalar")
                        def _(eng):
                            return eng.copy(ob_t, psO[:, :])

                        @s.step("tensor")
                        def _(eng):
                            eng.transpose(psTR[:, 0:128], ob_t[:, 0:128], id128_t)
                            return eng.transpose(psTR[:, 128:256], ob_t[:, 128:256], id128_t)

                        @s.step("vector", nodrain=True)
                        def _(eng, c0=c0):
                            eng.tensor_copy(dlt_t[:, c0:c0 + DCC], psTR[:, 0:128])
                            eng.tensor_copy(dlt_t[:, 512 + c0:512 + c0 + DCC], psTR[:, 128:256])
                            eng.tensor_tensor(s_f32, s_f32, psS[:, :], ALU.add)
                            eng.drain()
                            return eng.tensor_copy(s_bf, s_f32)

                    @s.dma("sync")
                    def _(eng, h=h, t0=t0):
                        return [eng.dma_start(out=dlt[h * 256:h * 256 + 128, t0:t0 + 512], in_=dlt_t[:, 0:512]),
                                eng.dma_start(out=dlt[h * 256 + 128:h * 256 + 256, t0:t0 + 512], in_=dlt_t[:, 512:1024])]

        with ExitStack() as st2:
            al2 = Alloc(nc, st2)
            emit_l1out(s, nc, al2, x4i, loci, midi, dlt, vdi, rw1, rb1, rw2, rb2,
                       mixw, onw, Wo1, id16, id128_t, esel, hsel, sel4, onk_t, onb_t,
                       fscratch, [x5int])

        with ExitStack() as st3:
            al3 = Alloc(nc, st3)
            wstage = al3.sb("M2_wst", [128, D], F32)
            wb_b = al3.sb("M2_wb", [128, DC * D], BF16)
            aa_b = al3.sb("M2_aa", [128, DC * 128], BF16)
            ba_b = al3.sb("M2_ba", [128, D], BF16)
            rw_b = al3.sb("M2_rw", [128, DC * 8], BF16)
            bb_t = al3.sb("M2_bb", [128, DC], F32)
            rb_t = al3.sb("M2_rb", [8, 1], F32)
            e2_t = al3.sb("M2_e2", [8, 128], BF16)
            id8_t = al3.sb("M2_id8", [8, 8], BF16)

            @s.dma("sync")
            def _(eng):
                return [eng.dma_start(out=bb_t, in_=bb[:, :]),
                        eng.dma_start(out=rb_t, in_=rb[:, :]),
                        eng.dma_start(out=e2_t, in_=E2[:, :]),
                        eng.dma_start(out=id8_t, in_=id8[:, :])]

            emit_weight_load(s, wstage, wb_b, bwT, DC, D)
            emit_weight_load(s, wstage, aa_b, Aall, DC, 128)
            emit_weight_load(s, wstage, ba_b, Ball, 1, D)
            emit_weight_load(s, wstage, rw_b, rwT, DC, 8)
            emit_moe(s, nc, al3, "M2_", x5int, [x6o], TPC, wb_b, aa_b, ba_b,
                     rw_b, bb_t, rb_t, e2_t, id8_t, id128_t, onk_t, onb_t)

        s.fence()
        nc._steps = (s.n_steps, s.n_dmas, s.target, s.gtarget)
    return nc

def emit_l1out(s, nc, al, xs, loc, mid, dlt, vd, rw1, rb1, rw2, rb2, mixw, onw,
               Wo, id16, id128_t, esel, hsel, sel4, onk_t, onb_t, fscratch, xo_hbms):
    ntok, nt, nsub = TPC, 512, TPC // 512
    xs_t = al.sb("O_xs", [128, DC * nt], F32)
    ln_t = al.sb("O_ln", [128, DC * nt], BF16)
    sq_t = al.sb("O_sq", [128, DC * nt], BF16)
    lntmp_t = al.sb("O_lntmp", [128, DC * nt], F32)
    br_t = [al.sb(f"O_br{b}", [128, DC * nt], BF16) for b in range(4)]
    h1_t = al.sb("O_h1", [128, NCH1 * nt], BF16)
    stageA_t = al.sb("O_stageA", [4, nt], F32)
    stageB_t = al.sb("O_stageB", [4, nt], F32)
    hsel_t = al.sb("O_hsel", [128, DC * 4], BF16)
    sel4_t = al.sb("O_sel4", [4, 4 * 128], BF16)
    hw2f_t = al.sb("O_hw2f", [128, DC * 4], F32)
    hw2_t = al.sb("O_hw2", [128, DC * 4], BF16)
    dn_t = al.sb("O_dn", [8, nt], F32)
    dnb_t = al.sb("O_dnb", [4, nt], BF16)
    feats_t = al.sb("O_feats", [64, nt], BF16)
    lgm_t = al.sb("O_lgm", [16, nt], BF16)
    lgtm_t = al.sb("O_lgtm", [128, 4 * 16], F32)
    ptok_t = al.sb("O_ptok", [128, 4 * 16], BF16)
    pfm_t = al.sb("O_pfm", [16, nt], BF16)
    stat_t = al.sb("O_stat", [128, 32], F32)
    on_t = al.sb("O_on", [128, DC * nt], BF16)
    hout = al.sb("O_hout", [128, 2 * nt], F32)
    srow = al.sb("O_srow", [1, 4 * nt], F32)
    srow_bf = al.sb("O_srowb", [1, 2 * nt], BF16)
    nrow2 = al.sb("O_nrow2", [1, 2 * nt], F32)
    nrow_bf = al.sb("O_nrowb", [1, nt], BF16)
    w1_b = al.sb("O_w1", [128, DC * R1N], BF16)
    w56_b = al.sb("O_w56", [64, R1N], BF16)
    w2_b = al.sb("O_w2", [128, NCH1 * 16], BF16)
    wo_b = al.sb("O_wo", [128, DC * D], BF16)
    wstage = al.sb("O_wst", [128, R1N // 2], F32)
    rb1_t = al.sb("O_rb1", [128, NCH1], F32)
    rb2_t = al.sb("O_rb2", [16, 1], F32)
    mixw_t = al.sb("O_mixw", [128, DC], F32)
    onw_t = al.sb("O_onw", [128, DC], F32)
    id16_t = al.sb("O_id16", [16, 16], BF16)
    esel_t = al.sb("O_esel", [16, 16 * 128], BF16)
    ps_a = al.ps("O_psa", [128, nt], F32)
    ps_b = al.ps("O_psb", [128, nt], F32)
    ps_mm = al.ps("O_psmm", [128, nt], F32)
    ps_row = al.ps("O_psrow", [8, nt], F32)
    ps_rowB = al.ps("O_psrowB", [8, nt], F32)
    ps_lg = al.ps("O_pslg", [16, nt], F32)
    ps_t1 = al.ps("O_pst1", [128, 64], BF16)
    ps_t2 = al.ps("O_pst2", [16, 512], BF16)
    ps_stA = ps_a[0:1, :]
    ps_stB = ps_b[0:1, :]

    @s.dma("sync")
    def _(eng):
        return [eng.dma_start(out=rb1_t, in_=rb1[:, :]), eng.dma_start(out=rb2_t, in_=rb2[:, :]),
                eng.dma_start(out=mixw_t, in_=mixw[:, :]), eng.dma_start(out=onw_t, in_=onw[:, :]),
                eng.dma_start(out=id16_t, in_=id16[:, :]),
                eng.dma_start(out=hsel_t, in_=hsel[:, :]),
                eng.dma_start(out=sel4_t, in_=sel4[:, :]),
                eng.dma_start(out=esel_t, in_=esel[:, :])]

    @s.step("vector")
    def _(eng):
        for c in range(DC):
            eng.tensor_scalar_mul(hw2f_t[:, c * 4:(c + 1) * 4], hsel_t[:, c * 4:(c + 1) * 4],
                                  mixw_t[:, c:c + 1])
        eng.drain()
        for c in range(DC):
            eng.tensor_scalar_mul(hw2f_t[:, c * 4:(c + 1) * 4], hw2f_t[:, c * 4:(c + 1) * 4],
                                  mixw_t[:, c:c + 1])
        eng.drain()
        return eng.tensor_copy(hw2_t, hw2f_t)

    for kc in range(DC):
        for hf in range(2):
            @s.dma("sync")
            def _(eng, kc=kc, hf=hf):
                return [eng.dma_start(out=wstage,
                                      in_=rw1[kc * 128:(kc + 1) * 128, hf * (R1N // 2):(hf + 1) * (R1N // 2)])]

            @s.step("vector")
            def _(eng, kc=kc, hf=hf):
                return eng.tensor_copy(w1_b[:, kc * R1N + hf * (R1N // 2): kc * R1N + (hf + 1) * (R1N // 2)], wstage)

    for hf in range(2):
        @s.dma("sync")
        def _(eng, hf=hf):
            return [eng.dma_start(out=wstage[0:56, :],
                                  in_=rw1[1024:1080, hf * (R1N // 2):(hf + 1) * (R1N // 2)])]

        @s.step("vector")
        def _(eng, hf=hf):
            return eng.tensor_copy(w56_b[0:56, hf * (R1N // 2):(hf + 1) * (R1N // 2)], wstage[0:56, :])

    @s.dma("sync")
    def _(eng):
        return [eng.dma_start(out=wstage[:, 0:NCH1 * 16].rearrange("p (c f) -> p c f", f=16),
                              in_=rw2.rearrange("(c p) f -> p c f", p=128))]

    @s.step("vector")
    def _(eng):
        return eng.tensor_copy(w2_b, wstage[:, 0:NCH1 * 16])

    for kc in range(DC):
        @s.dma("sync")
        def _(eng, kc=kc):
            return [eng.dma_start(out=wstage[:, 0:D], in_=Wo[kc * 128:(kc + 1) * 128, :])]

        @s.step("vector")
        def _(eng, kc=kc):
            return eng.tensor_scalar_mul(wo_b[:, kc * D:(kc + 1) * D], wstage[:, 0:D],
                                         onw_t[:, kc:kc + 1])

    xs3 = xs_t.rearrange("p (c t) -> p c t", c=DC)
    ln3 = ln_t.rearrange("p (c t) -> p c t", c=DC)
    sq3 = sq_t.rearrange("p (c t) -> p c t", c=DC)
    lntmp3 = lntmp_t.rearrange("p (c t) -> p c t", c=DC)
    br3 = [b.rearrange("p (c t) -> p c t", c=DC) for b in br_t]
    prod_t = on_t
    prod3 = prod_t.rearrange("p (c t) -> p c t", c=DC)
    on3 = on_t.rearrange("p (c t) -> p c t", c=DC)
    mixd_t = lntmp_t[:, 0:2 * nt]
    m2w_t = lntmp_t[:, 2 * nt:4 * nt]
    S1 = srow[0:1, 0:nt]
    S2 = srow[0:1, nt:2 * nt]
    DEN = srow[0:1, 2 * nt:3 * nt]
    RR = srow[0:1, 3 * nt:4 * nt]

    for j in range(nsub):
        t0 = j * nt

        @s.dma("sync")
        def _(eng, t0=t0):
            ins = [eng.dma_start(out=xs3, in_=xs.rearrange("(c p) t -> p c t", p=128)[:, :, t0:t0 + nt])]
            for bi, hbm in enumerate((loc, mid, dlt, vd)):
                ins.append(eng.dma_start(out=br3[bi],
                                         in_=hbm.rearrange("(c p) t -> p c t", p=128)[:, :, t0:t0 + nt]))
            return ins

        emit_ln_fm(s, nc, xs3, ln3, sq3, srow, srow_bf, ps_stA, ps_stB,
                   ps_a, ps_b, lntmp3, onk_t, onb_t, nt)

        def flush(first_row):
            @s.dma("sync")
            def _(eng):
                return [eng.dma_start(out=fscratch[0:4, :], in_=stageA_t),
                        eng.dma_start(out=fscratch[4:8, :], in_=stageB_t)]

            @s.dma("gpsimd")
            def _(eng, first_row=first_row):
                return [eng.dma_start(out=feats_t[first_row:first_row + 8, :], in_=fscratch[:, :])]

        for bi in range(4):
            @s.step("scalar")
            def _(eng, bi=bi):
                return eng.activation(sq_t, br_t[bi], AF.Square)

            @s.step("tensor")
            def _(eng, bi=bi):
                for c in range(DC):
                    eng.matmul(ps_row[0:4, :], hsel_t[:, c * 4:(c + 1) * 4], br3[bi][:, c, :],
                               start=(c == 0), stop=(c == DC - 1))
                for c in range(DC):
                    i = eng.matmul(ps_rowB[0:4, :], hsel_t[:, c * 4:(c + 1) * 4], sq3[:, c, :],
                                   start=(c == 0), stop=(c == DC - 1))
                return i

            @s.step("vector", nodrain=True)
            def _(eng, bi=bi):
                eng.tensor_single_scalar(stageA_t, ps_row[0:4, :], 1.0 / 256.0, ALU.mult)
                eng.tensor_single_scalar(stageB_t, ps_rowB[0:4, :], 1.0 / 255.0, ALU.mult)
                eng.drain()
                eng.tensor_tensor(dn_t[0:4, :], stageA_t, stageA_t, ALU.mult)
                eng.drain()
                eng.tensor_single_scalar(dn_t[0:4, :], dn_t[0:4, :], 256.0 / 255.0, ALU.mult)
                eng.drain()
                return eng.tensor_tensor(stageB_t, stageB_t, dn_t[0:4, :], ALU.subtract)

            flush(bi * 8)

        pairs = [(0, 1), (0, 2), (0, 3), (1, 2), (1, 3), (2, 3)]
        for gi in range(3):
            for pi in range(2):
                i_, j_ = pairs[gi * 2 + pi]

                @s.step("vector")
                def _(eng, i_=i_, j_=j_):
                    return eng.tensor_tensor(prod_t, br_t[i_], br_t[j_], ALU.mult)

                @s.step("tensor")
                def _(eng, pi=pi):
                    for c in range(DC):
                        i = eng.matmul((ps_row if pi == 0 else ps_rowB)[0:4, :],
                                       hsel_t[:, c * 4:(c + 1) * 4], prod3[:, c, :],
                                       start=(c == 0), stop=(c == DC - 1))
                    return i

            @s.step("vector", nodrain=True)
            def _(eng):
                eng.tensor_single_scalar(stageA_t, ps_row[0:4, :], 1.0 / 256.0, ALU.mult)
                return eng.tensor_single_scalar(stageB_t, ps_rowB[0:4, :], 1.0 / 256.0, ALU.mult)

            flush(32 + gi * 8)

        psl3 = [ps_mm, ps_a, ps_b]
        prev_evac = None
        for g0 in range(0, NCH1, 3):
            mcs = list(range(g0, min(g0 + 3, NCH1)))

            @s.step("tensor", wait_abs=prev_evac)
            def _(eng, mcs=mcs):
                for idx, mc in enumerate(mcs):
                    ps = psl3[idx]
                    for kc in range(DC):
                        eng.matmul(ps[:, :], w1_b[:, kc * R1N + mc * 128: kc * R1N + (mc + 1) * 128],
                                   ln3[:, kc, :], start=(kc == 0), stop=False)
                    i = eng.matmul(ps[:, :], w56_b[0:56, mc * 128:(mc + 1) * 128],
                                   feats_t[0:56, :], start=False, stop=True)
                return i

            prev_evac = s.target + 1

            @s.step("scalar", nodrain=True)
            def _(eng, mcs=mcs):
                for idx, mc in enumerate(mcs):
                    i = eng.activation(h1_t[:, mc * nt:(mc + 1) * nt], psl3[idx][:, :], AF.Silu,
                                       bias=rb1_t[:, mc:mc + 1])
                return i

        @s.step("tensor")
        def _(eng):
            for mc in range(NCH1):
                i = eng.matmul(ps_lg[:, :], w2_b[:, mc * 16:(mc + 1) * 16],
                               h1_t[:, mc * nt:(mc + 1) * nt], start=(mc == 0), stop=(mc == NCH1 - 1))
            return i

        @s.step("vector")
        def _(eng):
            return eng.tensor_scalar_add(lgm_t, ps_lg[:, :], rb2_t[:, 0:1])

        @s.step("tensor")
        def _(eng):
            for nb in range(nt // 128):
                i = eng.transpose(ps_t1[:, nb * 16:(nb + 1) * 16],
                                  lgm_t[:, nb * 128:(nb + 1) * 128], id16_t)
            return i

        @s.step("vector")
        def _(eng):
            return eng.tensor_copy(lgtm_t, ps_t1[:, :])

        LG = lgtm_t.rearrange("p (h i) -> p h i", i=4)
        M1 = stat_t[:, 0:16].rearrange("p (h o) -> p h o", o=1)
        SS = stat_t[:, 16:32].rearrange("p (h o) -> p h o", o=1)

        @s.step("vector")
        def _(eng):
            eng.tensor_reduce(M1, LG, AXX, ALU.max)
            eng.drain()
            am, bm = bcast(LG, M1)
            return eng.tensor_tensor(LG, am, bm, ALU.subtract)

        @s.step("scalar")
        def _(eng):
            return eng.activation(lgtm_t, lgtm_t, AF.Exp)

        @s.step("vector")
        def _(eng):
            eng.tensor_reduce(SS, LG, AXX, ALU.add)
            eng.drain()
            eng.reciprocal(stat_t[:, 16:32], stat_t[:, 16:32])
            eng.drain()
            am, bm = bcast(LG, SS)
            eng.tensor_tensor(LG, am, bm, ALU.mult)
            eng.drain()
            return eng.tensor_scalar(ptok_t, lgtm_t, 0.96, 0.01, ALU.mult, ALU.add)

        @s.step("tensor")
        def _(eng):
            for nb in range(nt // 128):
                i = eng.transpose(ps_t2[:, nb * 128:(nb + 1) * 128],
                                  ptok_t[:, nb * 16:(nb + 1) * 16], id128_t)
            return i

        @s.step("vector")
        def _(eng):
            return eng.tensor_copy(pfm_t, ps_t2[:, :])

        mixf3 = lntmp_t.rearrange("p (c t) -> p c t", c=DC)
        for bi in range(4):
            for hg in range(2):
                @s.step("tensor")
                def _(eng, bi=bi, hg=hg):
                    for hi in range(2):
                        h = hg * 2 + hi
                        r = h * 4 + bi
                        ps = (ps_mm, ps_a)[hi]
                        i = eng.matmul(ps[:, 0:nt], esel_t[:, r * 128:(r + 1) * 128],
                                       pfm_t[:, :], start=True, stop=True)
                    return i

                @s.step("vector", nodrain=True)
                def _(eng, bi=bi, hg=hg):
                    for hi in range(2):
                        h = hg * 2 + hi
                        ps = (ps_mm, ps_a)[hi]
                        for ci in range(2):
                            mc = 2 * h + ci
                            if bi == 0:
                                i = eng.tensor_tensor(mixf3[:, mc, :], br3[0][:, mc, :],
                                                      ps[:, 0:nt], ALU.mult)
                            else:
                                i = eng.tensor_tensor(prod3[:, mc, :], br3[bi][:, mc, :],
                                                      ps[:, 0:nt], ALU.mult)
                    if bi > 0:
                        eng.drain()
                        for hi in range(2):
                            h = hg * 2 + hi
                            for ci in range(2):
                                mc = 2 * h + ci
                                i = eng.tensor_tensor(mixf3[:, mc, :], mixf3[:, mc, :],
                                                      prod3[:, mc, :], ALU.add)
                    return i

        @s.step("scalar")
        def _(eng):
            return eng.activation(sq_t, lntmp_t, AF.Square)

        @s.step("tensor")
        def _(eng):
            for c in range(DC):
                eng.matmul(ps_row[0:4, :], hsel_t[:, c * 4:(c + 1) * 4], sq3[:, c, :],
                           start=(c == 0), stop=(c == DC - 1))
            for c in range(DC):
                i = eng.matmul(ps_rowB[0:4, :], hw2_t[:, c * 4:(c + 1) * 4], sq3[:, c, :],
                               start=(c == 0), stop=(c == DC - 1))
            return i

        @s.step("vector", nodrain=True)
        def _(eng):
            eng.tensor_single_scalar(dn_t[0:4, :], ps_row[0:4, :], 1.0 / 256.0, ALU.mult)
            eng.tensor_single_scalar(stageA_t, ps_rowB[0:4, :], 1.0 / 256.0, ALU.mult)
            eng.drain()
            eng.tensor_single_scalar(dn_t[0:4, :], dn_t[0:4, :], LN_EPS, ALU.add)
            eng.drain()
            eng.tensor_single_scalar(dn_t[0:4, :], dn_t[0:4, :], LN_EPS, ALU.mult)
            eng.drain()
            return eng.tensor_tensor(dn_t[0:4, :], stageA_t, dn_t[0:4, :], ALU.add)

        @s.step("scalar")
        def _(eng):
            return eng.activation(dn_t[0:4, :], dn_t[0:4, :], AF.Sqrt)

        @s.step("vector")
        def _(eng):
            eng.reciprocal(dn_t[0:4, :], dn_t[0:4, :])
            eng.drain()
            return eng.tensor_copy(dnb_t, dn_t[0:4, :])

        for hg in range(2):
            @s.step("tensor")
            def _(eng, hg=hg):
                for hi in range(2):
                    h = hg * 2 + hi
                    ps = (ps_mm, ps_a)[hi]
                    i = eng.matmul(ps[:, 0:nt], sel4_t[:, h * 128:(h + 1) * 128], dnb_t,
                                   start=True, stop=True)
                return i

            @s.step("vector", nodrain=True)
            def _(eng, hg=hg):
                for hi in range(2):
                    h = hg * 2 + hi
                    ps = (ps_mm, ps_a)[hi]
                    for ci in range(2):
                        mc = 2 * h + ci
                        i = eng.scalar_tensor_tensor(on3[:, mc, :], mixf3[:, mc, :],
                                                     mixw_t[:, mc:mc + 1],
                                                     ps[:, 0:nt], ALU.mult, ALU.mult)
                return i

        psl3 = [ps_mm, ps_a, ps_b]
        prev_evac = None
        for g0 in range(0, DC, 2):
            mcs = list(range(g0, g0 + 2))

            @s.step("tensor", wait_abs=prev_evac)
            def _(eng, mcs=mcs):
                for idx, mc in enumerate(mcs):
                    ps = psl3[idx]
                    for kc in range(DC):
                        i = eng.matmul(ps[:, :], wo_b[:, kc * D + mc * 128: kc * D + (mc + 1) * 128],
                                       on3[:, kc, :], start=(kc == 0), stop=(kc == DC - 1))
                return i

            prev_evac = s.target + 1

            @s.step("vector", nodrain=True)
            def _(eng, mcs=mcs):
                for idx, mc in enumerate(mcs):
                    i = eng.tensor_tensor(hout[:, idx * nt:(idx + 1) * nt], xs3[:, mc, :],
                                          psl3[idx][:, :], ALU.add)
                return i

            @s.dma("sync")
            def _(eng, mcs=mcs, j=j):
                return [eng.dma_start(out=h[mc * 128:(mc + 1) * 128, j * nt:(j + 1) * nt],
                                      in_=hout[:, idx * nt:(idx + 1) * nt])
                        for idx, mc in enumerate(mcs) for h in xo_hbms]

def _cvt_conv(w, taps):
    w = np.asarray(w, np.float32)
    return np.ascontiguousarray(w.reshape(DC, 128, taps).transpose(1, 0, 2).reshape(128, DC * taps))


def _l2_weight_maps(i):
    lora_A = np.asarray(i["L1_ffn_lora_A"], np.float32)
    lora_B = np.asarray(i["L1_ffn_lora_B"], np.float32)
    E2 = np.zeros((8, 128), dtype=BF)
    for e in range(8):
        E2[e, e * 16:(e + 1) * 16] = 2.0
    CCm = DCC
    msl_np = np.zeros((CCm, CCm), np.float32)
    for r in range(CCm):
        msl_np[r, :r] = -1.0
    return {
        "Wo": np.asarray(i["L0_attn_Wo"], np.float32),
        "gw": vec_tile(i["L0_attn_g_weight"]),
        "Wr": np.ascontiguousarray(np.asarray(i["L0_ffn_W_r"], np.float32).T),
        "Wk": np.ascontiguousarray(np.asarray(i["L0_ffn_W_k"], np.float32).T),
        "Wv": np.ascontiguousarray(np.asarray(i["L0_ffn_W_v"], np.float32).T),
        "mur": vec_tile(i["L0_ffn_mu_r"]), "muk": vec_tile(i["L0_ffn_mu_k"]),
        "bwT": np.ascontiguousarray(np.asarray(i["L1_ffn_base_w"], np.float32).T),
        "bb": vec_tile(i["L1_ffn_base_b"]),
        "Aall": np.ascontiguousarray(lora_A.transpose(1, 0, 2).reshape(D, 128)),
        "Ball": np.ascontiguousarray(lora_B.reshape(128, D)),
        "rwT": np.ascontiguousarray(np.asarray(i["L1_ffn_router_w"], np.float32).T),
        "rb": np.ascontiguousarray(np.asarray(i["L1_ffn_router_b"], np.float32).reshape(8, 1)),
        "E2": E2, "id8": np.eye(8, dtype=BF), "id128": np.eye(128, dtype=BF),
        "Wq1": np.asarray(i["L1_attn_Wq"], np.float32),
        "Wk1": np.asarray(i["L1_attn_Wk"], np.float32),
        "Wv1": np.asarray(i["L1_attn_Wv"], np.float32),
        "Wb1": np.asarray(i["L1_attn_Wb"], np.float32),
        "cq": _cvt_conv(i["L1_attn_conv_q_w"], 4), "ck": _cvt_conv(i["L1_attn_conv_k_w"], 4),
        "cv": _cvt_conv(i["L1_attn_conv_v_w"], 4), "lw": _cvt_conv(i["L1_attn_local_w"], 7),
        "mw": _cvt_conv(i["L1_attn_mid_w"], 25),
        "msl": msl_np, "msu": np.ascontiguousarray(msl_np.T),
        "mui": np.triu(np.ones((CCm, CCm), np.float32)),
        "ident": np.eye(128, dtype=BF),
    }


def _sel4():
    s = np.zeros((4, 4 * 128), dtype=BF)
    for h in range(4):
        s[h, h * 128:(h + 1) * 128] = 1.0
    return s


def _hsel():
    h = np.zeros((128, DC * 4), dtype=BF)
    for c in range(DC):
        h[:, c * 4 + c // 2] = 1.0
    return h


def _l3_weight_maps(i):
    rw1 = np.zeros((R1K, R1N), np.float32)
    rw1[:, 0:2160] = np.asarray(i["L1_attn_r_w1"], np.float32)
    rw2 = np.zeros((R1N, 16), np.float32)
    rw2[0:2160, :] = np.asarray(i["L1_attn_r_w2"], np.float32)
    rb1 = np.zeros((R1N,), np.float32)
    rb1[0:2160] = np.asarray(i["L1_attn_r_b1"], np.float32)
    esel = np.zeros((16, 16 * 128), dtype=BF)
    for r in range(16):
        esel[r, r * 128:(r + 1) * 128] = 1.0
    Ifm = np.zeros((128, 512), np.float32)
    Ifm[:, 0:128] = np.eye(128)
    Ifm[:, 384:512] = np.eye(128)
    lora_A = np.asarray(i["L1_ffn_lora_A"], np.float32)
    lora_B = np.asarray(i["L1_ffn_lora_B"], np.float32)
    E2 = np.zeros((8, 128), dtype=BF)
    for e in range(8):
        E2[e, e * 16:(e + 1) * 16] = 2.0
    return {
        "rw1": rw1, "rw2": rw2,
        "rb1": np.ascontiguousarray(rb1.reshape(NCH1, 128).T),
        "rb2": np.ascontiguousarray(np.asarray(i["L1_attn_r_b2"], np.float32).reshape(16, 1)),
        "mixw": vec_tile(np.asarray(i["L1_attn_mix_w"], np.float32).reshape(-1)),
        "onw": vec_tile(np.tile(np.asarray(i["L1_attn_onorm_w"], np.float32), 4)),
        "Wo1": np.asarray(i["L1_attn_Wo"], np.float32),
        "id16": np.eye(16, dtype=BF), "esel": esel, "Ifm": Ifm, "hsel": _hsel(), "sel4": _sel4(),
        "bwT": np.ascontiguousarray(np.asarray(i["L1_ffn_base_w"], np.float32).T),
        "bb": vec_tile(i["L1_ffn_base_b"]),
        "Aall": np.ascontiguousarray(lora_A.transpose(1, 0, 2).reshape(D, 128)),
        "Ball": np.ascontiguousarray(lora_B.reshape(128, D)),
        "rwT": np.ascontiguousarray(np.asarray(i["L1_ffn_router_w"], np.float32).T),
        "rb": np.ascontiguousarray(np.asarray(i["L1_ffn_router_b"], np.float32).reshape(8, 1)),
        "E2": E2, "id8": np.eye(8, dtype=BF), "id128": np.eye(128, dtype=BF),
    }


def _masks(q):
    am0 = np.zeros((128, 3), np.float32)
    am1 = np.zeros((128, 3), np.float32)
    for p in range(3):
        if p < q:
            am0[:, p] = 1.0
        if p < q - 1:
            am1[:, p] = 1.0
    return am0, am1


def kernel(**inputs):
    i = {k: np.asarray(v) for k, v in inputs.items()}
    cst = const_inputs()
    x0 = np.asarray(i["L0_attn_hidden_states"], np.float32)
    x0_fm = [np.ascontiguousarray(x0[b].T) for b in range(B)]

    nc1 = build_L1()
    l1_common = {
        "Wr": np.ascontiguousarray(i["L0_ffn_W_r"].astype(np.float32).T),
        "Wk": np.ascontiguousarray(i["L0_ffn_W_k"].astype(np.float32).T),
        "Wv": np.ascontiguousarray(i["L0_ffn_W_v"].astype(np.float32).T),
        "mur": vec_tile(i["L0_ffn_mu_r"]), "muk": vec_tile(i["L0_ffn_mu_k"]),
        "Wq": i["L0_attn_Wq"].astype(np.float32),
        "Wf": i["L0_attn_Wf"].astype(np.float32),
        "Wi": i["L0_attn_Wi"].astype(np.float32),
        "mask_u": np.triu(np.ones((CC, CC), np.float32)),
        "ident": np.eye(128, dtype=BF), **cst,
    }
    in_maps = []
    for c in range(NCORES):
        b, q = divmod(c, 4)
        lo = q * TPC
        halo = np.zeros((D, 1), np.float32) if q == 0 else x0_fm[b][:, lo - 1:lo]
        xs = np.ascontiguousarray(np.concatenate([halo, x0_fm[b][:, lo:lo + TPC]], axis=1))
        in_maps.append({"xs": xs, **l1_common})
    r1 = run_bass_kernel_spmd(nc1, in_maps, core_ids=list(range(NCORES)))
    R1 = [{k: np.asarray(v) for k, v in r.items()} for r in r1.results]

    nc2 = build_L2()
    wm2 = _l2_weight_maps(i)
    in_maps = []
    for c in range(NCORES):
        b, q = divmod(c, 4)
        own = R1[c]
        if q == 0:
            x1h = np.zeros((D, OH), np.float32)
            oh = np.zeros((D, OH), BF)
            qth = np.zeros((D, OH), BF)
            bch = np.zeros((128, 16), np.float32)
        else:
            prev = R1[c - 1]
            x1h = np.asarray(prev["x1o"][:, TPC - OH:], np.float32)
            oh = np.ascontiguousarray(prev["oloc"][:, TPC - OH:])
            qth = np.ascontiguousarray(prev["qto"][:, TPC - OH:])
            bch = np.zeros((128, 16), np.float32)
            pbc = np.asarray(prev["bco"], np.float32)
            for h in range(8):
                bch[:, h * 2 + 0] = pbc[:, h * NCH + NCH - 2]
                bch[:, h * 2 + 1] = pbc[:, h * NCH + NCH - 1]
        x1e = np.ascontiguousarray(np.concatenate([x1h, np.asarray(own["x1o"], np.float32)], axis=1))
        SL = np.ascontiguousarray(np.concatenate(
            [np.asarray(R1[b * 4 + p]["sloco"], np.float32) for p in range(4)], axis=1))
        AT = np.ascontiguousarray(np.concatenate(
            [np.asarray(R1[b * 4 + p]["atoto"], np.float32) for p in range(4)], axis=1))
        am0, am1 = _masks(q)
        hmask = np.full((128, HLO), 0.0 if q == 0 else 1.0, np.float32)
        in_maps.append({
            "x1e": x1e, "oloc": np.ascontiguousarray(own["oloc"]),
            "qto": np.ascontiguousarray(own["qto"]),
            "bco": np.asarray(own["bco"], np.float32),
            "oh": oh, "qth": qth, "bch": bch,
            "SL": SL, "AT": AT, "am0": am0, "am1": am1, "hmask": hmask,
            **wm2, **cst,
        })
    r2 = run_bass_kernel_spmd(nc2, in_maps, core_ids=list(range(NCORES)))
    R2 = [{k: np.asarray(v) for k, v in r.items()} for r in r2.results]

    nc3 = build_L3()
    wm3 = _l3_weight_maps(i)
    in_maps = []
    for c in range(NCORES):
        b, q = divmod(c, 4)
        own = R2[c]
        Mcat = np.ascontiguousarray(np.concatenate(
            [np.asarray(R2[b * 4 + p]["Mo"], np.float32) for p in range(4)], axis=1))
        Ccat = np.ascontiguousarray(np.concatenate(
            [np.asarray(R2[b * 4 + p]["Co"], np.float32) for p in range(4)], axis=1))
        am0, _ = _masks(q)
        in_maps.append({
            "x4i": np.asarray(own["x4o"], np.float32),
            "loci": np.ascontiguousarray(own["loco"]),
            "midi": np.ascontiguousarray(own["mido"]),
            "vdi": np.ascontiguousarray(own["vdo"]),
            "qhi": np.ascontiguousarray(own["qho"]),
            "wfi": np.ascontiguousarray(own["wfo"]),
            "utmi": np.ascontiguousarray(own["utm"]),
            "khtmi": np.ascontiguousarray(own["khtmo"]),
            "atni": np.ascontiguousarray(own["atno"]),
            "Mcat": Mcat, "Ccat": Ccat, "am0": am0,
            **wm3, **cst,
        })
    r3 = run_bass_kernel_spmd(nc3, in_maps, core_ids=list(range(NCORES)))
    R3 = [np.asarray(r["x6o"], np.float32) for r in r3.results]

    out = np.empty((B, T, D), np.float32)
    for c in range(NCORES):
        b, q = divmod(c, 4)
        out[b, q * TPC:(q + 1) * TPC, :] = R3[c].T
    return out


# version 13 (final)
# speedup vs baseline: 1.1880x; 1.1880x over previous
from contextlib import ExitStack
import numpy as np
import ml_dtypes
import concourse.bass as bass
import concourse.mybir as mybir
from concourse.bass_utils import run_bass_kernel_spmd

F32 = mybir.dt.float32
BF16 = mybir.dt.bfloat16
AF = mybir.ActivationFunctionType
ALU = mybir.AluOpType
AXX = mybir.AxisListType.X
BF = ml_dtypes.bfloat16

B, T, D = 2, 8192, 1024
NCORES = 8
TPC = T // 4
DC = D // 128
LN_EPS = 1e-5
CC = 64
NCH = TPC // CC
DCC = 128
DNCH = TPC // DCC
HB = 48
HLO = 32


class _DrainProxy:
    def __init__(self, eng):
        object.__setattr__(self, "_eng", eng)
        object.__setattr__(self, "_n", 0)

    def __getattr__(self, name):
        eng = object.__getattribute__(self, "_eng")
        attr = getattr(eng, name)
        if name in ("drain", "wait_ge", "then_inc") or not callable(attr):
            return attr

        def wrapped(*a, **k):
            if object.__getattribute__(self, "_n") > 0:
                eng.drain()
            object.__setattr__(self, "_n", object.__getattribute__(self, "_n") + 1)
            return attr(*a, **k)
        return wrapped


class S:
    def __init__(self, nc, block, tk, gk=None):
        self.nc, self.block, self.tk, self.gk = nc, block, tk, gk
        self.target = 0
        self.gtarget = 0
        self.n_steps = 0
        self.n_dmas = 0

    def _waits(self, eng, wait_t, wait_g):
        eng.wait_ge(self.tk, wait_t)
        if wait_g > 0:
            eng.wait_ge(self.gk, wait_g)

    def step(self, eng_name, nodrain=False, wait_back=0, wait_abs=None):
        auto = (eng_name in ("vector", "scalar", "gpsimd")) and not nodrain

        def deco(fn):
            wait_t = self.target - wait_back if wait_abs is None else wait_abs
            wait_t = max(0, wait_t)
            wait_g = self.gtarget

            def run(eng):
                self._waits(eng, wait_t, wait_g)
                last = fn(_DrainProxy(eng) if auto else eng)
                assert last is not None
                last.then_inc(self.tk, 1)

            getattr(self.block, eng_name)(run)
            self.target = self.target + 1
            self.n_steps += 1
        return deco

    def dma(self, eng_name):
        use_gk = eng_name == "gpsimd"

        def deco(fn):
            wait_t, wait_g = self.target, self.gtarget
            cell = {}

            def run(eng):
                self._waits(eng, wait_t, wait_g)
                insts = fn(eng)
                for i in insts:
                    i.then_inc(self.gk if use_gk else self.tk, 16)
                cell["n"] = len(insts)

            getattr(self.block, eng_name)(run)
            if use_gk:
                self.gtarget = wait_g + 16 * cell["n"]
            else:
                self.target = wait_t + 16 * cell["n"]
            self.n_dmas += cell["n"]
        return deco

    def fence(self, eng_name="sync"):
        wait_t, wait_g = self.target, self.gtarget

        def run(eng):
            self._waits(eng, wait_t, wait_g)

        getattr(self.block, eng_name)(run)


def bcast(a, b):
    return bass.broadcast_tensor_aps(a, b)


class Alloc:
    def __init__(self, nc, st):
        self.nc, self.st = nc, st

    def sb(self, name, shape, dtype=F32):
        return self.st.enter_context(self.nc.sbuf_tensor(name, shape, dtype))[:, :]

    def ps(self, name, shape, dtype=F32):
        return self.st.enter_context(self.nc.psum_tensor(name, shape, dtype))[:, :]

    def sem(self, name):
        return self.st.enter_context(self.nc.semaphore(name))

    def block(self):
        return self.st.enter_context(self.nc.Block())


def vec_tile(v):
    v = np.asarray(v, dtype=np.float32)
    return np.ascontiguousarray(v.reshape(-1, 128).T)


def const_inputs():
    return {
        "ones_k": np.ones((128, 1), dtype=BF),
        "ones_b": np.ones((1, 128), dtype=BF),
    }


def emit_weight_load(s, wf32, wb16, hbm_ap, nchunk, mwid):
    @s.dma("gpsimd")
    def _(eng):
        return [eng.dma_start(out=wb16.rearrange("p (c m) -> p c m", c=nchunk),
                              in_=hbm_ap.rearrange("(c p) m -> p c m", p=128))]


def emit_ln_fm(s, nc, x3, ln3, sq3, srow, srow_bf, ps_stA, ps_stB, ps_a, ps_b, tmp3,
               ones_k, ones_b, nt, nchunk=DC, dim=D, eps=LN_EPS):
    ntmax = srow.shape[1] // 4
    MEAN = srow[0:1, 0:nt]
    VAR = srow[0:1, ntmax:ntmax + nt]
    RSTD = srow[0:1, 2 * ntmax:2 * ntmax + nt]
    BROW = srow[0:1, 3 * ntmax:3 * ntmax + nt]
    RSTD_BF = srow_bf[0:1, 0:nt]
    BROW_BF = srow_bf[0:1, ntmax:ntmax + nt]

    @s.step("vector")
    def _(eng):
        return eng.tensor_copy(ln3, x3)

    @s.step("tensor")
    def _(eng):
        for c in range(nchunk):
            i = eng.matmul(ps_stA[0:1, 0:nt], ones_k, ln3[:, c, :],
                           start=(c == 0), stop=(c == nchunk - 1))
        return i

    @s.step("scalar")
    def _(eng):
        return eng.activation(sq3, ln3, AF.Square)

    @s.step("tensor")
    def _(eng):
        for c in range(nchunk):
            i = eng.matmul(ps_stB[0:1, 0:nt], ones_k, sq3[:, c, :],
                           start=(c == 0), stop=(c == nchunk - 1))
        return i

    @s.step("vector")
    def _(eng):
        eng.tensor_single_scalar(MEAN, ps_stA[0:1, 0:nt], 1.0 / dim, ALU.mult)
        eng.tensor_single_scalar(VAR, ps_stB[0:1, 0:nt], 1.0 / dim, ALU.mult)
        eng.drain()
        eng.tensor_tensor(RSTD, MEAN, MEAN, ALU.mult)
        eng.drain()
        eng.tensor_tensor(VAR, VAR, RSTD, ALU.subtract)
        eng.drain()
        return eng.tensor_single_scalar(VAR, VAR, eps, ALU.add)

    @s.step("scalar")
    def _(eng):
        return eng.activation(VAR, VAR, AF.Sqrt)

    @s.step("vector")
    def _(eng):
        eng.reciprocal(RSTD, VAR)
        eng.drain()
        eng.tensor_tensor(BROW, MEAN, RSTD, ALU.mult)
        eng.drain()
        eng.tensor_single_scalar(BROW, BROW, -1.0, ALU.mult)
        eng.drain()
        eng.tensor_copy(RSTD_BF, RSTD)
        return eng.tensor_copy(BROW_BF, BROW)

    @s.step("tensor")
    def _(eng):
        eng.matmul(ps_a[:, 0:nt], ones_b, RSTD_BF, start=True, stop=True)
        return eng.matmul(ps_b[:, 0:nt], ones_b, BROW_BF, start=True, stop=True)

    @s.step("vector")
    def _(eng):
        am, asm = bcast(x3, ps_a[:, 0:nt].rearrange("p (o t) -> p o t", o=1))
        eng.tensor_tensor(tmp3, am, asm, ALU.mult)
        eng.drain()
        bm, bsm = bcast(tmp3, ps_b[:, 0:nt].rearrange("p (o t) -> p o t", o=1))
        return eng.tensor_tensor(ln3, bm, bsm, ALU.add)




def emit_proj_grouped(s, w_b, in3, ps_list, wdt, nmc, post, kw=D, kchunks=DC):
    gsz = len(ps_list)
    prev_evac = None
    for g0 in range(0, nmc, gsz):
        mcs = list(range(g0, min(g0 + gsz, nmc)))

        @s.step("tensor", wait_abs=prev_evac)
        def _(eng, mcs=mcs):
            for idx, mc in enumerate(mcs):
                ps = ps_list[idx]
                for kc in range(kchunks):
                    i = eng.matmul(ps[:, 0:wdt], w_b[:, kc * kw + mc * 128: kc * kw + (mc + 1) * 128],
                                   in3[:, kc, 0:wdt], start=(kc == 0), stop=(kc == kchunks - 1))
            return i

        t_pre = s.target
        post(mcs)
        prev_evac = t_pre + 1

def emit_ffn(s, nc, al, tag, xs_hbm, xo_hbms, ncols, nhalo, wr_b, wk_b, wv_b,
             mur_t, muk_t, onk_t, onb_t, half=True):
    widths = []
    rem = ncols
    while rem > 0:
        w = min(512, rem)
        widths.append(w)
        rem -= w
    nt = 512
    xs_t = al.sb(f"{tag}xs_t", [128, DC * (nt + 1)], F32)
    ln_t = al.sb(f"{tag}ln_t", [128, DC * nt], BF16)
    lnp_t = al.sb(f"{tag}lnp_t", [128, DC], BF16)
    sq_t = al.sb(f"{tag}sq_t", [128, DC * nt], BF16)
    dd_t = al.sb(f"{tag}dd_t", [128, DC * nt], F32)
    xr_t = al.sb(f"{tag}xr_t", [128, DC * nt], BF16)
    xk_t = al.sb(f"{tag}xk_t", [128, DC * nt], BF16)
    rs_t = al.sb(f"{tag}rs_t", [128, DC * nt], BF16)
    ka_t = al.sb(f"{tag}ka_t", [128, DC * nt], BF16)
    krelu = al.sb(f"{tag}krelu", [128, 4 * nt], F32)
    hout = al.sb(f"{tag}hout", [128, 4 * nt], F32)
    srow = al.sb(f"{tag}srow", [1, 4 * nt], F32)
    srow_bf = al.sb(f"{tag}srow_bf", [1, 2 * nt], BF16)
    ps_mm = al.ps(f"{tag}ps_mm", [128, nt], F32)
    ps_stA = al.ps(f"{tag}ps_stA", [1, nt], F32)
    ps_stB = al.ps(f"{tag}ps_stB", [1, nt], F32)
    ps_a = al.ps(f"{tag}ps_a", [128, nt], F32)
    ps_b = al.ps(f"{tag}ps_b", [128, nt], F32)
    ps_c = al.ps(f"{tag}ps_c", [128, nt], F32)
    ps_list = [ps_mm, ps_a, ps_b, ps_c]

    xs3 = xs_t.rearrange("p (c t) -> p c t", c=DC)
    ln3 = ln_t.rearrange("p (c t) -> p c t", c=DC)
    lnp3 = lnp_t.rearrange("p (c o) -> p c o", o=1)
    sq3 = sq_t.rearrange("p (c t) -> p c t", c=DC)
    dd3 = dd_t.rearrange("p (c t) -> p c t", c=DC)
    xr3 = xr_t.rearrange("p (c t) -> p c t", c=DC)
    xk3 = xk_t.rearrange("p (c t) -> p c t", c=DC)
    ka3 = ka_t.rearrange("p (c t) -> p c t", c=DC)
    bufs = dict(xs_t=xs_t, ln_t=ln_t, sq_t=sq_t, dd_t=dd_t, srow=srow,
                srow_bf=srow_bf, ps_mm=ps_mm, ps_stA=ps_stA, ps_stB=ps_stB,
                ps_a=ps_a, ps_b=ps_b)

    @s.dma("sync")
    def _(eng):
        return [eng.dma_start(out=xs3[:, :, 0:1],
                              in_=xs_hbm.rearrange("(c p) t -> p c t", p=128)[:, :, nhalo - 1:nhalo],
                              allow_slow_non_contiguous=True)]

    emit_ln_fm(s, nc, xs3[:, :, 0:1], lnp3, sq3[:, :, 0:1], srow, srow_bf,
               ps_stA, ps_stB, ps_a, ps_b, dd3[:, :, 0:1], onk_t, onb_t, 1)

    col0 = 0
    for wdt in widths:
        lo = nhalo + col0

        @s.dma("sync")
        def _(eng, lo=lo, wdt=wdt):
            return [eng.dma_start(out=xs3[:, :, 1:1 + wdt],
                                  in_=xs_hbm.rearrange("(c p) t -> p c t", p=128)[:, :, lo:lo + wdt])]

        emit_ln_fm(s, nc, xs3[:, :, 1:1 + wdt], ln3[:, :, 0:wdt], sq3[:, :, 0:wdt],
                   srow, srow_bf, ps_stA, ps_stB, ps_a, ps_b, dd3[:, :, 0:wdt],
                   onk_t, onb_t, wdt)

        @s.step("vector")
        def _(eng, wdt=wdt):
            if wdt > 1:
                eng.tensor_tensor(dd3[:, :, 1:wdt], ln3[:, :, 1:wdt], ln3[:, :, 0:wdt - 1],
                                  ALU.subtract)
                eng.drain()
            return eng.tensor_tensor(dd3[:, :, 0:1], ln3[:, :, 0:1], lnp3, ALU.subtract)

        @s.step("vector")
        def _(eng, wdt=wdt):
            for c in range(DC):
                if wdt > 1:
                    eng.scalar_tensor_tensor(xr3[:, c, 1:wdt], dd3[:, c, 1:wdt], mur_t[:, c:c + 1],
                                             ln3[:, c, 0:wdt - 1], ALU.mult, ALU.add)
                eng.scalar_tensor_tensor(xr3[:, c, 0:1], dd3[:, c, 0:1], mur_t[:, c:c + 1],
                                         lnp3[:, c, :], ALU.mult, ALU.add)
                if wdt > 1:
                    eng.scalar_tensor_tensor(xk3[:, c, 1:wdt], dd3[:, c, 1:wdt], muk_t[:, c:c + 1],
                                             ln3[:, c, 0:wdt - 1], ALU.mult, ALU.add)
                eng.scalar_tensor_tensor(xk3[:, c, 0:1], dd3[:, c, 0:1], muk_t[:, c:c + 1],
                                         lnp3[:, c, :], ALU.mult, ALU.add)
            return eng.tensor_copy(lnp3, ln3[:, :, wdt - 1:wdt])

        def post_r(mcs, wdt=wdt):
            @s.step("scalar", nodrain=True)
            def _(eng):
                for idx, mc in enumerate(mcs):
                    i = eng.activation(rs_t[:, mc * nt:mc * nt + wdt], ps_list[idx][:, 0:wdt], AF.Sigmoid)
                return i

        emit_proj_grouped(s, wr_b, xr3, ps_list, wdt, DC, post_r)

        def post_k(mcs, wdt=wdt):
            @s.step("scalar", nodrain=True)
            def _(eng):
                for idx, mc in enumerate(mcs):
                    eng.activation(krelu[:, idx * nt:idx * nt + wdt], ps_list[idx][:, 0:wdt], AF.Relu)
                eng.drain()
                for idx, mc in enumerate(mcs):
                    i = eng.activation(ka_t[:, mc * nt:mc * nt + wdt], krelu[:, idx * nt:idx * nt + wdt], AF.Square)
                return i

        emit_proj_grouped(s, wk_b, xk3, ps_list, wdt, DC, post_k)

        def post_v(mcs, wdt=wdt, col0=col0):
            @s.step("vector", nodrain=True)
            def _(eng):
                for idx, mc in enumerate(mcs):
                    eng.tensor_tensor(hout[:, idx * nt:idx * nt + wdt], rs_t[:, mc * nt:mc * nt + wdt],
                                      ps_list[idx][:, 0:wdt], ALU.mult)
                eng.drain()
                for idx, mc in enumerate(mcs):
                    i = eng.scalar_tensor_tensor(hout[:, idx * nt:idx * nt + wdt], hout[:, idx * nt:idx * nt + wdt],
                                                 0.5 if half else 1.0,
                                                 xs3[:, mc, 1:1 + wdt], ALU.mult, ALU.add)
                return i

            @s.dma("sync")
            def _(eng):
                return [eng.dma_start(out=h[mc * 128:(mc + 1) * 128, col0:col0 + wdt],
                                      in_=hout[:, idx * nt:idx * nt + wdt])
                        for idx, mc in enumerate(mcs) for h in xo_hbms]

        emit_proj_grouped(s, wv_b, ka3, ps_list, wdt, DC, post_v)

        col0 += wdt
    return bufs


def build_L1():
    nc = bass.Bass("TRN2", target_bir_lowering=False, debug=False)
    xs = nc.dram_tensor("xs", [D, TPC + 1], F32, kind="ExternalInput").ap()
    Wr = nc.dram_tensor("Wr", [D, D], F32, kind="ExternalInput").ap()
    Wk = nc.dram_tensor("Wk", [D, D], F32, kind="ExternalInput").ap()
    Wv = nc.dram_tensor("Wv", [D, D], F32, kind="ExternalInput").ap()
    mur = nc.dram_tensor("mur", [128, DC], F32, kind="ExternalInput").ap()
    muk = nc.dram_tensor("muk", [128, DC], F32, kind="ExternalInput").ap()
    Wq = nc.dram_tensor("Wq", [D, D], F32, kind="ExternalInput").ap()
    Wf = nc.dram_tensor("Wf", [D, D], F32, kind="ExternalInput").ap()
    Wi = nc.dram_tensor("Wi", [D, D], F32, kind="ExternalInput").ap()
    onk = nc.dram_tensor("ones_k", [128, 1], BF16, kind="ExternalInput").ap()
    onb = nc.dram_tensor("ones_b", [1, 128], BF16, kind="ExternalInput").ap()
    msku = nc.dram_tensor("mask_u", [CC, CC], F32, kind="ExternalInput").ap()
    idin = nc.dram_tensor("ident", [128, 128], BF16, kind="ExternalInput").ap()
    x1o = nc.dram_tensor("x1o", [D, TPC], F32, kind="ExternalOutput").ap()
    oloc = nc.dram_tensor("oloc", [D, TPC], BF16, kind="ExternalOutput").ap()
    qto = nc.dram_tensor("qto", [D, TPC], BF16, kind="ExternalOutput").ap()
    bco = nc.dram_tensor("bco", [128, 8 * NCH], F32, kind="ExternalOutput").ap()
    sloco = nc.dram_tensor("sloco", [128, 8 * 128], F32, kind="ExternalOutput").ap()
    atoto = nc.dram_tensor("atoto", [128, 8], F32, kind="ExternalOutput").ap()
    ln1 = nc.dram_tensor("ln1", [D, TPC], BF16).ap()

    LNSC = float(np.log(128.0 ** -0.5))
    nt = 512
    ntile = TPC // nt
    nch = nt // CC

    with ExitStack() as st:
        al = Alloc(nc, st)
        tk = al.sem("tk")
        gk = al.sem("gk")
        block = al.block()
        s = S(nc, block, tk, gk)

        onk_t = al.sb("onk_t", [128, 1], BF16)
        onb_t = al.sb("onb_t", [1, 128], BF16)
        msk_t = al.sb("msk_t", [CC, CC], F32)
        ident = al.sb("ident_t", [128, 128], BF16)

        @s.dma("sync")
        def _(eng):
            return [eng.dma_start(out=onk_t, in_=onk[:, :]),
                    eng.dma_start(out=onb_t, in_=onb[:, :]),
                    eng.dma_start(out=msk_t, in_=msku[:, :]),
                    eng.dma_start(out=ident, in_=idin[:, :])]

        with ExitStack() as stA:
            alA = Alloc(nc, stA)
            wstage = alA.sb("A_wstage", [128, D], F32)
            wr_b = alA.sb("A_wr_b", [128, DC * D], BF16)
            wk_b = alA.sb("A_wk_b", [128, DC * D], BF16)
            wv_b = alA.sb("A_wv_b", [128, DC * D], BF16)
            mur_t = alA.sb("A_mur", [128, DC], F32)
            muk_t = alA.sb("A_muk", [128, DC], F32)

            @s.dma("sync")
            def _(eng):
                return [eng.dma_start(out=mur_t, in_=mur[:, :]),
                        eng.dma_start(out=muk_t, in_=muk[:, :])]

            emit_weight_load(s, wstage, wr_b, Wr, DC, D)
            emit_weight_load(s, wstage, wk_b, Wk, DC, D)
            emit_weight_load(s, wstage, wv_b, Wv, DC, D)
            x1int = nc.dram_tensor("x1int", [D, TPC], F32).ap()
            fb = emit_ffn(s, nc, alA, "A_", xs, [x1o, x1int], TPC, 1,
                          wr_b, wk_b, wv_b, mur_t, muk_t, onk_t, onb_t)

            xt3 = fb["xs_t"].rearrange("p (c t) -> p c t", c=DC)[:, :, 0:512]
            lt3 = fb["ln_t"].rearrange("p (c t) -> p c t", c=DC)
            sq3 = fb["sq_t"].rearrange("p (c t) -> p c t", c=DC)
            tm3 = fb["dd_t"].rearrange("p (c t) -> p c t", c=DC)
            for u in range(4):
                @s.dma("sync")
                def _(eng, u=u):
                    return [eng.dma_start(out=xt3,
                                          in_=x1int.rearrange("(c p) t -> p c t", p=128)[:, :, u * 512:(u + 1) * 512])]

                emit_ln_fm(s, nc, xt3, lt3, sq3, fb["srow"], fb["srow_bf"],
                           fb["ps_stA"], fb["ps_stB"], fb["ps_a"], fb["ps_b"],
                           tm3, onk_t, onb_t, 512)

                @s.dma("sync")
                def _(eng, u=u):
                    return [eng.dma_start(out=ln1.rearrange("(c p) t -> p c t", p=128)[:, :, u * 512:(u + 1) * 512],
                                          in_=lt3)]

        with ExitStack() as stB:
            alB = Alloc(nc, stB)
            ln_t = alB.sb("B_ln", [128, DC * nt], BF16)
            f_t = alB.sb("B_f", [128, 2 * nt], F32)
            g_t = alB.sb("B_g", [128, 2 * nt], F32)
            sc1_t = alB.sb("B_sc1", [128, 2 * nt], F32)
            arel_t = alB.sb("B_arel", [128, 2 * nt], F32)
            apad_t = alB.sb("B_apad", [128, 2 * (nt + 1)], F32)
            eb_t = alB.sb("B_eb", [128, 2 * nt], BF16)
            qs_t = alB.sb("B_qs", [128, 2 * nt], BF16)
            qt_t = alB.sb("B_qt", [128, 2 * nt], BF16)
            kg_t = alB.sb("B_kg", [128, 2 * nt], BF16)
            kt_t = alB.sb("B_kt", [128, 2 * nt], BF16)
            kh_t = alB.sb("B_kh", [128, 2 * nt], BF16)
            v_t = alB.sb("B_v", [128, (nt // 128) * 256], BF16)
            ob_t = alB.sb("B_ob", [128, 2 * nt], BF16)
            oacc_t = alB.sb("B_oacc", [128, 2 * TPC], BF16)
            qacc_t = alB.sb("B_qacc", [128, 2 * TPC], BF16)
            d_t = alB.sb("B_d", [128, 2 * nch], F32)
            bc_t = alB.sb("B_bc", [128, 2 * NCH], F32)
            zcol = alB.sb("B_zcol", [128, 1], F32)
            lnsc_t = alB.sb("B_lnsc", [128, 1], F32)
            carry = alB.sb("B_carry", [128, 2], F32)
            s_f32 = alB.sb("B_sf32", [128, 2 * 128], F32)
            s_bf = alB.sb("B_sbf", [128, 2 * 128], BF16)
            pm_t = alB.sb("B_pm", [128, 2 * CC], BF16)
            ktm_t = alB.sb("B_ktm", [128, 2 * 128], BF16)
            wq_b = alB.sb("B_wq", [128, DC * 256], BF16)
            wf_b = alB.sb("B_wf", [128, DC * 256], BF16)
            wi_b = alB.sb("B_wi", [128, DC * 256], BF16)
            wstage = alB.sb("B_wstage", [128, 256], F32)
            ps_mm = alB.ps("B_psmm", [128, 512], F32)
            ps_o2 = alB.ps("B_pso2", [128, 512], F32)
            ps_p = alB.ps("B_psp", [64, 2 * CC], F32)
            ps_o = alB.ps("B_pso", [128, 2 * CC], F32)
            ps_u = alB.ps("B_psu", [128, 2 * 128], F32)
            ps_tr = alB.ps("B_pstr", [64, 2 * 128], BF16)

            ln3 = ln_t.rearrange("p (c t) -> p c t", c=DC)

            for hp in range(4):
                @s.dma("gpsimd")
                def _(eng, hp=hp):
                    return [eng.dma_start(out=buf.rearrange("p (c m) -> p c m", c=DC),
                                          in_=hbm[:, hp * 256:(hp + 1) * 256].rearrange("(c p) m -> p c m", p=128))
                            for hbm, buf in ((Wq, wq_b), (Wf, wf_b), (Wi, wi_b))]

                @s.step("vector")
                def _(eng):
                    eng.memset(zcol, 0.0)
                    eng.memset(lnsc_t, LNSC)
                    eng.memset(carry, 0.0)
                    eng.memset(s_f32, 0.0)
                    return eng.memset(s_bf, 0.0)

                for j in range(ntile):
                    t0 = j * nt

                    @s.dma("sync")
                    def _(eng, t0=t0):
                        return [eng.dma_start(out=ln3,
                                              in_=ln1.rearrange("(c p) t -> p c t", p=128)[:, :, t0:t0 + nt])]

                    @s.step("tensor")
                    def _(eng):
                        for hc in range(2):
                            ps = ps_mm if hc == 0 else ps_o2
                            for kc in range(DC):
                                i = eng.matmul(ps[:, 0:nt],
                                               wf_b[:, kc * 256 + hc * 128: kc * 256 + (hc + 1) * 128],
                                               ln3[:, kc, :], start=(kc == 0), stop=(kc == DC - 1))
                        return i

                    @s.step("scalar", nodrain=True)
                    def _(eng):
                        eng.copy(f_t[:, 0:nt], ps_mm[:, 0:nt])
                        return eng.copy(f_t[:, nt:2 * nt], ps_o2[:, 0:nt])

                    @s.step("scalar", nodrain=True)
                    def _(eng):
                        eng.activation(kg_t, f_t, AF.Sigmoid, scale=-1.0)
                        return eng.activation(sc1_t, f_t, AF.Relu, scale=-1.0)

                    @s.step("vector")
                    def _(eng):
                        return eng.scalar_tensor_tensor(g_t, f_t, -1.0, f_t, ALU.mult, ALU.max)

                    @s.step("scalar")
                    def _(eng):
                        eng.activation(g_t, g_t, AF.Exp, scale=-1.0)
                        eng.drain()
                        return eng.activation(g_t, g_t, AF.Ln, bias=1.0)

                    @s.step("vector")
                    def _(eng):
                        eng.tensor_tensor(g_t, g_t, sc1_t, ALU.add)
                        eng.drain()
                        return eng.tensor_single_scalar(g_t, g_t, -1.0, ALU.mult)

                    @s.step("vector")
                    def _(eng):
                        eng.tensor_copy(apad_t[:, 0:1], carry[:, 0:1])
                        eng.tensor_copy(apad_t[:, nt + 1:nt + 2], carry[:, 1:2])
                        eng.drain()
                        d0, dz0 = bcast(g_t[:, 0:nt], zcol)
                        eng.tensor_tensor_scan(apad_t[:, 1:nt + 1], d0, dz0, carry[:, 0:1], ALU.add, ALU.add)
                        eng.drain()
                        d1, dz1 = bcast(g_t[:, nt:2 * nt], zcol)
                        eng.tensor_tensor_scan(apad_t[:, nt + 2:2 * nt + 2], d1, dz1, carry[:, 1:2], ALU.add, ALU.add)
                        eng.drain()
                        eng.tensor_copy(carry[:, 0:1], apad_t[:, nt:nt + 1])
                        return eng.tensor_copy(carry[:, 1:2], apad_t[:, 2 * nt + 1:2 * nt + 2])

                    @s.step("vector")
                    def _(eng):
                        for h in range(2):
                            off = h * (nt + 1)
                            a_in = apad_t[:, off + 1:off + nt + 1].rearrange("p (c t) -> p c t", t=CC)
                            base = apad_t[:, off:off + nt].rearrange("p (c t) -> p c t", t=CC)[:, :, 0:1]
                            am, bm = bcast(a_in, base)
                            eng.tensor_tensor(arel_t[:, h * nt:(h + 1) * nt].rearrange("p (c t) -> p c t", t=CC),
                                              am, bm, ALU.subtract)
                            eng.drain()
                            ends = a_in[:, :, CC - 1:CC]
                            bm2, am2 = bcast(a_in, ends)
                            i = eng.tensor_tensor(sc1_t[:, h * nt:(h + 1) * nt].rearrange("p (c t) -> p c t", t=CC),
                                                  am2, bm2, ALU.subtract)
                            eng.drain()
                        return i

                    @s.step("scalar")
                    def _(eng, j=j):
                        for h in range(2):
                            eng.activation(
                                d_t[:, h * nch:(h + 1) * nch].rearrange("p (c o) -> p c o", o=1),
                                arel_t[:, h * nt:(h + 1) * nt].rearrange("p (c t) -> p c t", t=CC)[:, :, CC - 1:CC],
                                AF.Exp)
                            eng.drain()
                            i = eng.activation(
                                bc_t[:, h * NCH + j * nch:h * NCH + (j + 1) * nch].rearrange("p (c o) -> p c o", o=1),
                                apad_t[:, h * (nt + 1):h * (nt + 1) + nt].rearrange("p (c t) -> p c t", t=CC)[:, :, 0:1],
                                AF.Exp)
                            eng.drain()
                        return i

                    @s.step("scalar")
                    def _(eng):
                        return eng.activation(eb_t, arel_t, AF.Exp, scale=-1.0)

                    @s.step("vector")
                    def _(eng):
                        return eng.tensor_tensor(kt_t, kg_t, eb_t, ALU.mult)

                    @s.step("scalar")
                    def _(eng):
                        return eng.activation(eb_t, sc1_t, AF.Exp)

                    @s.step("vector")
                    def _(eng):
                        return eng.tensor_tensor(kh_t, kg_t, eb_t, ALU.mult)

                    @s.step("tensor")
                    def _(eng):
                        for hc in range(2):
                            ps = ps_mm if hc == 0 else ps_o2
                            for kc in range(DC):
                                i = eng.matmul(ps[:, 0:nt],
                                               wq_b[:, kc * 256 + hc * 128: kc * 256 + (hc + 1) * 128],
                                               ln3[:, kc, :], start=(kc == 0), stop=(kc == DC - 1))
                        return i

                    @s.step("scalar", nodrain=True)
                    def _(eng):
                        eng.activation(qs_t[:, 0:nt], ps_mm[:, 0:nt], AF.Silu)
                        return eng.activation(qs_t[:, nt:2 * nt], ps_o2[:, 0:nt], AF.Silu)

                    @s.step("scalar")
                    def _(eng):
                        return eng.activation(eb_t, arel_t, AF.Exp, bias=lnsc_t)

                    @s.step("vector")
                    def _(eng):
                        return eng.tensor_tensor(qt_t, qs_t, eb_t, ALU.mult)

                    @s.step("tensor")
                    def _(eng):
                        for tb in range(nt // 128):
                            ps = (ps_mm if tb < 2 else ps_o2)[:, (tb % 2) * 256:(tb % 2) * 256 + 256]
                            for kc in range(DC):
                                i = eng.matmul(ps,
                                               ln3[:, kc, tb * 128:(tb + 1) * 128],
                                               wi_b[:, kc * 256:(kc + 1) * 256],
                                               start=(kc == 0), stop=(kc == DC - 1))
                        return i

                    @s.step("scalar", nodrain=True)
                    def _(eng):
                        for tb in range(nt // 128):
                            ps = (ps_mm if tb < 2 else ps_o2)[:, (tb % 2) * 256:(tb % 2) * 256 + 256]
                            i = eng.copy(v_t[:, tb * 256:(tb + 1) * 256], ps)
                        return i

                    prev_v1 = None
                    for cc_ in range(nch):
                        c0 = cc_ * CC
                        tb, poff = (c0 // 128), (c0 % 128)

                        @s.step("tensor", wait_abs=prev_v1)
                        def _(eng, c0=c0):
                            for h in range(2):
                                eng.matmul(ps_p[:, h * CC:(h + 1) * CC],
                                           kt_t[:, h * nt + c0:h * nt + c0 + CC],
                                           qt_t[:, h * nt + c0:h * nt + c0 + CC],
                                           start=True, stop=True)
                            for h in range(2):
                                i = eng.transpose(ps_tr[:, h * 128:(h + 1) * 128],
                                                  kh_t[:, h * nt + c0:h * nt + c0 + CC], ident)
                            return i

                        @s.step("vector", nodrain=True)
                        def _(eng, poff=poff):
                            for h in range(2):
                                eng.tensor_tensor(pm_t[poff:poff + CC, h * CC:(h + 1) * CC],
                                                  ps_p[:, h * CC:(h + 1) * CC], msk_t, ALU.mult)
                            return eng.tensor_copy(ktm_t[poff:poff + CC, :], ps_tr[:, :])

                        prev_v1 = s.target

                        @s.step("tensor")
                        def _(eng, c0=c0, tb=tb, poff=poff):
                            for h in range(2):
                                eng.matmul(ps_o[:, h * CC:(h + 1) * CC],
                                           s_bf[:, h * 128:(h + 1) * 128],
                                           qt_t[:, h * nt + c0:h * nt + c0 + CC],
                                           start=True, stop=False)
                                eng.matmul(ps_o[:, h * CC:(h + 1) * CC],
                                           v_t[poff:poff + CC, tb * 256 + h * 128: tb * 256 + (h + 1) * 128],
                                           pm_t[poff:poff + CC, h * CC:(h + 1) * CC],
                                           start=False, stop=True)
                            for h in range(2):
                                i = eng.matmul(ps_u[:, h * 128:(h + 1) * 128],
                                               ktm_t[poff:poff + CC, h * 128:(h + 1) * 128],
                                               v_t[poff:poff + CC, tb * 256 + h * 128: tb * 256 + (h + 1) * 128],
                                               start=True, stop=True)
                            return i

                        @s.step("vector", nodrain=True)
                        def _(eng, cc_=cc_, c0=c0):
                            for h in range(2):
                                eng.tensor_copy(ob_t[:, h * nt + c0:h * nt + c0 + CC],
                                                ps_o[:, h * CC:(h + 1) * CC])
                            for h in range(2):
                                eng.scalar_tensor_tensor(s_f32[:, h * 128:(h + 1) * 128],
                                                         s_f32[:, h * 128:(h + 1) * 128],
                                                         d_t[:, h * nch + cc_:h * nch + cc_ + 1],
                                                         ps_u[:, h * 128:(h + 1) * 128],
                                                         ALU.mult, ALU.add)
                            eng.drain()
                            return eng.tensor_copy(s_bf, s_f32)

                    @s.step("vector", nodrain=True)
                    def _(eng, t0=t0):
                        for h in range(2):
                            eng.tensor_copy(oacc_t[:, h * TPC + t0:h * TPC + t0 + nt],
                                            ob_t[:, h * nt:(h + 1) * nt])
                            i = eng.tensor_copy(qacc_t[:, h * TPC + t0:h * TPC + t0 + nt],
                                                qt_t[:, h * nt:(h + 1) * nt])
                        return i

                @s.dma("sync")
                def _(eng, hp=hp):
                    return [eng.dma_start(out=oloc.rearrange("(c p) t -> p c t", p=128)[:, 2 * hp:2 * hp + 2, :],
                                          in_=oacc_t.rearrange("p (c t) -> p c t", c=2)),
                            eng.dma_start(out=qto.rearrange("(c p) t -> p c t", p=128)[:, 2 * hp:2 * hp + 2, :],
                                          in_=qacc_t.rearrange("p (c t) -> p c t", c=2)),
                            eng.dma_start(out=sloco[:, hp * 256:(hp + 1) * 256], in_=s_f32),
                            eng.dma_start(out=atoto[:, hp * 2:hp * 2 + 2], in_=carry),
                            eng.dma_start(out=bco[:, hp * 2 * NCH:(hp + 1) * 2 * NCH], in_=bc_t)]

        s.fence()
        nc._steps = (s.n_steps, s.n_dmas, s.target, s.gtarget)
    return nc

def emit_moe(s, nc, al, tag, xs_hbm, xo_hbms, ncols, wb_b, aa_b, ba_b, rw_b,
             bb_t, rb_t, e2_t, id8_t, id128_t, onk_t, onb_t):
    widths = []
    rem = ncols
    while rem > 0:
        w = min(512, rem)
        widths.append(w)
        rem -= w
    nt = 512
    xs_t = al.sb(f"{tag}xs", [128, DC * nt], F32)
    ln_t = al.sb(f"{tag}ln", [128, DC * nt], BF16)
    sq_t = al.sb(f"{tag}sq", [128, DC * nt], BF16)
    lntmp_t = al.sb(f"{tag}lntmp", [128, DC * nt], F32)
    ax_t = al.sb(f"{tag}ax", [128, nt], BF16)
    axs_t = al.sb(f"{tag}axs", [128, nt], BF16)
    lg_t = al.sb(f"{tag}lg", [8, nt], BF16)
    lgtm_t = al.sb(f"{tag}lgtm", [128, 4 * 8], F32)
    wtok_t = al.sb(f"{tag}wtok", [128, 4 * 8], BF16)
    wfm_t = al.sb(f"{tag}wfm", [8, nt], BF16)
    srow = al.sb(f"{tag}srow", [1, 4 * nt], F32)
    srow_bf = al.sb(f"{tag}srowb", [1, 2 * nt], BF16)
    stat_t = al.sb(f"{tag}stat", [128, 32], F32)
    hout = al.sb(f"{tag}hout", [128, 4 * nt], F32)
    ps_a = al.ps(f"{tag}psa", [128, nt], F32)
    ps_b = al.ps(f"{tag}psb", [128, nt], F32)
    ps_mm = al.ps(f"{tag}psmm", [128, nt], F32)
    ps_ax = al.ps(f"{tag}psax", [128, nt], F32)
    ps_list = [ps_mm, ps_a, ps_b, ps_ax]
    ps_lg = al.ps(f"{tag}pslg", [8, nt], F32)
    ps_t1 = al.ps(f"{tag}pst1", [128, 8], BF16)
    ps_t2 = al.ps(f"{tag}pst2", [8, 128], BF16)
    ps_stA = ps_a[0:1, :]
    ps_stB = ps_b[0:1, :]

    xs3 = xs_t.rearrange("p (c t) -> p c t", c=DC)
    ln3 = ln_t.rearrange("p (c t) -> p c t", c=DC)
    sq3 = sq_t.rearrange("p (c t) -> p c t", c=DC)
    lntmp3 = lntmp_t.rearrange("p (c t) -> p c t", c=DC)

    col0 = 0
    for wdt in widths:
        blks = []
        k0 = 0
        while k0 < wdt:
            blks.append((k0, min(128, wdt - k0)))
            k0 += min(128, wdt - k0)

        @s.dma("sync")
        def _(eng, col0=col0, wdt=wdt):
            return [eng.dma_start(out=xs3[:, :, 0:wdt],
                                  in_=xs_hbm.rearrange("(c p) t -> p c t", p=128)[:, :, col0:col0 + wdt])]

        emit_ln_fm(s, nc, xs3[:, :, 0:wdt], ln3[:, :, 0:wdt], sq3[:, :, 0:wdt],
                   srow, srow_bf, ps_stA, ps_stB, ps_a, ps_b, lntmp3[:, :, 0:wdt],
                   onk_t, onb_t, wdt)

        @s.step("tensor")
        def _(eng, wdt=wdt):
            for kc in range(DC):
                i = eng.matmul(ps_ax[:, 0:wdt], aa_b[:, kc * 128:(kc + 1) * 128],
                               ln3[:, kc, 0:wdt], start=(kc == 0), stop=(kc == DC - 1))
            return i

        @s.step("vector")
        def _(eng, wdt=wdt):
            return eng.tensor_copy(ax_t[:, 0:wdt], ps_ax[:, 0:wdt])

        @s.step("tensor")
        def _(eng, wdt=wdt):
            for kc in range(DC):
                i = eng.matmul(ps_lg[:, 0:wdt], rw_b[:, kc * 8:(kc + 1) * 8],
                               ln3[:, kc, 0:wdt], start=(kc == 0), stop=(kc == DC - 1))
            return i

        @s.step("vector")
        def _(eng, wdt=wdt):
            return eng.tensor_scalar_add(lg_t[:, 0:wdt], ps_lg[:, 0:wdt], rb_t[:, 0:1])

        for nb, (b0, bw) in enumerate(blks):
            @s.step("tensor")
            def _(eng, b0=b0, bw=bw):
                return eng.transpose(ps_t1[0:bw, :], lg_t[:, b0:b0 + bw], id8_t)

            @s.step("vector")
            def _(eng, nb=nb, bw=bw):
                return eng.tensor_copy(lgtm_t[0:bw, nb * 8:(nb + 1) * 8], ps_t1[0:bw, :])

        for nb, (b0, bw) in enumerate(blks):
            LG = lgtm_t[0:bw, nb * 8:(nb + 1) * 8]
            m1 = stat_t[0:bw, 0:1]; cnt = stat_t[0:bw, 1:2]; m2 = stat_t[0:bw, 2:3]
            kth = stat_t[0:bw, 3:4]; nm1 = stat_t[0:bw, 4:5]; ssum = stat_t[0:bw, 5:6]
            eqb = stat_t[0:bw, 8:16]; rbuf = stat_t[0:bw, 16:24]; ebuf = stat_t[0:bw, 24:32]

            @s.step("vector")
            def _(eng, LG=LG, m1=m1, cnt=cnt, m2=m2, kth=kth, nm1=nm1,
                  ssum=ssum, eqb=eqb, rbuf=rbuf, ebuf=ebuf):
                eng.tensor_reduce(m1, LG, AXX, ALU.max)
                am, bm = bcast(LG, m1)
                eng.tensor_tensor(eqb, am, bm, ALU.subtract)
                eng.tensor_single_scalar(eqb, eqb, 0.0, ALU.is_equal)
                eng.tensor_reduce(cnt, eqb, AXX, ALU.add)
                eng.scalar_tensor_tensor(rbuf, eqb, -1e9, LG, ALU.mult, ALU.add)
                eng.tensor_reduce(m2, rbuf, AXX, ALU.max)
                eng.tensor_single_scalar(cnt, cnt, 2.0, ALU.is_ge)
                eng.tensor_tensor(kth, m1, m2, ALU.subtract)
                eng.tensor_tensor(kth, kth, cnt, ALU.mult)
                eng.tensor_tensor(kth, kth, m2, ALU.add)
                am2, bm2 = bcast(LG, kth)
                eng.tensor_tensor(eqb, am2, bm2, ALU.subtract)
                eng.tensor_single_scalar(eqb, eqb, 0.0, ALU.is_ge)
                return eng.tensor_single_scalar(nm1, m1, -1.0, ALU.mult)

            @s.step("scalar")
            def _(eng, LG=LG, nm1=nm1, ebuf=ebuf):
                return eng.activation(ebuf, LG, AF.Exp, bias=nm1)

            @s.step("vector")
            def _(eng, eqb=eqb, ebuf=ebuf, ssum=ssum, nb=nb, bw=bw):
                eng.tensor_tensor(ebuf, ebuf, eqb, ALU.mult)
                eng.tensor_reduce(ssum, ebuf, AXX, ALU.add)
                eng.reciprocal(ssum, ssum)
                am, bm = bcast(ebuf, ssum)
                return eng.tensor_tensor(wtok_t[0:bw, nb * 8:(nb + 1) * 8], am, bm, ALU.mult)

        for nb, (b0, bw) in enumerate(blks):
            @s.step("tensor")
            def _(eng, nb=nb, b0=b0, bw=bw):
                return eng.transpose(ps_t2[:, 0:bw], wtok_t[0:bw, nb * 8:(nb + 1) * 8], id128_t[0:bw, 0:bw])

            @s.step("vector")
            def _(eng, nb=nb, b0=b0, bw=bw):
                return eng.tensor_copy(wfm_t[:, b0:b0 + bw], ps_t2[:, 0:bw])

        @s.step("tensor")
        def _(eng, wdt=wdt):
            return eng.matmul(ps_ax[0:128, 0:wdt], e2_t, wfm_t[:, 0:wdt], start=True, stop=True)

        @s.step("vector")
        def _(eng, wdt=wdt):
            return eng.tensor_tensor(axs_t[:, 0:wdt], ax_t[:, 0:wdt], ps_ax[0:128, 0:wdt], ALU.mult)

        prev_evac = None
        for g0 in range(0, DC, 4):
            mcs = list(range(g0, g0 + 4))

            @s.step("tensor", wait_abs=prev_evac)
            def _(eng, mcs=mcs, wdt=wdt):
                for idx, mc in enumerate(mcs):
                    ps = ps_list[idx]
                    for kc in range(DC):
                        eng.matmul(ps[:, 0:wdt], wb_b[:, kc * D + mc * 128: kc * D + (mc + 1) * 128],
                                   ln3[:, kc, 0:wdt], start=(kc == 0), stop=False)
                    i = eng.matmul(ps[:, 0:wdt], ba_b[:, mc * 128:(mc + 1) * 128],
                                   axs_t[:, 0:wdt], start=False, stop=True)
                return i

            prev_evac = s.target + 1

            @s.step("vector", nodrain=True)
            def _(eng, mcs=mcs, wdt=wdt):
                for idx, mc in enumerate(mcs):
                    eng.tensor_scalar_add(hout[:, idx * nt:idx * nt + wdt], ps_list[idx][:, 0:wdt],
                                          bb_t[:, mc:mc + 1])
                eng.drain()
                for idx, mc in enumerate(mcs):
                    i = eng.scalar_tensor_tensor(hout[:, idx * nt:idx * nt + wdt],
                                                 hout[:, idx * nt:idx * nt + wdt], 0.5,
                                                 xs3[:, mc, 0:wdt], ALU.mult, ALU.add)
                return i

            @s.dma("sync")
            def _(eng, mcs=mcs, col0=col0, wdt=wdt):
                return [eng.dma_start(out=h[mc * 128:(mc + 1) * 128, col0:col0 + wdt],
                                      in_=hout[:, idx * nt:idx * nt + wdt])
                        for idx, mc in enumerate(mcs) for h in xo_hbms]

        col0 += wdt

def emit_attnout(s, nc, al, tag, x1_hbm, o_hbm, xo_hbms, ncols, wo_b,
                 onk_t, onb_t):
    widths = []
    rem = ncols
    while rem > 0:
        w = min(512, rem)
        widths.append(w)
        rem -= w
    nt = 512
    x1_t = al.sb(f"{tag}x1", [128, DC * nt], F32)
    o_t = al.sb(f"{tag}o", [128, DC * nt], BF16)
    on_t = al.sb(f"{tag}on", [128, DC * nt], BF16)
    sq_t = al.sb(f"{tag}sq", [128, DC * nt], BF16)
    srow = al.sb(f"{tag}srow", [1, 4 * nt], F32)
    srow_bf = al.sb(f"{tag}srowb", [1, 2 * nt], BF16)
    hout = al.sb(f"{tag}hout", [128, 4 * nt], F32)
    ps_a = al.ps(f"{tag}psa", [128, nt], F32)
    ps_mm = al.ps(f"{tag}psmm", [128, nt], F32)
    ps_b = al.ps(f"{tag}psb", [128, nt], F32)
    ps_c = al.ps(f"{tag}psc", [128, nt], F32)
    ps_list = [ps_mm, ps_a, ps_b, ps_c]
    x13 = x1_t.rearrange("p (c t) -> p c t", c=DC)
    o3 = o_t.rearrange("p (c t) -> p c t", c=DC)
    on3 = on_t.rearrange("p (c t) -> p c t", c=DC)
    sq3 = sq_t.rearrange("p (c t) -> p c t", c=DC)
    RSTD = srow[0:1, 0:nt]
    RSTD_BF = srow_bf[0:1, 0:nt]

    col0 = 0
    for wdt in widths:
        @s.dma("sync")
        def _(eng, col0=col0, wdt=wdt):
            return [eng.dma_start(out=x13[:, :, 0:wdt],
                                  in_=x1_hbm.rearrange("(c p) t -> p c t", p=128)[:, :, col0:col0 + wdt]),
                    eng.dma_start(out=o3[:, :, 0:wdt],
                                  in_=o_hbm.rearrange("(c p) t -> p c t", p=128)[:, :, col0:col0 + wdt])]

        @s.step("scalar")
        def _(eng, wdt=wdt):
            return eng.activation(sq3[:, :, 0:wdt], o3[:, :, 0:wdt], AF.Square)

        @s.step("tensor")
        def _(eng, wdt=wdt):
            for c in range(DC):
                i = eng.matmul(ps_a[0:1, 0:wdt], onk_t, sq3[:, c, 0:wdt],
                               start=(c == 0), stop=(c == DC - 1))
            return i

        @s.step("vector")
        def _(eng, wdt=wdt):
            eng.tensor_single_scalar(RSTD[0:1, 0:wdt], ps_a[0:1, 0:wdt], 1.0 / D, ALU.mult)
            eng.drain()
            return eng.tensor_single_scalar(RSTD[0:1, 0:wdt], RSTD[0:1, 0:wdt], LN_EPS, ALU.add)

        @s.step("scalar")
        def _(eng, wdt=wdt):
            return eng.activation(RSTD[0:1, 0:wdt], RSTD[0:1, 0:wdt], AF.Sqrt)

        @s.step("vector")
        def _(eng, wdt=wdt):
            eng.reciprocal(RSTD[0:1, 0:wdt], RSTD[0:1, 0:wdt])
            eng.drain()
            return eng.tensor_copy(RSTD_BF[0:1, 0:wdt], RSTD[0:1, 0:wdt])

        @s.step("tensor")
        def _(eng, wdt=wdt):
            return eng.matmul(ps_a[:, 0:wdt], onb_t, RSTD_BF[0:1, 0:wdt], start=True, stop=True)

        @s.step("vector")
        def _(eng, wdt=wdt):
            am, bm = bcast(o3[:, :, 0:wdt], ps_a[:, 0:wdt].rearrange("p (o t) -> p o t", o=1))
            return eng.tensor_tensor(on3[:, :, 0:wdt], am, bm, ALU.mult)

        def post_o(mcs, wdt=wdt, col0=col0):
            @s.step("vector", nodrain=True)
            def _(eng):
                for idx, mc in enumerate(mcs):
                    i = eng.tensor_tensor(hout[:, idx * nt:idx * nt + wdt], x13[:, mc, 0:wdt],
                                          ps_list[idx][:, 0:wdt], ALU.add)
                return i

            @s.dma("sync")
            def _(eng):
                return [eng.dma_start(out=h[mc * 128:(mc + 1) * 128, col0:col0 + wdt],
                                      in_=hout[:, idx * nt:idx * nt + wdt])
                        for idx, mc in enumerate(mcs) for h in xo_hbms]

        emit_proj_grouped(s, wo_b, on3, ps_list, wdt, DC, post_o)

        col0 += wdt


def emit_l1prep(s, nc, al, tag, xs_hbm, Wq, Wk, Wv, Wb, cq_t, ck_t, cv_t,
                lw_t, mw_t, hmask_t, onk_t, onb_t, qf, kf, vf, beta, loc, mid):
    ntok = TPC
    ntot = ntok + HLO
    xstage = al.sb(f"{tag}xstage", [128, DC * 512], F32)
    ln_t = al.sb(f"{tag}ln", [128, DC * ntot], BF16)
    sq_t = al.sb(f"{tag}sq", [128, DC * 512], BF16)
    lntmp_t = al.sb(f"{tag}lntmp", [128, DC * 512], F32)
    acc0 = al.sb(f"{tag}acc0", [128, ntot], F32)
    acc1 = al.sb(f"{tag}acc1", [128, ntot], F32)
    oout = al.sb(f"{tag}oout", [128, ntok], BF16)
    w_b = al.sb(f"{tag}w_b", [128, DC * D], BF16)
    wb_b = al.sb(f"{tag}wb_b", [128, DC * 4], BF16)
    wstage = al.sb(f"{tag}wstage", [128, D], F32)
    srow = al.sb(f"{tag}srow", [1, 4 * 512], F32)
    srow_bf = al.sb(f"{tag}srowb", [1, 2 * 512], BF16)
    bt_t = al.sb(f"{tag}bt", [4, ntok], F32)
    zbuf = al.sb(f"{tag}zbuf", [128, DC * ntot], BF16)
    vflat = al.sb(f"{tag}vflat", [128, DC * ntot], BF16)
    ps_a = al.ps(f"{tag}psa", [128, 512], F32)
    ps_b = al.ps(f"{tag}psb", [128, 512], F32)
    ps_mm = al.ps(f"{tag}psmm", [128, 512], F32)
    ps_c = al.ps(f"{tag}psc", [128, 512], F32)
    ps_bt = al.ps(f"{tag}psbt", [4, 512], F32)
    ps_list = [ps_mm, ps_a, ps_b, ps_c]
    ps_stA = ps_a[0:1, :]
    ps_stB = ps_b[0:1, :]

    xs3 = xstage.rearrange("p (c t) -> p c t", c=DC)
    ln3 = ln_t.rearrange("p (c t) -> p c t", c=DC)
    sq3 = sq_t.rearrange("p (c t) -> p c t", c=DC)
    lntmp3 = lntmp_t.rearrange("p (c t) -> p c t", c=DC)
    zb3 = zbuf.rearrange("p (c t) -> p c t", c=DC)
    vf3 = vflat.rearrange("p (c t) -> p c t", c=DC)

    for u in range((ntot + 511) // 512):
        c0, c1 = u * 512, min((u + 1) * 512, ntot)

        @s.dma("sync")
        def _(eng, c0=c0, c1=c1):
            return [eng.dma_start(out=xs3[:, :, 0:c1 - c0],
                                  in_=xs_hbm.rearrange("(c p) t -> p c t", p=128)[:, :, c0:c1])]

        emit_ln_fm(s, nc, xs3[:, :, 0:c1 - c0], ln3[:, :, c0:c1],
                   sq3[:, :, 0:c1 - c0], srow, srow_bf, ps_stA, ps_stB,
                   ps_a, ps_b, lntmp3[:, :, 0:c1 - c0], onk_t, onb_t, c1 - c0)

    emit_weight_load(s, wstage, wb_b, Wb, DC, 4)
    for u in range(ntok // 512):
        @s.step("tensor")
        def _(eng, u=u):
            for kc in range(DC):
                i = eng.matmul(ps_bt[:, :], wb_b[:, kc * 4:(kc + 1) * 4],
                               ln3[:, kc, HLO + u * 512:HLO + (u + 1) * 512],
                               start=(kc == 0), stop=(kc == DC - 1))
            return i

        @s.step("scalar")
        def _(eng, u=u):
            return eng.activation(bt_t[:, u * 512:(u + 1) * 512], ps_bt[:, :], AF.Sigmoid)

    @s.dma("sync")
    def _(eng):
        return [eng.dma_start(out=beta[:, :], in_=bt_t)]

    def proj(W_hbm):
        emit_weight_load(s, wstage, w_b, W_hbm, DC, D)
        for u in range((ntot + 511) // 512):
            c0, c1 = u * 512, min((u + 1) * 512, ntot)

            def post_z(mcs, c0=c0, c1=c1):
                @s.step("scalar", nodrain=True)
                def _(eng):
                    for idx, mc in enumerate(mcs):
                        i = eng.copy(zb3[:, mc, c0:c1], ps_list[idx][:, 0:c1 - c0])
                    return i

            emit_proj_grouped(s, w_b, ln3[:, :, c0:c1], ps_list, c1 - c0, DC, post_z)

        @s.step("vector")
        def _(eng):
            am, bm = bcast(zb3[:, :, 0:HLO], hmask_t.rearrange("p (o t) -> p o t", o=1))
            return eng.tensor_tensor(zb3[:, :, 0:HLO], am, bm, ALU.mult)

    def conv4_silu(cw_t, out_fm3, keep_halo):
        lo = 3 if keep_halo else HLO
        nw = ntot - 3 if keep_halo else ntok
        for mc in range(DC):
            @s.step("vector")
            def _(eng, mc=mc, lo=lo, nw=nw):
                eng.tensor_scalar_mul(acc0[:, 0:nw], zb3[:, mc, lo - 3:lo - 3 + nw], cw_t[:, mc * 4 + 0:mc * 4 + 1])
                eng.drain()
                eng.scalar_tensor_tensor(acc1[:, 0:nw], zb3[:, mc, lo - 2:lo - 2 + nw], cw_t[:, mc * 4 + 1:mc * 4 + 2], acc0[:, 0:nw], ALU.mult, ALU.add)
                eng.drain()
                eng.scalar_tensor_tensor(acc0[:, 0:nw], zb3[:, mc, lo - 1:lo - 1 + nw], cw_t[:, mc * 4 + 2:mc * 4 + 3], acc1[:, 0:nw], ALU.mult, ALU.add)
                eng.drain()
                return eng.scalar_tensor_tensor(acc1[:, 0:nw], zb3[:, mc, lo:lo + nw], cw_t[:, mc * 4 + 3:mc * 4 + 4], acc0[:, 0:nw], ALU.mult, ALU.add)

            @s.step("scalar")
            def _(eng, mc=mc, lo=lo, nw=nw):
                off = lo - HLO
                return eng.activation(out_fm3[:, mc, HLO + off:HLO + off + nw], acc1[:, 0:nw], AF.Silu)

    proj(Wv)
    conv4_silu(cv_t, vf3, keep_halo=True)
    @s.dma("sync")
    def _(eng):
        return [eng.dma_start(out=vf.rearrange("(c p) t -> p c t", p=128), in_=vf3[:, :, HLO:ntot])]

    cvo3 = zb3[:, :, 0:ntok]
    for w_t, taps, out_hbm in ((lw_t, 7, loc), (mw_t, 25, mid)):
        for mc in range(DC):
            @s.step("vector")
            def _(eng, mc=mc, w_t=w_t, taps=taps):
                a, b = acc0[:, 0:ntok], acc1[:, 0:ntok]
                eng.tensor_scalar_mul(a, vf3[:, mc, HLO - (taps - 1):HLO - (taps - 1) + ntok],
                                      w_t[:, mc * taps + 0:mc * taps + 1])
                for tp in range(1, taps):
                    eng.drain()
                    sh = HLO - (taps - 1) + tp
                    eng.scalar_tensor_tensor(b, vf3[:, mc, sh:sh + ntok],
                                             w_t[:, mc * taps + tp:mc * taps + tp + 1],
                                             a, ALU.mult, ALU.add)
                    a, b = b, a
                eng.drain()
                return eng.tensor_copy(cvo3[:, mc, :], a)

        @s.dma("sync")
        def _(eng, out_hbm=out_hbm):
            return [eng.dma_start(out=out_hbm.rearrange("(c p) t -> p c t", p=128), in_=cvo3)]

    proj(Wq)
    conv4_silu(cq_t, vf3, keep_halo=False)

    @s.dma("sync")
    def _(eng):
        return [eng.dma_start(out=qf.rearrange("(c p) t -> p c t", p=128), in_=vf3[:, :, HLO:ntot])]

    proj(Wk)
    conv4_silu(ck_t, vf3, keep_halo=False)

    @s.dma("sync")
    def _(eng):
        return [eng.dma_start(out=kf.rearrange("(c p) t -> p c t", p=128), in_=vf3[:, :, HLO:ntot])]

OH = 128
EXT1 = OH + TPC
EXT2 = EXT1 - 1
X4OFF = EXT2 - TPC


def build_L2():
    nc = bass.Bass("TRN2", target_bir_lowering=False, debug=False)
    x1e = nc.dram_tensor("x1e", [D, EXT1], F32, kind="ExternalInput").ap()
    oloc = nc.dram_tensor("oloc", [D, TPC], BF16, kind="ExternalInput").ap()
    qto = nc.dram_tensor("qto", [D, TPC], BF16, kind="ExternalInput").ap()
    bco = nc.dram_tensor("bco", [128, 8 * NCH], F32, kind="ExternalInput").ap()
    oh = nc.dram_tensor("oh", [D, OH], BF16, kind="ExternalInput").ap()
    qth = nc.dram_tensor("qth", [D, OH], BF16, kind="ExternalInput").ap()
    bch = nc.dram_tensor("bch", [128, 16], F32, kind="ExternalInput").ap()
    SL = nc.dram_tensor("SL", [128, 4 * 1024], F32, kind="ExternalInput").ap()
    AT = nc.dram_tensor("AT", [128, 4 * 8], F32, kind="ExternalInput").ap()
    am0 = nc.dram_tensor("am0", [128, 3], F32, kind="ExternalInput").ap()
    am1 = nc.dram_tensor("am1", [128, 3], F32, kind="ExternalInput").ap()
    hmask = nc.dram_tensor("hmask", [128, HLO], F32, kind="ExternalInput").ap()
    Wo = nc.dram_tensor("Wo", [D, D], F32, kind="ExternalInput").ap()
    gw = nc.dram_tensor("gw", [128, DC], F32, kind="ExternalInput").ap()
    Wr = nc.dram_tensor("Wr", [D, D], F32, kind="ExternalInput").ap()
    Wk = nc.dram_tensor("Wk", [D, D], F32, kind="ExternalInput").ap()
    Wv = nc.dram_tensor("Wv", [D, D], F32, kind="ExternalInput").ap()
    mur = nc.dram_tensor("mur", [128, DC], F32, kind="ExternalInput").ap()
    muk = nc.dram_tensor("muk", [128, DC], F32, kind="ExternalInput").ap()
    bwT = nc.dram_tensor("bwT", [D, D], F32, kind="ExternalInput").ap()
    bb = nc.dram_tensor("bb", [128, DC], F32, kind="ExternalInput").ap()
    Aall = nc.dram_tensor("Aall", [D, 128], F32, kind="ExternalInput").ap()
    Ball = nc.dram_tensor("Ball", [128, D], F32, kind="ExternalInput").ap()
    rwT = nc.dram_tensor("rwT", [D, 8], F32, kind="ExternalInput").ap()
    rb = nc.dram_tensor("rb", [8, 1], F32, kind="ExternalInput").ap()
    E2 = nc.dram_tensor("E2", [8, 128], BF16, kind="ExternalInput").ap()
    id8 = nc.dram_tensor("id8", [8, 8], BF16, kind="ExternalInput").ap()
    id128 = nc.dram_tensor("id128", [128, 128], BF16, kind="ExternalInput").ap()
    Wq1 = nc.dram_tensor("Wq1", [D, D], F32, kind="ExternalInput").ap()
    Wk1 = nc.dram_tensor("Wk1", [D, D], F32, kind="ExternalInput").ap()
    Wv1 = nc.dram_tensor("Wv1", [D, D], F32, kind="ExternalInput").ap()
    Wb1 = nc.dram_tensor("Wb1", [D, 4], F32, kind="ExternalInput").ap()
    cq = nc.dram_tensor("cq", [128, DC * 4], F32, kind="ExternalInput").ap()
    ck = nc.dram_tensor("ck", [128, DC * 4], F32, kind="ExternalInput").ap()
    cv = nc.dram_tensor("cv", [128, DC * 4], F32, kind="ExternalInput").ap()
    lw = nc.dram_tensor("lw", [128, DC * 7], F32, kind="ExternalInput").ap()
    mw = nc.dram_tensor("mw", [128, DC * 25], F32, kind="ExternalInput").ap()
    msl = nc.dram_tensor("msl", [DCC, DCC], F32, kind="ExternalInput").ap()
    msu = nc.dram_tensor("msu", [DCC, DCC], F32, kind="ExternalInput").ap()
    mui = nc.dram_tensor("mui", [DCC, DCC], F32, kind="ExternalInput").ap()
    idin = nc.dram_tensor("ident", [128, 128], BF16, kind="ExternalInput").ap()
    onk = nc.dram_tensor("ones_k", [128, 1], BF16, kind="ExternalInput").ap()
    onb = nc.dram_tensor("ones_b", [1, 128], BF16, kind="ExternalInput").ap()
    x4o = nc.dram_tensor("x4o", [D, TPC], F32, kind="ExternalOutput").ap()
    vdo = nc.dram_tensor("vdo", [D, TPC], BF16, kind="ExternalOutput").ap()
    loco = nc.dram_tensor("loco", [D, TPC], BF16, kind="ExternalOutput").ap()
    mido = nc.dram_tensor("mido", [D, TPC], BF16, kind="ExternalOutput").ap()
    qho = nc.dram_tensor("qho", [D, TPC], BF16, kind="ExternalOutput").ap()
    wfo = nc.dram_tensor("wfo", [D, TPC], BF16, kind="ExternalOutput").ap()
    utm = nc.dram_tensor("utm", [TPC, 1024], BF16, kind="ExternalOutput").ap()
    khtmo = nc.dram_tensor("khtmo", [TPC, 1024], BF16, kind="ExternalOutput").ap()
    atno = nc.dram_tensor("atno", [DCC, 64 * DCC], BF16, kind="ExternalOutput").ap()
    Mo = nc.dram_tensor("Mo", [128, 4 * 512], F32, kind="ExternalOutput").ap()
    Co = nc.dram_tensor("Co", [128, 4 * 512], F32, kind="ExternalOutput").ap()
    betao = nc.dram_tensor("betao", [4, TPC], F32, kind="ExternalOutput").ap()
    o_int = nc.dram_tensor("o_int", [D, EXT1], BF16).ap()
    x2int = nc.dram_tensor("x2int", [D, EXT1], F32).ap()
    x3int = nc.dram_tensor("x3int", [D, EXT2], F32).ap()
    x4int = nc.dram_tensor("x4int", [D, EXT2], F32).ap()
    qfint = nc.dram_tensor("qfint", [D, TPC], BF16).ap()
    kfint = nc.dram_tensor("kfint", [D, TPC], BF16).ap()
    vfint = nc.dram_tensor("vfint", [D, TPC], BF16).ap()

    with ExitStack() as st:
        al = Alloc(nc, st)
        tk = al.sem("tk")
        gk = al.sem("gk")
        block = al.block()
        s = S(nc, block, tk, gk)

        onk_t = al.sb("onk_t", [128, 1], BF16)
        onb_t = al.sb("onb_t", [1, 128], BF16)
        ident = al.sb("ident_t", [128, 128], BF16)

        @s.dma("sync")
        def _(eng):
            return [eng.dma_start(out=onk_t, in_=onk[:, :]),
                    eng.dma_start(out=onb_t, in_=onb[:, :]),
                    eng.dma_start(out=ident, in_=idin[:, :])]

        with ExitStack() as st1:
            al1 = Alloc(nc, st1)
            sin = al1.sb("C_sin", [128, 1024], F32)
            sin_bf = al1.sb("C_sinb", [128, 1024], BF16)
            spv = al1.sb("C_spv", [128, 1024], F32)
            spv_bf = al1.sb("C_spvb", [128, 1024], BF16)
            slt = al1.sb("C_sl", [128, 1024], F32)
            att = al1.sb("C_at", [128, 4 * 8], F32)
            am0_t = al1.sb("C_am0", [128, 3], F32)
            am1_t = al1.sb("C_am1", [128, 3], F32)
            dec = al1.sb("C_dec", [128, 8], F32)
            qtt = al1.sb("C_qt", [128, 8 * 512], BF16)
            olc = al1.sb("C_ol", [128, 8 * 512], BF16)
            tmpq = al1.sb("C_tmpq", [128, 8 * 512], BF16)
            bc_t = al1.sb("C_bc", [128, 8 * NCH], F32)
            bch_t = al1.sb("C_bch", [128, 16], F32)
            obuf = al1.sb("C_obuf", [128, 8 * 512], BF16)
            ps_c = al1.ps("C_ps", [128, 512], F32)
            ps_c2 = al1.ps("C_ps2", [128, 512], F32)
            ps_c3 = al1.ps("C_ps3", [128, 512], F32)
            ps_c4 = al1.ps("C_ps4", [128, 512], F32)
            ps_cl = [ps_c, ps_c2, ps_c3, ps_c4]

            @s.dma("sync")
            def _(eng):
                return [eng.dma_start(out=att, in_=AT[:, :]),
                        eng.dma_start(out=am0_t, in_=am0[:, :]),
                        eng.dma_start(out=am1_t, in_=am1[:, :]),
                        eng.dma_start(out=bc_t, in_=bco[:, :]),
                        eng.dma_start(out=bch_t, in_=bch[:, :])]

            for Sx, Sb, amx in ((sin, sin_bf, am0_t), (spv, spv_bf, am1_t)):
                @s.step("vector")
                def _(eng, Sx=Sx):
                    return eng.memset(Sx, 0.0)

                for p in range(3):
                    @s.dma("sync")
                    def _(eng, p=p):
                        return [eng.dma_start(out=slt, in_=SL[:, p * 1024:(p + 1) * 1024])]

                    @s.step("vector")
                    def _(eng, p=p, amx=amx):
                        return eng.tensor_scalar_mul(dec, att[:, p * 8:(p + 1) * 8], amx[:, p:p + 1])

                    @s.step("scalar")
                    def _(eng):
                        return eng.activation(dec, dec, AF.Exp)

                    @s.step("vector")
                    def _(eng, p=p, Sx=Sx, amx=amx):
                        for h in range(8):
                            eng.tensor_scalar_mul(Sx[:, h * 128:(h + 1) * 128],
                                                  Sx[:, h * 128:(h + 1) * 128], dec[:, h:h + 1])
                            eng.drain()
                        eng.scalar_tensor_tensor(slt, slt, amx[:, p:p + 1], Sx, ALU.mult, ALU.add)
                        eng.drain()
                        return eng.tensor_copy(Sx, slt)

                @s.step("vector")
                def _(eng, Sx=Sx, Sb=Sb):
                    return eng.tensor_copy(Sb, Sx)

            @s.dma("sync")
            def _(eng):
                return [eng.dma_start(out=qtt[:, 0:8 * 128].rearrange("p (c t) -> p c t", c=8),
                                      in_=qth.rearrange("(c p) t -> p c t", p=128)),
                        eng.dma_start(out=olc[:, 0:8 * 128].rearrange("p (c t) -> p c t", c=8),
                                      in_=oh.rearrange("(c p) t -> p c t", p=128))]

            @s.step("vector", nodrain=True)
            def _(eng):
                for h in range(8):
                    for hc in range(2):
                        i = eng.tensor_scalar_mul(
                            tmpq[:, h * 128 + hc * CC: h * 128 + (hc + 1) * CC],
                            qtt[:, h * 128 + hc * CC: h * 128 + (hc + 1) * CC],
                            bch_t[:, h * 2 + hc:h * 2 + hc + 1])
                return i

            @s.step("tensor")
            def _(eng):
                for h in range(8):
                    ps = ps_cl[h // 4]
                    i = eng.matmul(ps[:, (h % 4) * 128:(h % 4 + 1) * 128],
                                   spv_bf[:, h * 128:(h + 1) * 128],
                                   tmpq[:, h * 128:(h + 1) * 128], start=True, stop=True)
                return i

            @s.step("vector", nodrain=True)
            def _(eng):
                for h in range(8):
                    ps = ps_cl[h // 4]
                    i = eng.tensor_tensor(obuf[:, h * 128:(h + 1) * 128],
                                          olc[:, h * 128:(h + 1) * 128],
                                          ps[:, (h % 4) * 128:(h % 4 + 1) * 128], ALU.add)
                return i

            @s.dma("sync")
            def _(eng):
                return [eng.dma_start(out=o_int.rearrange("(c p) t -> p c t", p=128)[:, :, 0:OH],
                                      in_=obuf[:, 0:8 * 128].rearrange("p (c t) -> p c t", c=8))]

            for grp in range(4):
                g0 = grp * 512

                @s.dma("sync")
                def _(eng, g0=g0):
                    return [eng.dma_start(out=qtt.rearrange("p (c t) -> p c t", c=8),
                                          in_=qto.rearrange("(c p) t -> p c t", p=128)[:, :, g0:g0 + 512]),
                            eng.dma_start(out=olc.rearrange("p (c t) -> p c t", c=8),
                                          in_=oloc.rearrange("(c p) t -> p c t", p=128)[:, :, g0:g0 + 512])]

                @s.step("vector")
                def _(eng, grp=grp):
                    for h in range(8):
                        qv = qtt[:, h * 512:(h + 1) * 512].rearrange("p (c t) -> p c t", t=CC)
                        bv = bc_t[:, h * NCH + grp * 8:h * NCH + (grp + 1) * 8].rearrange("p (c o) -> p c o", o=1)
                        amv, bmv = bcast(qv, bv)
                        i = eng.tensor_tensor(tmpq[:, h * 512:(h + 1) * 512].rearrange("p (c t) -> p c t", t=CC),
                                              amv, bmv, ALU.mult)
                        eng.drain()
                    return i

                for hg in range(2):
                    @s.step("tensor")
                    def _(eng, hg=hg):
                        for idx in range(4):
                            h = hg * 4 + idx
                            i = eng.matmul(ps_cl[idx][:, :], sin_bf[:, h * 128:(h + 1) * 128],
                                           tmpq[:, h * 512:(h + 1) * 512], start=True, stop=True)
                        return i

                    @s.step("vector", nodrain=True)
                    def _(eng, hg=hg):
                        for idx in range(4):
                            h = hg * 4 + idx
                            i = eng.tensor_tensor(obuf[:, h * 512:(h + 1) * 512],
                                                  olc[:, h * 512:(h + 1) * 512], ps_cl[idx][:, :], ALU.add)
                        return i

                @s.dma("sync")
                def _(eng, g0=g0):
                    return [eng.dma_start(out=o_int.rearrange("(c p) t -> p c t", p=128)[:, :, OH + g0:OH + g0 + 512],
                                          in_=obuf.rearrange("p (c t) -> p c t", c=8))]

        with ExitStack() as st2:
            al2 = Alloc(nc, st2)
            wo_b = al2.sb("AO_wo", [128, DC * D], BF16)
            wstage = al2.sb("AO_wst", [128, D], F32)
            gw_t = al2.sb("AO_gw", [128, DC], F32)

            @s.dma("sync")
            def _(eng):
                return [eng.dma_start(out=gw_t, in_=gw[:, :])]

            @s.dma("gpsimd")
            def _(eng):
                return [eng.dma_start(out=wo_b.rearrange("p (c m) -> p c m", c=DC),
                                      in_=Wo.rearrange("(c p) m -> p c m", p=128))]

            @s.step("vector", nodrain=True)
            def _(eng):
                for c in range(DC):
                    i = eng.tensor_scalar_mul(wo_b[:, c * D:(c + 1) * D],
                                              wo_b[:, c * D:(c + 1) * D], gw_t[:, c:c + 1])
                return i

            emit_attnout(s, nc, al2, "AO_", x1e, o_int, [x2int], EXT1, wo_b, onk_t, onb_t)

        with ExitStack() as st3:
            al3 = Alloc(nc, st3)
            wstage = al3.sb("F2_wst", [128, D], F32)
            wr_b = al3.sb("F2_wr", [128, DC * D], BF16)
            wk_b = al3.sb("F2_wk", [128, DC * D], BF16)
            wv_b = al3.sb("F2_wv", [128, DC * D], BF16)
            mur_t = al3.sb("F2_mur", [128, DC], F32)
            muk_t = al3.sb("F2_muk", [128, DC], F32)

            @s.dma("sync")
            def _(eng):
                return [eng.dma_start(out=mur_t, in_=mur[:, :]),
                        eng.dma_start(out=muk_t, in_=muk[:, :])]

            emit_weight_load(s, wstage, wr_b, Wr, DC, D)
            emit_weight_load(s, wstage, wk_b, Wk, DC, D)
            emit_weight_load(s, wstage, wv_b, Wv, DC, D)
            emit_ffn(s, nc, al3, "F2_", x2int, [x3int], EXT2, 1,
                     wr_b, wk_b, wv_b, mur_t, muk_t, onk_t, onb_t)

        with ExitStack() as st4:
            al4 = Alloc(nc, st4)
            wstage = al4.sb("M1_wst", [128, D], F32)
            wb_b = al4.sb("M1_wb", [128, DC * D], BF16)
            aa_b = al4.sb("M1_aa", [128, DC * 128], BF16)
            ba_b = al4.sb("M1_ba", [128, D], BF16)
            rw_b = al4.sb("M1_rw", [128, DC * 8], BF16)
            bb_t = al4.sb("M1_bb", [128, DC], F32)
            rb_t = al4.sb("M1_rb", [8, 1], F32)
            e2_t = al4.sb("M1_e2", [8, 128], BF16)
            id8_t = al4.sb("M1_id8", [8, 8], BF16)

            @s.dma("sync")
            def _(eng):
                return [eng.dma_start(out=bb_t, in_=bb[:, :]),
                        eng.dma_start(out=rb_t, in_=rb[:, :]),
                        eng.dma_start(out=e2_t, in_=E2[:, :]),
                        eng.dma_start(out=id8_t, in_=id8[:, :])]

            emit_weight_load(s, wstage, wb_b, bwT, DC, D)
            emit_weight_load(s, wstage, aa_b, Aall, DC, 128)
            emit_weight_load(s, wstage, ba_b, Ball, 1, D)
            emit_weight_load(s, wstage, rw_b, rwT, DC, 8)
            emit_moe(s, nc, al4, "M1_", x3int, [x4int], EXT2, wb_b, aa_b, ba_b,
                     rw_b, bb_t, rb_t, e2_t, id8_t, ident, onk_t, onb_t)

            @s.dma("sync")
            def _(eng):
                return [eng.dma_start(out=x4o.rearrange("d t -> d t"),
                                      in_=x4int[:, X4OFF:X4OFF + TPC])]

        with ExitStack() as st5:
            al5 = Alloc(nc, st5)
            cq_t = al5.sb("P_cq", [128, DC * 4], F32)
            ck_t = al5.sb("P_ck", [128, DC * 4], F32)
            cv_t = al5.sb("P_cv", [128, DC * 4], F32)
            lw_t = al5.sb("P_lw", [128, DC * 7], F32)
            mw_t = al5.sb("P_mw", [128, DC * 25], F32)
            hmask_t = al5.sb("P_hm", [128, HLO], F32)

            @s.dma("sync")
            def _(eng):
                return [eng.dma_start(out=cq_t, in_=cq[:, :]), eng.dma_start(out=ck_t, in_=ck[:, :]),
                        eng.dma_start(out=cv_t, in_=cv[:, :]), eng.dma_start(out=lw_t, in_=lw[:, :]),
                        eng.dma_start(out=mw_t, in_=mw[:, :]), eng.dma_start(out=hmask_t, in_=hmask[:, :])]

            emit_l1prep(s, nc, al5, "P_", x4int[:, X4OFF - HLO:EXT2], Wq1, Wk1, Wv1, Wb1,
                        cq_t, ck_t, cv_t, lw_t, mw_t, hmask_t, onk_t, onb_t,
                        qfint, kfint, vfint, betao, loco, mido)

            @s.dma("sync")
            def _(eng):
                return [eng.dma_start(out=vdo.rearrange("d t -> d t"), in_=vfint[:, :])]

        emit_delta_phase1(s, nc, al, qfint, kfint, vfint, betao, msl, msu, mui,
                          ident, onk_t, onb_t, qho, wfo, utm, khtmo, atno, Mo, Co)

        s.fence()
        nc._steps = (s.n_steps, s.n_dmas, s.target, s.gtarget)
    return nc

def emit_delta_phase1(s, nc, al, qfint, kfint, vfint, betao, msl, msu, mui,
                      ident, onk_t, onb_t, qho, wfo, utm, khtmo, atno, Mo, Co):
    nt = TPC
    with ExitStack() as st6:
        al6 = Alloc(nc, st6)
        qf_t = al6.sb("D_qf", [128, 2 * nt], BF16)
        kf_t = al6.sb("D_kf", [128, 2 * nt], BF16)
        vf_t = al6.sb("D_vf", [128, 2 * nt], BF16)
        qh_t = al6.sb("D_qh", [128, 2 * nt], BF16)
        kh_t = al6.sb("D_kh", [128, 2 * nt], BF16)
        kb_t = al6.sb("D_kb", [128, 2 * nt], BF16)
        vb_t = al6.sb("D_vb", [128, 2 * nt], BF16)
        sq_t = al6.sb("D_sq", [128, 2 * nt], BF16)
        rq_t = al6.sb("D_rq", [1, 2 * nt], F32)
        rqb_t = al6.sb("D_rqb", [1, 2 * nt], BF16)
        btf_t = al6.sb("D_btf", [1, nt], F32)
        bt_t = al6.sb("D_bt", [1, nt], BF16)
        s_f32 = al6.sb("D_sf", [128, 512], F32)
        s_bf = al6.sb("D_sb", [128, 512], BF16)
        p_f32 = al6.sb("D_pf", [128, 512], F32)
        p_bf = al6.sb("D_pb", [128, 512], BF16)
        pA = al6.sb("D_pA", [128, 7 * DCC], F32)
        pG = al6.sb("D_pG", [128, 7 * DCC], F32)
        atn_t = al6.sb("D_atn", [128, DNCH * DCC], BF16)
        uall_t = al6.sb("D_uall", [128, DNCH * 256], BF16)
        kall_t = al6.sb("D_kall", [128, DNCH * 256], BF16)
        wall_t = al6.sb("D_wall", [128, DNCH * 2 * DCC], BF16)
        z0 = al6.sb("D_z0", [128, 512], F32)
        z1 = al6.sb("D_z1", [128, 512], F32)
        
        khc_t = al6.sb("D_khc", [64, 256], BF16)
        ut_t = al6.sb("D_ut", [128, 256], BF16)
        ub_t = al6.sb("D_ub", [64, 256], BF16)
        xb_t = al6.sb("D_xb", [128, 256], BF16)
        wf_t = al6.sb("D_wf", [128, 2 * DCC], F32)
        wfb_t = al6.sb("D_wfb", [128, 2 * CC], BF16)
        msl_t = al6.sb("D_msl", [DCC, DCC], F32)
        msu_t = al6.sb("D_msu", [DCC, DCC], F32)
        mui_t = al6.sb("D_mui", [DCC, DCC], F32)
        psA = al6.ps("D_psA", [128, 512], F32)
        psB = al6.ps("D_psB", [128, 2 * DCC], F32)
        psC = al6.ps("D_psC", [128, DCC], F32)
        psD = al6.ps("D_psD", [128, 512], F32)
        psE = al6.ps("D_psE", [128, 2 * DCC], F32)
        psF = al6.ps("D_psF", [128, 256], F32)
        psX = al6.ps("D_psX", [128, 256], F32)
        psTR = al6.ps("D_psTR", [128, 768], BF16)

        @s.dma("sync")
        def _(eng):
            return [eng.dma_start(out=msl_t, in_=msl[:, :]), eng.dma_start(out=msu_t, in_=msu[:, :]),
                    eng.dma_start(out=mui_t, in_=mui[:, :])]

        qf3 = qf_t.rearrange("p (c t) -> p c t", c=2)
        kf3 = kf_t.rearrange("p (c t) -> p c t", c=2)
        vf3 = vf_t.rearrange("p (c t) -> p c t", c=2)
        qh3 = qh_t.rearrange("p (c t) -> p c t", c=2)
        kh3 = kh_t.rearrange("p (c t) -> p c t", c=2)
        kb3 = kb_t.rearrange("p (c t) -> p c t", c=2)
        vb3 = vb_t.rearrange("p (c t) -> p c t", c=2)
        sq3 = sq_t.rearrange("p (c t) -> p c t", c=2)

        def l2norm_fm(src3, src_flat, dst3, roff):
            @s.step("scalar")
            def _(eng):
                return eng.activation(sq_t, src_flat, AF.Square)

            for u in range(nt // 512):
                @s.step("tensor")
                def _(eng, u=u):
                    for c in range(2):
                        i = eng.matmul(psA[0:1, 0:512], onk_t, sq3[:, c, u * 512:(u + 1) * 512],
                                       start=(c == 0), stop=(c == 1))
                    return i

                @s.step("vector")
                def _(eng, u=u):
                    return eng.tensor_single_scalar(rq_t[0:1, roff + u * 512:roff + (u + 1) * 512],
                                                    psA[0:1, 0:512], 1e-12, ALU.add)

            @s.step("scalar")
            def _(eng):
                return eng.activation(rq_t[0:1, roff:roff + nt], rq_t[0:1, roff:roff + nt], AF.Sqrt)

            @s.step("vector")
            def _(eng):
                eng.reciprocal(rq_t[0:1, roff:roff + nt], rq_t[0:1, roff:roff + nt])
                eng.drain()
                return eng.tensor_copy(rqb_t[0:1, roff:roff + nt], rq_t[0:1, roff:roff + nt])

            for u in range(nt // 512):
                @s.step("tensor")
                def _(eng, u=u):
                    return eng.matmul(psA[:, 0:512], onb_t,
                                      rqb_t[0:1, roff + u * 512:roff + (u + 1) * 512],
                                      start=True, stop=True)

                @s.step("vector")
                def _(eng, u=u):
                    am, bm = bcast(src3[:, :, u * 512:(u + 1) * 512],
                                   psA[:, 0:512].rearrange("p (o t) -> p o t", o=1))
                    return eng.tensor_tensor(dst3[:, :, u * 512:(u + 1) * 512], am, bm, ALU.mult)

        for h in range(4):
            @s.dma("sync")
            def _(eng, h=h):
                return [
                    eng.dma_start(out=qf3, in_=qfint.rearrange("(c p) t -> p c t", p=128)[:, 2 * h:2 * h + 2, :]),
                    eng.dma_start(out=kf3, in_=kfint.rearrange("(c p) t -> p c t", p=128)[:, 2 * h:2 * h + 2, :]),
                    eng.dma_start(out=vf3, in_=vfint.rearrange("(c p) t -> p c t", p=128)[:, 2 * h:2 * h + 2, :]),
                    eng.dma_start(out=btf_t, in_=betao[h:h + 1, :]),
                ]

            @s.step("vector")
            def _(eng):
                return eng.tensor_copy(bt_t, btf_t)

            l2norm_fm(qf3, qf_t, qh3, 0)
            l2norm_fm(kf3, kf_t, kh3, nt)

            @s.dma("sync")
            def _(eng, h=h):
                return [eng.dma_start(out=qho[h * 256:h * 256 + 128, :], in_=qh_t[:, 0:nt]),
                        eng.dma_start(out=qho[h * 256 + 128:h * 256 + 256, :], in_=qh_t[:, nt:2 * nt])]

            for u in range(nt // 512):
                @s.step("tensor")
                def _(eng, u=u):
                    return eng.matmul(psA[:, 0:512], onb_t, bt_t[0:1, u * 512:(u + 1) * 512],
                                      start=True, stop=True)

                @s.step("vector")
                def _(eng, u=u):
                    am, bm = bcast(kh3[:, :, u * 512:(u + 1) * 512],
                                   psA[:, 0:512].rearrange("p (o t) -> p o t", o=1))
                    eng.tensor_tensor(kb3[:, :, u * 512:(u + 1) * 512], am, bm, ALU.mult)
                    eng.drain()
                    am2, bm2 = bcast(vf3[:, :, u * 512:(u + 1) * 512],
                                     psA[:, 0:512].rearrange("p (o t) -> p o t", o=1))
                    return eng.tensor_tensor(vb3[:, :, u * 512:(u + 1) * 512], am2, bm2, ALU.mult)

            @s.step("vector")
            def _(eng):
                eng.memset(s_f32, 0.0)
                eng.memset(s_bf, 0.0)
                eng.memset(p_f32, 0.0)
                return eng.memset(p_bf, 0.0)

            @s.step("vector")
            def _(eng):
                i = eng.tensor_tensor(p_f32[:, 0:128], ident[:, 0:128], ident[:, 0:128], ALU.mult)
                eng.drain()
                i = eng.tensor_tensor(p_f32[:, 256 + 128:512], ident[:, 0:128], ident[:, 0:128], ALU.mult)
                eng.drain()
                return eng.tensor_copy(p_bf, p_f32)

            chain_done = None
            for cc_ in range(DNCH):
                c0 = cc_ * DCC

                @s.step("tensor", wait_abs=chain_done)
                def _(eng, c0=c0):
                    for gi, src3 in enumerate((kh3, vb3, kb3)):
                        for c in range(2):
                            i = eng.transpose(psTR[:, gi * 256 + c * 128:gi * 256 + (c + 1) * 128],
                                              src3[:, c, c0:c0 + DCC], ident)
                    return i

                khc_t = kall_t[:, cc_ * 256:(cc_ + 1) * 256]
                ub_t = uall_t[:, cc_ * 256:(cc_ + 1) * 256]
                wfb_t = wall_t[:, cc_ * 2 * DCC:(cc_ + 1) * 2 * DCC]
                atnc_t = atn_t[:, cc_ * DCC:(cc_ + 1) * DCC]

                @s.step("vector", nodrain=True)
                def _(eng, c0=c0, khc_t=khc_t):
                    eng.tensor_copy(khc_t, psTR[:, 0:256])
                    eng.tensor_copy(z0[:, 0:256], psTR[:, 256:512])
                    eng.tensor_copy(z0[:, 256:512], psTR[:, 512:768])
                    eng.tensor_copy(wf_t[:, 0:DCC], kb3[:, 0, c0:c0 + DCC])
                    return eng.tensor_copy(wf_t[:, DCC:2 * DCC], kb3[:, 1, c0:c0 + DCC])

                @s.step("tensor")
                def _(eng, c0=c0):
                    for c in range(2):
                        eng.matmul(psB[:, 0:DCC], kb3[:, c, c0:c0 + DCC], kh3[:, c, c0:c0 + DCC],
                                   start=(c == 0), stop=(c == 1))
                    for c in range(2):
                        eng.matmul(psB[:, DCC:2 * DCC], kh3[:, c, c0:c0 + DCC], kb3[:, c, c0:c0 + DCC],
                                   start=(c == 0), stop=(c == 1))
                    for c in range(2):
                        i = eng.matmul(psC[:, :], kh3[:, c, c0:c0 + DCC], qh3[:, c, c0:c0 + DCC],
                                       start=(c == 0), stop=(c == 1))
                    return i

                @s.step("vector")
                def _(eng):
                    eng.tensor_tensor(pA[:, 0:DCC], psB[:, 0:DCC], msl_t, ALU.mult)
                    eng.drain()
                    eng.tensor_tensor(pG[:, 0:DCC], psB[:, DCC:2 * DCC], msu_t, ALU.mult)
                    eng.drain()
                    return eng.tensor_tensor(atnc_t, psC[:, :], mui_t, ALU.mult)

                zs = [z0, z1]
                for l in range(7):
                    zc, zn = zs[l % 2], zs[(l + 1) % 2]

                    @s.step("tensor")
                    def _(eng, l=l, zc=zc):
                        eng.matmul(psE[:, 0:DCC], zc[:, 256:384], pG[:, l * DCC:(l + 1) * DCC],
                                   start=True, stop=True)
                        eng.matmul(psE[:, DCC:2 * DCC], zc[:, 384:512], pG[:, l * DCC:(l + 1) * DCC],
                                   start=True, stop=True)
                        i = eng.matmul(psA[:, :], pG[:, l * DCC:(l + 1) * DCC], zc[:, :],
                                       start=True, stop=True)
                        if l < 6:
                            eng.matmul(psD[:, 0:DCC], pG[:, l * DCC:(l + 1) * DCC],
                                       pA[:, l * DCC:(l + 1) * DCC], start=True, stop=True)
                            i = eng.matmul(psD[:, DCC:2 * DCC], pA[:, l * DCC:(l + 1) * DCC],
                                           pG[:, l * DCC:(l + 1) * DCC], start=True, stop=True)
                        return i

                    @s.step("vector", nodrain=True)
                    def _(eng, l=l, zc=zc, zn=zn):
                        eng.tensor_tensor(wf_t, wf_t, psE[:, :], ALU.add)
                        i = eng.tensor_tensor(zn, zc, psA[:, :], ALU.add)
                        if l < 6:
                            eng.tensor_copy(pA[:, (l + 1) * DCC:(l + 2) * DCC], psD[:, 0:DCC])
                            i = eng.tensor_copy(pG[:, (l + 1) * DCC:(l + 2) * DCC], psD[:, DCC:2 * DCC])
                        return i

                zfin = zs[1]

                @s.step("vector", nodrain=True)
                def _(eng, zfin=zfin, wfb_t=wfb_t, ub_t=ub_t):
                    eng.tensor_copy(wfb_t, wf_t)
                    return eng.tensor_copy(ub_t, zfin[:, 0:256])

                chain_done = s.target

                @s.step("tensor")
                def _(eng):
                    for c in range(2):
                        eng.matmul(psF[:, :], wfb_t[:, c * DCC:(c + 1) * DCC],
                                   s_bf[:, c * 256:(c + 1) * 256], start=(c == 0), stop=(c == 1))
                    for c in range(2):
                        i = eng.matmul(psX[:, :], wfb_t[:, c * DCC:(c + 1) * DCC],
                                       p_bf[:, c * 256:(c + 1) * 256], start=(c == 0), stop=(c == 1))
                    return i

                @s.step("vector", nodrain=True)
                def _(eng):
                    eng.tensor_tensor(ut_t, ub_t, psF[:, :], ALU.subtract)
                    return eng.tensor_copy(xb_t, psX[:, :])

                @s.step("tensor")
                def _(eng):
                    eng.matmul(psA[:, 0:256], khc_t[:, 0:128], ut_t, start=True, stop=True)
                    eng.matmul(psA[:, 256:512], khc_t[:, 128:256], ut_t, start=True, stop=True)
                    eng.matmul(psD[:, 0:256], khc_t[:, 0:128], xb_t, start=True, stop=True)
                    return eng.matmul(psD[:, 256:512], khc_t[:, 128:256], xb_t, start=True, stop=True)

                @s.step("vector", nodrain=True)
                def _(eng):
                    eng.tensor_tensor(s_f32, s_f32, psA[:, :], ALU.add)
                    eng.tensor_tensor(p_f32, p_f32, psD[:, :], ALU.subtract)
                    eng.drain()
                    eng.tensor_copy(s_bf, s_f32)
                    return eng.tensor_copy(p_bf, p_f32)

            @s.dma("sync")
            def _(eng, h=h):
                return [
                    eng.dma_start(out=atno[:, h * DNCH * DCC:(h + 1) * DNCH * DCC], in_=atn_t),
                    eng.dma_start(out=utm.rearrange("(c p) f -> p c f", p=DCC)[:, :, h * 256:(h + 1) * 256],
                                  in_=uall_t.rearrange("p (c f) -> p c f", f=256)),
                    eng.dma_start(out=khtmo.rearrange("(c p) f -> p c f", p=DCC)[:, :, h * 256:(h + 1) * 256],
                                  in_=kall_t.rearrange("p (c f) -> p c f", f=256)),
                    eng.dma_start(out=wfo[h * 256:h * 256 + 128, :].rearrange("p (c t) -> p c t", t=DCC),
                                  in_=wall_t.rearrange("p (c two t) -> p (c two) t", two=2, t=DCC)[:, 0::2, :]),
                    eng.dma_start(out=wfo[h * 256 + 128:h * 256 + 256, :].rearrange("p (c t) -> p c t", t=DCC),
                                  in_=wall_t.rearrange("p (c two t) -> p (c two) t", two=2, t=DCC)[:, 1::2, :]),
                    eng.dma_start(out=Mo[:, h * 512:(h + 1) * 512], in_=p_f32),
                    eng.dma_start(out=Co[:, h * 512:(h + 1) * 512], in_=s_f32),
                ]

R1K = 1080
R1N = 2176
NCH1 = 17


def build_L3():
    nc = bass.Bass("TRN2", target_bir_lowering=False, debug=False)
    x4i = nc.dram_tensor("x4i", [D, TPC], F32, kind="ExternalInput").ap()
    loci = nc.dram_tensor("loci", [D, TPC], BF16, kind="ExternalInput").ap()
    midi = nc.dram_tensor("midi", [D, TPC], BF16, kind="ExternalInput").ap()
    vdi = nc.dram_tensor("vdi", [D, TPC], BF16, kind="ExternalInput").ap()
    qhi = nc.dram_tensor("qhi", [D, TPC], BF16, kind="ExternalInput").ap()
    wfi = nc.dram_tensor("wfi", [D, TPC], BF16, kind="ExternalInput").ap()
    utmi = nc.dram_tensor("utmi", [TPC, 1024], BF16, kind="ExternalInput").ap()
    khtmi = nc.dram_tensor("khtmi", [TPC, 1024], BF16, kind="ExternalInput").ap()
    atni = nc.dram_tensor("atni", [DCC, 64 * DCC], BF16, kind="ExternalInput").ap()
    Mcat = nc.dram_tensor("Mcat", [128, 4 * 2048], F32, kind="ExternalInput").ap()
    Ccat = nc.dram_tensor("Ccat", [128, 4 * 2048], F32, kind="ExternalInput").ap()
    am0 = nc.dram_tensor("am0", [128, 3], F32, kind="ExternalInput").ap()
    Ifm = nc.dram_tensor("Ifm", [128, 512], F32, kind="ExternalInput").ap()
    rw1 = nc.dram_tensor("rw1", [R1K, R1N], F32, kind="ExternalInput").ap()
    rb1 = nc.dram_tensor("rb1", [128, NCH1], F32, kind="ExternalInput").ap()
    rw2 = nc.dram_tensor("rw2", [R1N, 16], F32, kind="ExternalInput").ap()
    rb2 = nc.dram_tensor("rb2", [16, 1], F32, kind="ExternalInput").ap()
    mixw = nc.dram_tensor("mixw", [128, DC], F32, kind="ExternalInput").ap()
    onw = nc.dram_tensor("onw", [128, DC], F32, kind="ExternalInput").ap()
    Wo1 = nc.dram_tensor("Wo1", [D, D], F32, kind="ExternalInput").ap()
    id16 = nc.dram_tensor("id16", [16, 16], BF16, kind="ExternalInput").ap()
    hsel = nc.dram_tensor("hsel", [128, DC * 4], BF16, kind="ExternalInput").ap()
    sel4 = nc.dram_tensor("sel4", [4, 4 * 128], BF16, kind="ExternalInput").ap()
    esel = nc.dram_tensor("esel", [16, 16 * 128], BF16, kind="ExternalInput").ap()
    bwT = nc.dram_tensor("bwT", [D, D], F32, kind="ExternalInput").ap()
    bb = nc.dram_tensor("bb", [128, DC], F32, kind="ExternalInput").ap()
    Aall = nc.dram_tensor("Aall", [D, 128], F32, kind="ExternalInput").ap()
    Ball = nc.dram_tensor("Ball", [128, D], F32, kind="ExternalInput").ap()
    rwT = nc.dram_tensor("rwT", [D, 8], F32, kind="ExternalInput").ap()
    rb = nc.dram_tensor("rb", [8, 1], F32, kind="ExternalInput").ap()
    E2 = nc.dram_tensor("E2", [8, 128], BF16, kind="ExternalInput").ap()
    id8 = nc.dram_tensor("id8", [8, 8], BF16, kind="ExternalInput").ap()
    id128 = nc.dram_tensor("id128", [128, 128], BF16, kind="ExternalInput").ap()
    onk = nc.dram_tensor("ones_k", [128, 1], BF16, kind="ExternalInput").ap()
    onb = nc.dram_tensor("ones_b", [1, 128], BF16, kind="ExternalInput").ap()
    x6o = nc.dram_tensor("x6o", [D, TPC], F32, kind="ExternalOutput").ap()
    dlt = nc.dram_tensor("dlt", [D, TPC], BF16).ap()
    x5int = nc.dram_tensor("x5int", [D, TPC], F32).ap()
    fscratch = nc.dram_tensor("fscratch", [8, 512], F32).ap()

    with ExitStack() as st:
        al = Alloc(nc, st)
        tk = al.sem("tk")
        gk = al.sem("gk")
        block = al.block()
        s = S(nc, block, tk, gk)

        onk_t = al.sb("onk_t", [128, 1], BF16)
        onb_t = al.sb("onb_t", [1, 128], BF16)
        id128_t = al.sb("id128_t", [128, 128], BF16)

        @s.dma("sync")
        def _(eng):
            return [eng.dma_start(out=onk_t, in_=onk[:, :]),
                    eng.dma_start(out=onb_t, in_=onb[:, :]),
                    eng.dma_start(out=id128_t, in_=id128[:, :])]

        with ExitStack() as st1:
            al1 = Alloc(nc, st1)
            sin4 = al1.sb("D2_sin", [128, 4 * 512], F32)
            sin4b = al1.sb("D2_sinb", [128, 4 * 512], BF16)
            mp_t = al1.sb("D2_mp", [128, 2048], F32)
            mpb_t = al1.sb("D2_mpb", [128, 2048], BF16)
            mpc_t = al1.sb("D2_mpc", [128, 2048], BF16)
            cp_t = al1.sb("D2_cp", [128, 2048], F32)
            am_t = al1.sb("D2_am", [128, 3], F32)
            ifm_t = al1.sb("D2_ifm", [128, 512], F32)
            s_f32 = al1.sb("D2_sf", [128, 512], F32)
            s_bf = al1.sb("D2_sb", [128, 512], BF16)
            qh_t = al1.sb("D2_qh", [128, 2 * 512], BF16)
            wf_t = al1.sb("D2_wf", [128, 2 * 512], BF16)
            u_t = al1.sb("D2_u", [128, 4 * 256], BF16)
            kh_t = al1.sb("D2_kh", [128, 4 * 256], BF16)
            atn_t = al1.sb("D2_atn", [128, 4 * DCC], BF16)
            ut_t = al1.sb("D2_ut", [128, 256], BF16)
            ob_t = al1.sb("D2_ob", [128, 256], BF16)
            dlt_t = al1.sb("D2_dlt", [128, 2 * 512], BF16)
            psF = al1.ps("D2_psF", [128, 256], F32)
            psO = al1.ps("D2_psO", [128, 256], F32)
            psS = al1.ps("D2_psS", [128, 512], F32)
            psTR = al1.ps("D2_psTR", [128, 256], BF16)

            @s.dma("sync")
            def _(eng):
                return [eng.dma_start(out=am_t, in_=am0[:, :]),
                        eng.dma_start(out=ifm_t, in_=Ifm[:, :])]

            @s.step("vector")
            def _(eng):
                return eng.memset(sin4, 0.0)

            for p in range(3):
                @s.dma("sync")
                def _(eng, p=p):
                    return [eng.dma_start(out=mp_t, in_=Mcat[:, p * 2048:(p + 1) * 2048]),
                            eng.dma_start(out=cp_t, in_=Ccat[:, p * 2048:(p + 1) * 2048])]

                @s.step("vector")
                def _(eng, p=p):
                    for h in range(4):
                        sl = slice(h * 512, (h + 1) * 512)
                        eng.tensor_tensor(mp_t[:, sl], mp_t[:, sl], ifm_t, ALU.subtract)
                        eng.drain()
                        eng.tensor_scalar_mul(mp_t[:, sl], mp_t[:, sl], am_t[:, p:p + 1])
                        eng.drain()
                        eng.tensor_tensor(mp_t[:, sl], mp_t[:, sl], ifm_t, ALU.add)
                        eng.drain()
                    return eng.tensor_copy(mpc_t, mp_t)

                for h in range(4):
                    for bi in range(2):
                        @s.step("tensor")
                        def _(eng, h=h, bi=bi):
                            for bj in range(2):
                                i = eng.transpose(
                                    psTR[:, bj * 128:(bj + 1) * 128],
                                    mpc_t[:, h * 512 + bi * 256 + bj * 128: h * 512 + bi * 256 + (bj + 1) * 128],
                                    id128_t)
                            return i

                        @s.step("vector", nodrain=True)
                        def _(eng, h=h, bi=bi):
                            for bj in range(2):
                                i = eng.tensor_copy(
                                    mpb_t[:, h * 512 + bj * 256 + bi * 128: h * 512 + bj * 256 + (bi + 1) * 128],
                                    psTR[:, bj * 128:(bj + 1) * 128])
                            return i

                @s.step("vector")
                def _(eng):
                    return eng.tensor_copy(sin4b, sin4)

                for h in range(4):
                    @s.step("tensor")
                    def _(eng, h=h):
                        for r in range(2):
                            for kb_ in range(2):
                                i = eng.matmul(psS[:, r * 256:(r + 1) * 256],
                                               mpb_t[:, h * 512 + kb_ * 256 + r * 128: h * 512 + kb_ * 256 + (r + 1) * 128],
                                               sin4b[:, h * 512 + kb_ * 256:h * 512 + (kb_ + 1) * 256],
                                               start=(kb_ == 0), stop=(kb_ == 1))
                        return i

                    @s.step("vector")
                    def _(eng, h=h, p=p):
                        sl = slice(h * 512, (h + 1) * 512)
                        eng.scalar_tensor_tensor(sin4[:, sl], cp_t[:, sl], am_t[:, p:p + 1],
                                                 psS[:, :], ALU.mult, ALU.add)
                        eng.drain()
                        return i if False else eng.tensor_copy(sin4b[:, sl], sin4[:, sl])

            for h in range(4):
                @s.step("vector")
                def _(eng, h=h):
                    eng.tensor_copy(s_f32, sin4[:, h * 512:(h + 1) * 512])
                    eng.drain()
                    return eng.tensor_copy(s_bf, s_f32)

                for j in range(4):
                    t0 = j * 512

                    @s.dma("sync")
                    def _(eng, h=h, j=j, t0=t0):
                        return [
                            eng.dma_start(out=qh_t.rearrange("p (c t) -> p c t", c=2),
                                          in_=qhi.rearrange("(c p) t -> p c t", p=128)[:, 2 * h:2 * h + 2, t0:t0 + 512]),
                            eng.dma_start(out=wf_t.rearrange("p (c t) -> p c t", c=2),
                                          in_=wfi.rearrange("(c p) t -> p c t", p=128)[:, 2 * h:2 * h + 2, t0:t0 + 512]),
                            eng.dma_start(out=u_t.rearrange("p (b f) -> p b f", f=256),
                                          in_=utmi.rearrange("(b p) f -> p b f", p=DCC)[:, j * 4:(j + 1) * 4, h * 256:(h + 1) * 256]),
                            eng.dma_start(out=kh_t.rearrange("p (b f) -> p b f", f=256),
                                          in_=khtmi.rearrange("(b p) f -> p b f", p=DCC)[:, j * 4:(j + 1) * 4, h * 256:(h + 1) * 256]),
                            eng.dma_start(out=atn_t,
                                          in_=atni[:, (h * DNCH + j * 4) * DCC:(h * DNCH + (j + 1) * 4) * DCC]),
                        ]

                    for c in range(4):
                        c0 = c * DCC

                        @s.step("tensor")
                        def _(eng, c0=c0):
                            for kb_ in range(2):
                                i = eng.matmul(psF[:, :], wf_t[:, kb_ * 512 + c0:kb_ * 512 + c0 + DCC],
                                               s_bf[:, kb_ * 256:(kb_ + 1) * 256],
                                               start=(kb_ == 0), stop=(kb_ == 1))
                            return i

                        @s.step("vector")
                        def _(eng, c=c):
                            return eng.tensor_tensor(
                                ut_t, u_t.rearrange("p (b f) -> p b f", f=256)[:, c, :],
                                psF[:, :], ALU.subtract)

                        @s.step("tensor")
                        def _(eng, c0=c0, c=c):
                            for kb_ in range(2):
                                eng.matmul(psO[:, :], qh_t[:, kb_ * 512 + c0:kb_ * 512 + c0 + DCC],
                                           s_bf[:, kb_ * 256:(kb_ + 1) * 256],
                                           start=(kb_ == 0), stop=False)
                            eng.matmul(psO[:, :], atn_t[:, c * DCC:(c + 1) * DCC], ut_t,
                                       start=False, stop=True)
                            kh3 = kh_t.rearrange("p (b f) -> p b f", f=256)
                            eng.matmul(psS[:, 0:256], kh3[:, c, 0:128], ut_t, start=True, stop=True)
                            return eng.matmul(psS[:, 256:512], kh3[:, c, 128:256], ut_t, start=True, stop=True)

                        @s.step("scalar")
                        def _(eng):
                            return eng.copy(ob_t, psO[:, :])

                        @s.step("tensor")
                        def _(eng):
                            eng.transpose(psTR[:, 0:128], ob_t[:, 0:128], id128_t)
                            return eng.transpose(psTR[:, 128:256], ob_t[:, 128:256], id128_t)

                        @s.step("vector", nodrain=True)
                        def _(eng, c0=c0):
                            eng.tensor_copy(dlt_t[:, c0:c0 + DCC], psTR[:, 0:128])
                            eng.tensor_copy(dlt_t[:, 512 + c0:512 + c0 + DCC], psTR[:, 128:256])
                            eng.tensor_tensor(s_f32, s_f32, psS[:, :], ALU.add)
                            eng.drain()
                            return eng.tensor_copy(s_bf, s_f32)

                    @s.dma("sync")
                    def _(eng, h=h, t0=t0):
                        return [eng.dma_start(out=dlt[h * 256:h * 256 + 128, t0:t0 + 512], in_=dlt_t[:, 0:512]),
                                eng.dma_start(out=dlt[h * 256 + 128:h * 256 + 256, t0:t0 + 512], in_=dlt_t[:, 512:1024])]

        with ExitStack() as st2:
            al2 = Alloc(nc, st2)
            emit_l1out(s, nc, al2, x4i, loci, midi, dlt, vdi, rw1, rb1, rw2, rb2,
                       mixw, onw, Wo1, id16, id128_t, esel, hsel, sel4, onk_t, onb_t,
                       fscratch, [x5int])

        with ExitStack() as st3:
            al3 = Alloc(nc, st3)
            wstage = al3.sb("M2_wst", [128, D], F32)
            wb_b = al3.sb("M2_wb", [128, DC * D], BF16)
            aa_b = al3.sb("M2_aa", [128, DC * 128], BF16)
            ba_b = al3.sb("M2_ba", [128, D], BF16)
            rw_b = al3.sb("M2_rw", [128, DC * 8], BF16)
            bb_t = al3.sb("M2_bb", [128, DC], F32)
            rb_t = al3.sb("M2_rb", [8, 1], F32)
            e2_t = al3.sb("M2_e2", [8, 128], BF16)
            id8_t = al3.sb("M2_id8", [8, 8], BF16)

            @s.dma("sync")
            def _(eng):
                return [eng.dma_start(out=bb_t, in_=bb[:, :]),
                        eng.dma_start(out=rb_t, in_=rb[:, :]),
                        eng.dma_start(out=e2_t, in_=E2[:, :]),
                        eng.dma_start(out=id8_t, in_=id8[:, :])]

            emit_weight_load(s, wstage, wb_b, bwT, DC, D)
            emit_weight_load(s, wstage, aa_b, Aall, DC, 128)
            emit_weight_load(s, wstage, ba_b, Ball, 1, D)
            emit_weight_load(s, wstage, rw_b, rwT, DC, 8)
            emit_moe(s, nc, al3, "M2_", x5int, [x6o], TPC, wb_b, aa_b, ba_b,
                     rw_b, bb_t, rb_t, e2_t, id8_t, id128_t, onk_t, onb_t)

        s.fence()
        nc._steps = (s.n_steps, s.n_dmas, s.target, s.gtarget)
    return nc

def emit_l1out(s, nc, al, xs, loc, mid, dlt, vd, rw1, rb1, rw2, rb2, mixw, onw,
               Wo, id16, id128_t, esel, hsel, sel4, onk_t, onb_t, fscratch, xo_hbms):
    ntok, nt, nsub = TPC, 512, TPC // 512
    xs_t = al.sb("O_xs", [128, DC * nt], F32)
    ln_t = al.sb("O_ln", [128, DC * nt], BF16)
    sq_t = al.sb("O_sq", [128, DC * nt], BF16)
    lntmp_t = al.sb("O_lntmp", [128, DC * nt], F32)
    br_t = [al.sb(f"O_br{b}", [128, DC * nt], BF16) for b in range(4)]
    h1_t = al.sb("O_h1", [128, NCH1 * nt], BF16)
    stageA_t = al.sb("O_stageA", [4, nt], F32)
    stageB_t = al.sb("O_stageB", [4, nt], F32)
    hsel_t = al.sb("O_hsel", [128, DC * 4], BF16)
    sel4_t = al.sb("O_sel4", [4, 4 * 128], BF16)
    hw2f_t = al.sb("O_hw2f", [128, DC * 4], F32)
    hw2_t = al.sb("O_hw2", [128, DC * 4], BF16)
    dn_t = al.sb("O_dn", [8, nt], F32)
    dnb_t = al.sb("O_dnb", [4, nt], BF16)
    feats_t = al.sb("O_feats", [64, nt], BF16)
    lgm_t = al.sb("O_lgm", [16, nt], BF16)
    lgtm_t = al.sb("O_lgtm", [128, 4 * 16], F32)
    ptok_t = al.sb("O_ptok", [128, 4 * 16], BF16)
    pfm_t = al.sb("O_pfm", [16, nt], BF16)
    stat_t = al.sb("O_stat", [128, 32], F32)
    on_t = al.sb("O_on", [128, DC * nt], BF16)
    hout = al.sb("O_hout", [128, 2 * nt], F32)
    srow = al.sb("O_srow", [1, 4 * nt], F32)
    srow_bf = al.sb("O_srowb", [1, 2 * nt], BF16)
    nrow2 = al.sb("O_nrow2", [1, 2 * nt], F32)
    nrow_bf = al.sb("O_nrowb", [1, nt], BF16)
    w1_b = al.sb("O_w1", [128, DC * R1N], BF16)
    w56_b = al.sb("O_w56", [64, R1N], BF16)
    w2_b = al.sb("O_w2", [128, NCH1 * 16], BF16)
    wo_b = al.sb("O_wo", [128, DC * D], BF16)
    wstage = al.sb("O_wst", [128, R1N // 2], F32)
    rb1_t = al.sb("O_rb1", [128, NCH1], F32)
    rb2_t = al.sb("O_rb2", [16, 1], F32)
    mixw_t = al.sb("O_mixw", [128, DC], F32)
    onw_t = al.sb("O_onw", [128, DC], F32)
    id16_t = al.sb("O_id16", [16, 16], BF16)
    esel_t = al.sb("O_esel", [16, 16 * 128], BF16)
    ps_a = al.ps("O_psa", [128, nt], F32)
    ps_b = al.ps("O_psb", [128, nt], F32)
    ps_mm = al.ps("O_psmm", [128, nt], F32)
    ps_row = al.ps("O_psrow", [8, nt], F32)
    ps_rowB = al.ps("O_psrowB", [8, nt], F32)
    ps_lg = al.ps("O_pslg", [16, nt], F32)
    ps_t1 = al.ps("O_pst1", [128, 64], BF16)
    ps_t2 = al.ps("O_pst2", [16, 512], BF16)
    ps_stA = ps_a[0:1, :]
    ps_stB = ps_b[0:1, :]

    @s.dma("sync")
    def _(eng):
        return [eng.dma_start(out=rb1_t, in_=rb1[:, :]), eng.dma_start(out=rb2_t, in_=rb2[:, :]),
                eng.dma_start(out=mixw_t, in_=mixw[:, :]), eng.dma_start(out=onw_t, in_=onw[:, :]),
                eng.dma_start(out=id16_t, in_=id16[:, :]),
                eng.dma_start(out=hsel_t, in_=hsel[:, :]),
                eng.dma_start(out=sel4_t, in_=sel4[:, :]),
                eng.dma_start(out=esel_t, in_=esel[:, :])]

    @s.step("vector")
    def _(eng):
        for c in range(DC):
            eng.tensor_scalar_mul(hw2f_t[:, c * 4:(c + 1) * 4], hsel_t[:, c * 4:(c + 1) * 4],
                                  mixw_t[:, c:c + 1])
        eng.drain()
        for c in range(DC):
            eng.tensor_scalar_mul(hw2f_t[:, c * 4:(c + 1) * 4], hw2f_t[:, c * 4:(c + 1) * 4],
                                  mixw_t[:, c:c + 1])
        eng.drain()
        return eng.tensor_copy(hw2_t, hw2f_t)

    @s.dma("gpsimd")
    def _(eng):
        return [eng.dma_start(out=w1_b.rearrange("p (c m) -> p c m", c=DC),
                              in_=rw1[0:1024, :].rearrange("(c p) m -> p c m", p=128)),
                eng.dma_start(out=w56_b[0:56, :], in_=rw1[1024:1080, :]),
                eng.dma_start(out=w2_b.rearrange("p (c f) -> p c f", f=16),
                              in_=rw2.rearrange("(c p) f -> p c f", p=128)),
                eng.dma_start(out=wo_b.rearrange("p (c m) -> p c m", c=DC),
                              in_=Wo.rearrange("(c p) m -> p c m", p=128))]

    @s.step("vector", nodrain=True)
    def _(eng):
        for kc in range(DC):
            i = eng.tensor_scalar_mul(wo_b[:, kc * D:(kc + 1) * D],
                                      wo_b[:, kc * D:(kc + 1) * D], onw_t[:, kc:kc + 1])
        return i

    xs3 = xs_t.rearrange("p (c t) -> p c t", c=DC)
    ln3 = ln_t.rearrange("p (c t) -> p c t", c=DC)
    sq3 = sq_t.rearrange("p (c t) -> p c t", c=DC)
    lntmp3 = lntmp_t.rearrange("p (c t) -> p c t", c=DC)
    br3 = [b.rearrange("p (c t) -> p c t", c=DC) for b in br_t]
    prod_t = on_t
    prod3 = prod_t.rearrange("p (c t) -> p c t", c=DC)
    on3 = on_t.rearrange("p (c t) -> p c t", c=DC)
    mixd_t = lntmp_t[:, 0:2 * nt]
    m2w_t = lntmp_t[:, 2 * nt:4 * nt]
    S1 = srow[0:1, 0:nt]
    S2 = srow[0:1, nt:2 * nt]
    DEN = srow[0:1, 2 * nt:3 * nt]
    RR = srow[0:1, 3 * nt:4 * nt]

    for j in range(nsub):
        t0 = j * nt

        @s.dma("sync")
        def _(eng, t0=t0):
            ins = [eng.dma_start(out=xs3, in_=xs.rearrange("(c p) t -> p c t", p=128)[:, :, t0:t0 + nt])]
            for bi, hbm in enumerate((loc, mid, dlt, vd)):
                ins.append(eng.dma_start(out=br3[bi],
                                         in_=hbm.rearrange("(c p) t -> p c t", p=128)[:, :, t0:t0 + nt]))
            return ins

        emit_ln_fm(s, nc, xs3, ln3, sq3, srow, srow_bf, ps_stA, ps_stB,
                   ps_a, ps_b, lntmp3, onk_t, onb_t, nt)

        def flush(first_row):
            @s.dma("sync")
            def _(eng):
                return [eng.dma_start(out=fscratch[0:4, :], in_=stageA_t),
                        eng.dma_start(out=fscratch[4:8, :], in_=stageB_t)]

            @s.dma("gpsimd")
            def _(eng, first_row=first_row):
                return [eng.dma_start(out=feats_t[first_row:first_row + 8, :], in_=fscratch[:, :])]

        for bi in range(4):
            @s.step("scalar")
            def _(eng, bi=bi):
                return eng.activation(sq_t, br_t[bi], AF.Square)

            @s.step("tensor")
            def _(eng, bi=bi):
                for c in range(DC):
                    eng.matmul(ps_row[0:4, :], hsel_t[:, c * 4:(c + 1) * 4], br3[bi][:, c, :],
                               start=(c == 0), stop=(c == DC - 1))
                for c in range(DC):
                    i = eng.matmul(ps_rowB[0:4, :], hsel_t[:, c * 4:(c + 1) * 4], sq3[:, c, :],
                                   start=(c == 0), stop=(c == DC - 1))
                return i

            @s.step("vector", nodrain=True)
            def _(eng, bi=bi):
                eng.tensor_single_scalar(stageA_t, ps_row[0:4, :], 1.0 / 256.0, ALU.mult)
                eng.tensor_single_scalar(stageB_t, ps_rowB[0:4, :], 1.0 / 255.0, ALU.mult)
                eng.drain()
                eng.tensor_tensor(dn_t[0:4, :], stageA_t, stageA_t, ALU.mult)
                eng.drain()
                eng.tensor_single_scalar(dn_t[0:4, :], dn_t[0:4, :], 256.0 / 255.0, ALU.mult)
                eng.drain()
                return eng.tensor_tensor(stageB_t, stageB_t, dn_t[0:4, :], ALU.subtract)

            flush(bi * 8)

        pairs = [(0, 1), (0, 2), (0, 3), (1, 2), (1, 3), (2, 3)]
        for gi in range(3):
            for pi in range(2):
                i_, j_ = pairs[gi * 2 + pi]

                @s.step("vector")
                def _(eng, i_=i_, j_=j_):
                    return eng.tensor_tensor(prod_t, br_t[i_], br_t[j_], ALU.mult)

                @s.step("tensor")
                def _(eng, pi=pi):
                    for c in range(DC):
                        i = eng.matmul((ps_row if pi == 0 else ps_rowB)[0:4, :],
                                       hsel_t[:, c * 4:(c + 1) * 4], prod3[:, c, :],
                                       start=(c == 0), stop=(c == DC - 1))
                    return i

            @s.step("vector", nodrain=True)
            def _(eng):
                eng.tensor_single_scalar(stageA_t, ps_row[0:4, :], 1.0 / 256.0, ALU.mult)
                return eng.tensor_single_scalar(stageB_t, ps_rowB[0:4, :], 1.0 / 256.0, ALU.mult)

            flush(32 + gi * 8)

        psl3 = [ps_mm, ps_a, ps_b]
        prev_evac = None
        for g0 in range(0, NCH1, 3):
            mcs = list(range(g0, min(g0 + 3, NCH1)))

            @s.step("tensor", wait_abs=prev_evac)
            def _(eng, mcs=mcs):
                for idx, mc in enumerate(mcs):
                    ps = psl3[idx]
                    for kc in range(DC):
                        eng.matmul(ps[:, :], w1_b[:, kc * R1N + mc * 128: kc * R1N + (mc + 1) * 128],
                                   ln3[:, kc, :], start=(kc == 0), stop=False)
                    i = eng.matmul(ps[:, :], w56_b[0:56, mc * 128:(mc + 1) * 128],
                                   feats_t[0:56, :], start=False, stop=True)
                return i

            prev_evac = s.target + 1

            @s.step("scalar", nodrain=True)
            def _(eng, mcs=mcs):
                for idx, mc in enumerate(mcs):
                    i = eng.activation(h1_t[:, mc * nt:(mc + 1) * nt], psl3[idx][:, :], AF.Silu,
                                       bias=rb1_t[:, mc:mc + 1])
                return i

        @s.step("tensor")
        def _(eng):
            for mc in range(NCH1):
                i = eng.matmul(ps_lg[:, :], w2_b[:, mc * 16:(mc + 1) * 16],
                               h1_t[:, mc * nt:(mc + 1) * nt], start=(mc == 0), stop=(mc == NCH1 - 1))
            return i

        @s.step("vector")
        def _(eng):
            return eng.tensor_scalar_add(lgm_t, ps_lg[:, :], rb2_t[:, 0:1])

        @s.step("tensor")
        def _(eng):
            for nb in range(nt // 128):
                i = eng.transpose(ps_t1[:, nb * 16:(nb + 1) * 16],
                                  lgm_t[:, nb * 128:(nb + 1) * 128], id16_t)
            return i

        @s.step("vector")
        def _(eng):
            return eng.tensor_copy(lgtm_t, ps_t1[:, :])

        LG = lgtm_t.rearrange("p (h i) -> p h i", i=4)
        M1 = stat_t[:, 0:16].rearrange("p (h o) -> p h o", o=1)
        SS = stat_t[:, 16:32].rearrange("p (h o) -> p h o", o=1)

        @s.step("vector")
        def _(eng):
            eng.tensor_reduce(M1, LG, AXX, ALU.max)
            eng.drain()
            am, bm = bcast(LG, M1)
            return eng.tensor_tensor(LG, am, bm, ALU.subtract)

        @s.step("scalar")
        def _(eng):
            return eng.activation(lgtm_t, lgtm_t, AF.Exp)

        @s.step("vector")
        def _(eng):
            eng.tensor_reduce(SS, LG, AXX, ALU.add)
            eng.drain()
            eng.reciprocal(stat_t[:, 16:32], stat_t[:, 16:32])
            eng.drain()
            am, bm = bcast(LG, SS)
            eng.tensor_tensor(LG, am, bm, ALU.mult)
            eng.drain()
            return eng.tensor_scalar(ptok_t, lgtm_t, 0.96, 0.01, ALU.mult, ALU.add)

        @s.step("tensor")
        def _(eng):
            for nb in range(nt // 128):
                i = eng.transpose(ps_t2[:, nb * 128:(nb + 1) * 128],
                                  ptok_t[:, nb * 16:(nb + 1) * 16], id128_t)
            return i

        @s.step("vector")
        def _(eng):
            return eng.tensor_copy(pfm_t, ps_t2[:, :])

        mixf3 = lntmp_t.rearrange("p (c t) -> p c t", c=DC)
        for bi in range(4):
            for hg in range(2):
                @s.step("tensor")
                def _(eng, bi=bi, hg=hg):
                    for hi in range(2):
                        h = hg * 2 + hi
                        r = h * 4 + bi
                        ps = (ps_mm, ps_a)[hi]
                        i = eng.matmul(ps[:, 0:nt], esel_t[:, r * 128:(r + 1) * 128],
                                       pfm_t[:, :], start=True, stop=True)
                    return i

                @s.step("vector", nodrain=True)
                def _(eng, bi=bi, hg=hg):
                    for hi in range(2):
                        h = hg * 2 + hi
                        ps = (ps_mm, ps_a)[hi]
                        for ci in range(2):
                            mc = 2 * h + ci
                            if bi == 0:
                                i = eng.tensor_tensor(mixf3[:, mc, :], br3[0][:, mc, :],
                                                      ps[:, 0:nt], ALU.mult)
                            else:
                                i = eng.tensor_tensor(prod3[:, mc, :], br3[bi][:, mc, :],
                                                      ps[:, 0:nt], ALU.mult)
                    if bi > 0:
                        eng.drain()
                        for hi in range(2):
                            h = hg * 2 + hi
                            for ci in range(2):
                                mc = 2 * h + ci
                                i = eng.tensor_tensor(mixf3[:, mc, :], mixf3[:, mc, :],
                                                      prod3[:, mc, :], ALU.add)
                    return i

        @s.step("scalar")
        def _(eng):
            return eng.activation(sq_t, lntmp_t, AF.Square)

        @s.step("tensor")
        def _(eng):
            for c in range(DC):
                eng.matmul(ps_row[0:4, :], hsel_t[:, c * 4:(c + 1) * 4], sq3[:, c, :],
                           start=(c == 0), stop=(c == DC - 1))
            for c in range(DC):
                i = eng.matmul(ps_rowB[0:4, :], hw2_t[:, c * 4:(c + 1) * 4], sq3[:, c, :],
                               start=(c == 0), stop=(c == DC - 1))
            return i

        @s.step("vector", nodrain=True)
        def _(eng):
            eng.tensor_single_scalar(dn_t[0:4, :], ps_row[0:4, :], 1.0 / 256.0, ALU.mult)
            eng.tensor_single_scalar(stageA_t, ps_rowB[0:4, :], 1.0 / 256.0, ALU.mult)
            eng.drain()
            eng.tensor_single_scalar(dn_t[0:4, :], dn_t[0:4, :], LN_EPS, ALU.add)
            eng.drain()
            eng.tensor_single_scalar(dn_t[0:4, :], dn_t[0:4, :], LN_EPS, ALU.mult)
            eng.drain()
            return eng.tensor_tensor(dn_t[0:4, :], stageA_t, dn_t[0:4, :], ALU.add)

        @s.step("scalar")
        def _(eng):
            return eng.activation(dn_t[0:4, :], dn_t[0:4, :], AF.Sqrt)

        @s.step("vector")
        def _(eng):
            eng.reciprocal(dn_t[0:4, :], dn_t[0:4, :])
            eng.drain()
            return eng.tensor_copy(dnb_t, dn_t[0:4, :])

        for hg in range(2):
            @s.step("tensor")
            def _(eng, hg=hg):
                for hi in range(2):
                    h = hg * 2 + hi
                    ps = (ps_mm, ps_a)[hi]
                    i = eng.matmul(ps[:, 0:nt], sel4_t[:, h * 128:(h + 1) * 128], dnb_t,
                                   start=True, stop=True)
                return i

            @s.step("vector", nodrain=True)
            def _(eng, hg=hg):
                for hi in range(2):
                    h = hg * 2 + hi
                    ps = (ps_mm, ps_a)[hi]
                    for ci in range(2):
                        mc = 2 * h + ci
                        i = eng.scalar_tensor_tensor(on3[:, mc, :], mixf3[:, mc, :],
                                                     mixw_t[:, mc:mc + 1],
                                                     ps[:, 0:nt], ALU.mult, ALU.mult)
                return i

        psl3 = [ps_mm, ps_a, ps_b]
        prev_evac = None
        for g0 in range(0, DC, 2):
            mcs = list(range(g0, g0 + 2))

            @s.step("tensor", wait_abs=prev_evac)
            def _(eng, mcs=mcs):
                for idx, mc in enumerate(mcs):
                    ps = psl3[idx]
                    for kc in range(DC):
                        i = eng.matmul(ps[:, :], wo_b[:, kc * D + mc * 128: kc * D + (mc + 1) * 128],
                                       on3[:, kc, :], start=(kc == 0), stop=(kc == DC - 1))
                return i

            prev_evac = s.target + 1

            @s.step("vector", nodrain=True)
            def _(eng, mcs=mcs):
                for idx, mc in enumerate(mcs):
                    i = eng.tensor_tensor(hout[:, idx * nt:(idx + 1) * nt], xs3[:, mc, :],
                                          psl3[idx][:, :], ALU.add)
                return i

            @s.dma("sync")
            def _(eng, mcs=mcs, j=j):
                return [eng.dma_start(out=h[mc * 128:(mc + 1) * 128, j * nt:(j + 1) * nt],
                                      in_=hout[:, idx * nt:(idx + 1) * nt])
                        for idx, mc in enumerate(mcs) for h in xo_hbms]

def _cvt_conv(w, taps):
    w = np.asarray(w, np.float32)
    return np.ascontiguousarray(w.reshape(DC, 128, taps).transpose(1, 0, 2).reshape(128, DC * taps))


def _l2_weight_maps(i):
    lora_A = np.asarray(i["L1_ffn_lora_A"], np.float32)
    lora_B = np.asarray(i["L1_ffn_lora_B"], np.float32)
    E2 = np.zeros((8, 128), dtype=BF)
    for e in range(8):
        E2[e, e * 16:(e + 1) * 16] = 2.0
    CCm = DCC
    msl_np = np.zeros((CCm, CCm), np.float32)
    for r in range(CCm):
        msl_np[r, :r] = -1.0
    return {
        "Wo": np.asarray(i["L0_attn_Wo"], np.float32),
        "gw": vec_tile(i["L0_attn_g_weight"]),
        "Wr": np.ascontiguousarray(np.asarray(i["L0_ffn_W_r"], np.float32).T),
        "Wk": np.ascontiguousarray(np.asarray(i["L0_ffn_W_k"], np.float32).T),
        "Wv": np.ascontiguousarray(np.asarray(i["L0_ffn_W_v"], np.float32).T),
        "mur": vec_tile(i["L0_ffn_mu_r"]), "muk": vec_tile(i["L0_ffn_mu_k"]),
        "bwT": np.ascontiguousarray(np.asarray(i["L1_ffn_base_w"], np.float32).T),
        "bb": vec_tile(i["L1_ffn_base_b"]),
        "Aall": np.ascontiguousarray(lora_A.transpose(1, 0, 2).reshape(D, 128)),
        "Ball": np.ascontiguousarray(lora_B.reshape(128, D)),
        "rwT": np.ascontiguousarray(np.asarray(i["L1_ffn_router_w"], np.float32).T),
        "rb": np.ascontiguousarray(np.asarray(i["L1_ffn_router_b"], np.float32).reshape(8, 1)),
        "E2": E2, "id8": np.eye(8, dtype=BF), "id128": np.eye(128, dtype=BF),
        "Wq1": np.asarray(i["L1_attn_Wq"], np.float32),
        "Wk1": np.asarray(i["L1_attn_Wk"], np.float32),
        "Wv1": np.asarray(i["L1_attn_Wv"], np.float32),
        "Wb1": np.asarray(i["L1_attn_Wb"], np.float32),
        "cq": _cvt_conv(i["L1_attn_conv_q_w"], 4), "ck": _cvt_conv(i["L1_attn_conv_k_w"], 4),
        "cv": _cvt_conv(i["L1_attn_conv_v_w"], 4), "lw": _cvt_conv(i["L1_attn_local_w"], 7),
        "mw": _cvt_conv(i["L1_attn_mid_w"], 25),
        "msl": msl_np, "msu": np.ascontiguousarray(msl_np.T),
        "mui": np.triu(np.ones((CCm, CCm), np.float32)),
        "ident": np.eye(128, dtype=BF),
    }


def _sel4():
    s = np.zeros((4, 4 * 128), dtype=BF)
    for h in range(4):
        s[h, h * 128:(h + 1) * 128] = 1.0
    return s


def _hsel():
    h = np.zeros((128, DC * 4), dtype=BF)
    for c in range(DC):
        h[:, c * 4 + c // 2] = 1.0
    return h


def _l3_weight_maps(i):
    rw1 = np.zeros((R1K, R1N), np.float32)
    rw1[:, 0:2160] = np.asarray(i["L1_attn_r_w1"], np.float32)
    rw2 = np.zeros((R1N, 16), np.float32)
    rw2[0:2160, :] = np.asarray(i["L1_attn_r_w2"], np.float32)
    rb1 = np.zeros((R1N,), np.float32)
    rb1[0:2160] = np.asarray(i["L1_attn_r_b1"], np.float32)
    esel = np.zeros((16, 16 * 128), dtype=BF)
    for r in range(16):
        esel[r, r * 128:(r + 1) * 128] = 1.0
    Ifm = np.zeros((128, 512), np.float32)
    Ifm[:, 0:128] = np.eye(128)
    Ifm[:, 384:512] = np.eye(128)
    lora_A = np.asarray(i["L1_ffn_lora_A"], np.float32)
    lora_B = np.asarray(i["L1_ffn_lora_B"], np.float32)
    E2 = np.zeros((8, 128), dtype=BF)
    for e in range(8):
        E2[e, e * 16:(e + 1) * 16] = 2.0
    return {
        "rw1": rw1, "rw2": rw2,
        "rb1": np.ascontiguousarray(rb1.reshape(NCH1, 128).T),
        "rb2": np.ascontiguousarray(np.asarray(i["L1_attn_r_b2"], np.float32).reshape(16, 1)),
        "mixw": vec_tile(np.asarray(i["L1_attn_mix_w"], np.float32).reshape(-1)),
        "onw": vec_tile(np.tile(np.asarray(i["L1_attn_onorm_w"], np.float32), 4)),
        "Wo1": np.asarray(i["L1_attn_Wo"], np.float32),
        "id16": np.eye(16, dtype=BF), "esel": esel, "Ifm": Ifm, "hsel": _hsel(), "sel4": _sel4(),
        "bwT": np.ascontiguousarray(np.asarray(i["L1_ffn_base_w"], np.float32).T),
        "bb": vec_tile(i["L1_ffn_base_b"]),
        "Aall": np.ascontiguousarray(lora_A.transpose(1, 0, 2).reshape(D, 128)),
        "Ball": np.ascontiguousarray(lora_B.reshape(128, D)),
        "rwT": np.ascontiguousarray(np.asarray(i["L1_ffn_router_w"], np.float32).T),
        "rb": np.ascontiguousarray(np.asarray(i["L1_ffn_router_b"], np.float32).reshape(8, 1)),
        "E2": E2, "id8": np.eye(8, dtype=BF), "id128": np.eye(128, dtype=BF),
    }


def _masks(q):
    am0 = np.zeros((128, 3), np.float32)
    am1 = np.zeros((128, 3), np.float32)
    for p in range(3):
        if p < q:
            am0[:, p] = 1.0
        if p < q - 1:
            am1[:, p] = 1.0
    return am0, am1


def kernel(**inputs):
    i = {k: np.asarray(v) for k, v in inputs.items()}
    cst = const_inputs()
    x0 = np.asarray(i["L0_attn_hidden_states"], np.float32)
    x0_fm = [np.ascontiguousarray(x0[b].T) for b in range(B)]

    nc1 = build_L1()
    l1_common = {
        "Wr": np.ascontiguousarray(i["L0_ffn_W_r"].astype(np.float32).T),
        "Wk": np.ascontiguousarray(i["L0_ffn_W_k"].astype(np.float32).T),
        "Wv": np.ascontiguousarray(i["L0_ffn_W_v"].astype(np.float32).T),
        "mur": vec_tile(i["L0_ffn_mu_r"]), "muk": vec_tile(i["L0_ffn_mu_k"]),
        "Wq": i["L0_attn_Wq"].astype(np.float32),
        "Wf": i["L0_attn_Wf"].astype(np.float32),
        "Wi": i["L0_attn_Wi"].astype(np.float32),
        "mask_u": np.triu(np.ones((CC, CC), np.float32)),
        "ident": np.eye(128, dtype=BF), **cst,
    }
    in_maps = []
    for c in range(NCORES):
        b, q = divmod(c, 4)
        lo = q * TPC
        halo = np.zeros((D, 1), np.float32) if q == 0 else x0_fm[b][:, lo - 1:lo]
        xs = np.ascontiguousarray(np.concatenate([halo, x0_fm[b][:, lo:lo + TPC]], axis=1))
        in_maps.append({"xs": xs, **l1_common})
    r1 = run_bass_kernel_spmd(nc1, in_maps, core_ids=list(range(NCORES)))
    R1 = [{k: np.asarray(v) for k, v in r.items()} for r in r1.results]

    nc2 = build_L2()
    wm2 = _l2_weight_maps(i)
    in_maps = []
    for c in range(NCORES):
        b, q = divmod(c, 4)
        own = R1[c]
        if q == 0:
            x1h = np.zeros((D, OH), np.float32)
            oh = np.zeros((D, OH), BF)
            qth = np.zeros((D, OH), BF)
            bch = np.zeros((128, 16), np.float32)
        else:
            prev = R1[c - 1]
            x1h = np.asarray(prev["x1o"][:, TPC - OH:], np.float32)
            oh = np.ascontiguousarray(prev["oloc"][:, TPC - OH:])
            qth = np.ascontiguousarray(prev["qto"][:, TPC - OH:])
            bch = np.zeros((128, 16), np.float32)
            pbc = np.asarray(prev["bco"], np.float32)
            for h in range(8):
                bch[:, h * 2 + 0] = pbc[:, h * NCH + NCH - 2]
                bch[:, h * 2 + 1] = pbc[:, h * NCH + NCH - 1]
        x1e = np.ascontiguousarray(np.concatenate([x1h, np.asarray(own["x1o"], np.float32)], axis=1))
        SL = np.ascontiguousarray(np.concatenate(
            [np.asarray(R1[b * 4 + p]["sloco"], np.float32) for p in range(4)], axis=1))
        AT = np.ascontiguousarray(np.concatenate(
            [np.asarray(R1[b * 4 + p]["atoto"], np.float32) for p in range(4)], axis=1))
        am0, am1 = _masks(q)
        hmask = np.full((128, HLO), 0.0 if q == 0 else 1.0, np.float32)
        in_maps.append({
            "x1e": x1e, "oloc": np.ascontiguousarray(own["oloc"]),
            "qto": np.ascontiguousarray(own["qto"]),
            "bco": np.asarray(own["bco"], np.float32),
            "oh": oh, "qth": qth, "bch": bch,
            "SL": SL, "AT": AT, "am0": am0, "am1": am1, "hmask": hmask,
            **wm2, **cst,
        })
    r2 = run_bass_kernel_spmd(nc2, in_maps, core_ids=list(range(NCORES)))
    R2 = [{k: np.asarray(v) for k, v in r.items()} for r in r2.results]

    nc3 = build_L3()
    wm3 = _l3_weight_maps(i)
    in_maps = []
    for c in range(NCORES):
        b, q = divmod(c, 4)
        own = R2[c]
        Mcat = np.ascontiguousarray(np.concatenate(
            [np.asarray(R2[b * 4 + p]["Mo"], np.float32) for p in range(4)], axis=1))
        Ccat = np.ascontiguousarray(np.concatenate(
            [np.asarray(R2[b * 4 + p]["Co"], np.float32) for p in range(4)], axis=1))
        am0, _ = _masks(q)
        in_maps.append({
            "x4i": np.asarray(own["x4o"], np.float32),
            "loci": np.ascontiguousarray(own["loco"]),
            "midi": np.ascontiguousarray(own["mido"]),
            "vdi": np.ascontiguousarray(own["vdo"]),
            "qhi": np.ascontiguousarray(own["qho"]),
            "wfi": np.ascontiguousarray(own["wfo"]),
            "utmi": np.ascontiguousarray(own["utm"]),
            "khtmi": np.ascontiguousarray(own["khtmo"]),
            "atni": np.ascontiguousarray(own["atno"]),
            "Mcat": Mcat, "Ccat": Ccat, "am0": am0,
            **wm3, **cst,
        })
    r3 = run_bass_kernel_spmd(nc3, in_maps, core_ids=list(range(NCORES)))
    R3 = [np.asarray(r["x6o"], np.float32) for r in r3.results]

    out = np.empty((B, T, D), np.float32)
    for c in range(NCORES):
        b, q = divmod(c, 4)
        out[b, q * TPC:(q + 1) * TPC, :] = R3[c].T
    return out
